# version 01
# baseline (speedup 1.0000x reference)
import numpy as np
import ml_dtypes
import concourse.bass as bass
import concourse.mybir as mybir
from concourse.bass_utils import run_bass_kernel_spmd
from concourse.alu_op_type import AluOpType as OP
from contextlib import ExitStack

F32 = mybir.dt.float32
BF16 = mybir.dt.bfloat16
AF = mybir.ActivationFunctionType
BF = ml_dtypes.bfloat16

D = 1024
T = 8192
NCORE = 8
OWN = 2048
CHK = 128
NT_DN = OWN + 256
NT_AT = OWN + 128
NCH = NT_DN // CHK
NTL_AT = NT_AT // CHK
NTL_OWN = OWN // CHK
H, DK, DV, NM = 8, 64, 64, 8
FH, FKV, FDH = 16, 4, 64
NE, DFF = 32, 1024
TOPK = 4
LN_EPS = 1e-5
L2_EPS = 1e-6
CLIP = 1e-6
SCALE_DN = float(DK) ** -0.5
SCALE_AT = float(FDH) ** -0.5
MAGIC = 12582912.0
TWO_PI = float(2 * np.pi)
INV_2PI = float(1.0 / TWO_PI)
RR_C1 = 6.28125
RR_C2 = float(np.float32(TWO_PI) - np.float32(6.28125))
PI = float(np.pi)

DEBUG_TAPS = []
DN_STAGE = 99
DN_CHUNKS = 99
BUILD_PHASES = 99


def _ts(i, n):
    return slice(i * n, (i + 1) * n)


_CHAIN_OPS = {
    "tensor_copy", "tensor_tensor", "tensor_scalar", "tensor_single_scalar",
    "scalar_tensor_tensor", "tensor_reduce", "reduce_max", "reduce_sum", "memset",
    "memzero", "reciprocal", "tensor_tensor_scan", "max_index", "max_with_indices",
    "select", "activation", "copy", "add", "mul", "sqrt", "square", "sign",
    "tensor_add", "tensor_mul", "tensor_sub", "tensor_max", "tensor_relu",
    "tensor_scalar_add", "tensor_scalar_mul", "tensor_scalar_sub", "tensor_scalar_max",
    "tensor_scalar_min",
}


class _Chain:

    def __init__(self, kb, eng, sem):
        self._kb = kb
        self._eng = eng
        self._sem = sem

    def __getattr__(self, name):
        real = getattr(self._eng, name)
        if name not in _CHAIN_OPS:
            return real
        kb, eng, sem = self._kb, self._eng, self._sem

        def call(*a, **k):
            eng.wait_ge(kb.sems[sem], kb.cnt[sem])
            ins = real(*a, **k)
            kb.cnt[sem] += 1
            ins.then_inc(kb.sems[sem], 1)
            return ins
        return call


class KB:
    def __init__(self):
        self.nc = bass.Bass("TRN2", target_bir_lowering=False, debug=False,
                            detect_race_conditions=False)
        self.t = {}
        self.dr = {}
        self.cnt = {}
        self.sems = {}

    def sem(self, name):
        self.sems[name] = self.stack.enter_context(self.nc.semaphore(name))
        self.cnt[name] = 0

    def inc(self, instr, name, n=1):
        self.cnt[name] += n
        instr.then_inc(self.sems[name], n)

    def w(self, eng, name):
        eng.wait_ge(self.sems[name], self.cnt[name])

    def dma(self, eng, sem, out, in_, **kw):
        eng.wait_ge(self.sems[sem], self.cnt[sem])
        self.inc(eng.dma_start(out=out, in_=in_, **kw), sem, 16)

    def mk_chains(self):
        self._vchain = _Chain(self, self.nc.vector, "dve")
        self._schain = _Chain(self, self.nc.scalar, "act")

    def din(self, name, shape, dt=F32):
        self.dr[name] = self.nc.dram_tensor(name, shape, dt, kind="ExternalInput").ap()

    def sb(self, name, sh, dt=F32):
        self.t[name] = self.stack.enter_context(self.nc.sbuf_tensor(name, sh, dt))

    @property
    def v(self): return self._vchain
    @property
    def s(self): return self._schain
    @property
    def te(self): return self.nc.tensor
    @property
    def g(self): return self.nc.gpsimd
    @property
    def sy(self): return self.nc.sync

    def build(self):
        nc = self.nc
        for name, shape, dt in [
            ("x0", [NT_DN, D], F32),
            ("Wq", [D, 512], F32), ("Wk", [D, 512], F32), ("Wv", [D, 512], F32),
            ("Wbt", [D, 128], F32), ("Wg", [D, 512], F32), ("Wo", [512, D], F32),
            ("log_decay", [8, 8], F32), ("mode_logits", [8, 8], F32), ("omega_ls", [8, 8], F32),
            ("attn_norm_w", [1, D], F32), ("ffn_norm_w", [1, D], F32),
            ("qkv_w", [D, 1536], F32), ("qkv_b", [1, 1536], F32),
            ("proj_w", [D, D], F32), ("proj_b", [1, D], F32),
            ("attn_sink", [1, FH], F32),
            ("gate_w", [D, NE], F32), ("gate_b", [1, NE], F32),
            ("W_up", [NE, D, 2 * DFF], F32), ("W_down", [NE, DFF, D], F32),
            ("identsb", [128, 128], BF16), ("identsf", [128, 128], F32),
            ("ones8c", [8, 128], F32), ("ones1r", [1, 128], F32),
            ("e8bcc", [8, 8 * CHK], F32), ("e8negc", [8, 8 * CHK], F32),
            ("e8diagc", [8, 8], F32),
            ("tril01c", [128, 8 * CHK], BF16), ("diag01c", [128, 8 * CHK], BF16),
            ("pos_tc", [128, NCH], F32), ("invf8c", [8, 8 * 32], F32),
            ("cconstc", [128, 4], F32),
            ("cumsumc", [128, 128], F32),
            ("maskband", [128, 2 * CHK], F32), ("maskfirst", [128, 2 * CHK], F32),
            ("ropecos", [NT_AT, D], F32), ("ropesin", [NT_AT, D], F32),
        ]:
            self.din(name, shape, dt)
        self.dr["out"] = nc.dram_tensor("out", [OWN, D], F32, kind="ExternalOutput").ap()
        self.dr["x1_spill"] = nc.dram_tensor("x1_spill", [NT_AT, D], F32, kind="Internal").ap()
        self.dr["x2_spill"] = nc.dram_tensor("x2_spill", [OWN, D], F32, kind="Internal").ap()
        self.dr["flat_scr"] = nc.dram_tensor("flat_scr", [16, 64], F32, kind="Internal").ap()
        for name, tile, lo, hi, ph in DEBUG_TAPS:
            self.dr["tap_" + name] = nc.dram_tensor(
                "tap_" + name, [128, hi - lo], F32, kind="ExternalOutput").ap()

        with ExitStack() as stack:
            self.stack = stack
            for sname in ["ld", "ldw", "ldd", "pe", "dve", "act", "sto"]:
                self.sem(sname)
            self.mk_chains()
            sb = self.sb
            ps = lambda name, sh, dt=F32: self.t.__setitem__(
                name, stack.enter_context(nc.psum_tensor(name, sh, dt)))
            sb("identb", [128, 128], BF16); sb("identf", [128, 128], F32)
            sb("ones8", [8, 128], F32); sb("ones1", [1, 128], F32)
            sb("stat", [128, 8 * NCH], F32)
            sb("cconst", [128, 4], F32)
            self.pb = []
            for i in range(6):
                ps(f"pb{i}", [128, 512], F32)
                self.pb.append(self.t[f"pb{i}"])
            ps("ptrA", [128, 512], BF16)
            ps("ptrB", [128, 512], BF16)
            self.ptrA = self.t["ptrA"]; self.ptrB = self.t["ptrB"]

            blk = stack.enter_context(nc.Block())

            @blk.vector
            def _(_v):
                self.emit()
        return nc

    def emit(self):
        sb = self.sb
        self.p_const_smalls()
        with ExitStack() as scA:
            sbA = lambda n, sh, dt=F32: self.t.__setitem__(
                n, scA.enter_context(self.nc.sbuf_tensor(n, sh, dt)))
            sbA("q_tm", [128, NCH * 512], BF16)
            sbA("k_tm", [128, NCH * 512], BF16)
            sbA("v_tm", [128, NCH * 512], BF16)
            sbA("beta", [128, NCH * 64], F32)
            sbA("g_tm", [128, NCH * 64], F32)
            sbA("gate", [128, NCH * 512], BF16)
            sbA("og", [128, NCH * 512], BF16)
            sbA("rsq_q", [128, NCH * 8], F32)
            sbA("rsq_k", [128, NCH * 8], F32)
            sbA("wrow", [128, 64], F32)
            sbA("rhow", [128, 64], F32)
            sbA("pos", [128, NCH], F32)
            sbA("omf", [128, NM * 256], F32)
            sbA("e8diag", [8, 8], F32)
            sbA("e8bc", [8, 8 * CHK], F32)
            sbA("invf8", [8, 8 * 32], F32)
            sbA("cumsum", [128, 128], F32)
            sbA("sm64", [128, 64], F32); sbA("sm64b", [128, 64], F32)
            sbA("sm8", [8, 256], F32); sbA("sm8b", [8, 256], F32)
            self.p_const_dn()
            self.emit_attn_block()
        if BUILD_PHASES >= 4:
            self.emit_ffn_block()
        self.p_finish()

    def emit_attn_block(self):
        sb = self.sb
        with ExitStack() as sc1:
            self._scope = sc1
            sb1 = lambda n, sh, dt=F32: self.t.__setitem__(
                n, sc1.enter_context(self.nc.sbuf_tensor(n, sh, dt)))
            sb1("lnx_fm", [128, 8 * NT_DN], BF16)
            sb1("wbuf", [128, 8 * 512], BF16)
            sb1("x0t", [128, 2 * D], F32)
            sb1("lnt", [128, 2 * D], BF16)
            sb1("sqscr", [128, D], F32)
            self.p1a_small()
            self.p0_lnx()
            if BUILD_PHASES >= 1:
                self.p1b_proj()
            if BUILD_PHASES >= 1.5:
                self.p1c_cumsum()
            self.p_taps(1)
            self.phase_barrier()
        if BUILD_PHASES >= 2:
            with ExitStack() as sc2:
                sb2 = lambda n, sh, dt=F32: self.t.__setitem__(
                    n, sc2.enter_context(self.nc.sbuf_tensor(n, sh, dt)))
                sb2("S", [64, 64 * DV], F32)
                sb2("S_bf", [64, 64 * DV], BF16)
                sb2("gmc", [8, 8 * 128], F32)
                sb2("ghi", [128, 64], BF16)
                sb2("glo", [128, 64], BF16)
                sb2("tril01", [128, 8 * CHK], BF16)
                sb2("diag01", [128, 8 * CHK], BF16)
                sb2("trg", [128, 256], F32); sb2("trg2", [128, 256], F32)
                sb2("trg3", [128, 256], F32)
                sb2("sinv", [128, 256], F32); sb2("cosv", [128, 256], F32)
                sb2("rot1", [128, 256], F32); sb2("rot2", [128, 256], F32)
                sb2("rot", [128, 512], F32)
                sb2("qe_tm", [128, 512], BF16); sb2("kr_tm", [128, 512], BF16)
                sb2("qn_tm", [128, 512], BF16); sb2("qn_fm", [64, 1024], BF16)
                sb2("qe_fm", [64, 1024], BF16); sb2("kr_fm", [64, 1024], BF16)
                sb2("fexp", [128, 1024], F32)
                sb2("btril", [128, 1024], BF16)
                sb2("eargp", [128, 1024], BF16)
                sb2("earg2", [128, 1024], BF16)
                sb2("x0m", [128, 1024], BF16)
                sb2("y0m", [128, 1024], BF16)
                sb2("ptm", [128, 1024], BF16)
                sb2("xpow", [128, 2 * 1024], BF16)
                sb2("ypow", [128, 2 * 1024], BF16)
                sb2("dsol", [128, 512], BF16)
                sb2("dfp", [128, 512], F32)
                sb2("bexp", [128, 512], F32)
                sb2("bexp2", [128, 512], F32)
                sb2("f8", [128, 32], F32)
                sb2("sups", [64, 1088], F32)
                sb2("d2bf", [128, 512], BF16)
                self.load_dn_consts()
                self.p2_deltanet()
                self.p_taps(2)
                self.phase_barrier()
        if BUILD_PHASES >= 2.7:
            with ExitStack() as sc3:
                sb3 = lambda n, sh, dt=F32: self.t.__setitem__(
                    n, sc3.enter_context(self.nc.sbuf_tensor(n, sh, dt)))
                sb3("wo", [128, 4 * 1024], BF16)
                sb3("x0t3", [128, 2 * D], F32)
                sb3("x1t", [128, 2 * D], F32)
                sb3("aofm", [128, 512], BF16)
                sb3("sqscr3", [128, D], F32)
                if BUILD_PHASES >= 3:
                    self.p3_attn_out()
                self.p_taps(3)
                self.phase_barrier()

    def phase_barrier(self):
        for eng in [self.v, self.s, self.te, self.g, self.sy]:
            for sname in ["ld", "ldw", "ldd", "pe", "dve", "act", "sto"]:
                eng.wait_ge(self.sems[sname], self.cnt[sname])

    def p_const_smalls(self):
        t, dr = self.t, self.dr
        for name, dst in [("identsb", "identb"), ("identsf", "identf"), ("ones8c", "ones8"),
                          ("ones1r", "ones1"), ("cconstc", "cconst")]:
            self.dma(self.sy, "ld", t[dst][:], dr[name])

    def p_const_dn(self):
        t, dr = self.t, self.dr
        for name, dst in [("e8diagc", "e8diag"),
                          ("invf8c", "invf8"), ("pos_tc", "pos"),
                          ("e8bcc", "e8bc"),
                          ("cumsumc", "cumsum")]:
            self.dma(self.sy, "ld", t[dst][:], dr[name])
        self.dma(self.sy, "ld", t["sm8"][:, 0:8], dr["log_decay"])
        self.dma(self.sy, "ld", t["sm8"][:, 8:16], dr["mode_logits"])
        self.dma(self.sy, "ld", t["sm8"][:, 16:24], dr["omega_ls"])

    def load_dn_consts(self):
        t, dr = self.t, self.dr
        for name, dst in [("tril01c", "tril01"), ("diag01c", "diag01")]:
            self.dma(self.sy, "ld", t[dst][:], dr[name])

    def p1a_small(self):
        t, dr = self.t, self.dr
        v, s, te, g = self.v, self.s, self.te, self.g
        sm = t["sm8"]
        self.w(v, "ld"); self.w(s, "ld")
        s.activation(sm[:, 24:32], sm[:, 0:8], AF.Abs)
        self.w(s, "act")
        s.activation(sm[:, 32:40], sm[:, 24:32], AF.Exp, scale=-1.0)
        self.w(s, "act")
        s.activation(sm[:, 40:48], sm[:, 32:40], AF.Ln, bias=self.t["cconst"][0:8, 3:4])
        self.w(v, "act")
        v.tensor_scalar(sm[:, 48:56], sm[:, 0:8], 0.0, None, OP.max)
        v.tensor_tensor(sm[:, 48:56], sm[:, 48:56], sm[:, 40:48], OP.add)
        self.w(s, "dve")
        s.activation(sm[:, 56:64], sm[:, 48:56], AF.Exp, scale=-1.0)
        v.tensor_reduce(sm[:, 64:65], sm[:, 8:16], axis=mybir.AxisListType.X,
                                 op=OP.max)
        v.tensor_scalar(sm[:, 72:80], sm[:, 8:16], sm[:, 64:65], None, OP.subtract)
        self.w(s, "dve")
        s.activation(sm[:, 80:88], sm[:, 72:80], AF.Exp, accum_out=sm[:, 88:89])
        self.w(v, "act")
        v.reciprocal(sm[:, 96:97], sm[:, 88:89])
        v.tensor_scalar(sm[:, 104:112], sm[:, 80:88], sm[:, 96:97], SCALE_DN,
                                 OP.mult, OP.mult)
        self.w(s, "dve")
        s.activation(sm[:, 112:120], sm[:, 16:24], AF.Exp)
        e8sel = self.t["e8bc"][:].rearrange("p (h s) -> p h s", h=8)[:, :, 0:8]
        smb = self.t["sm8b"]
        self.w(v, "act"); self.w(v, "dve")
        v.tensor_copy(smb[0:8, 0:64].rearrange("p (h m) -> p h m", m=8),
                      sm[:, 56:64].unsqueeze(1).broadcast_to([8, 8, 8]))
        v.tensor_tensor(smb[0:8, 0:64].rearrange("p (h m) -> p h m", m=8),
                        smb[0:8, 0:64].rearrange("p (h m) -> p h m", m=8), e8sel, OP.mult)
        v.tensor_copy(smb[0:8, 64:128].rearrange("p (h m) -> p h m", m=8),
                      sm[:, 104:112].unsqueeze(1).broadcast_to([8, 8, 8]))
        v.tensor_tensor(smb[0:8, 64:128].rearrange("p (h m) -> p h m", m=8),
                                 smb[0:8, 64:128].rearrange("p (h m) -> p h m", m=8),
                                 e8sel, OP.mult)
        self.w(te, "dve"); self.w(te, "ld")
        ins = te.matmul(self.pb[5][:, 0:128], t["ones8"][:], smb[0:8, 0:128],
                        start=True, stop=True)
        self.inc(ins, "pe", 1)
        self.w(v, "pe")
        v.tensor_copy(t["rhow"][:], self.pb[5][:, 0:64])
        v.tensor_copy(t["wrow"][:], self.pb[5][:, 64:128])
        self.w(te, "dve")
        for m in range(NM):
            v.tensor_scalar(t["sm8b"][:, 0:256], t["invf8"][:],
                                     sm[:, 112 + m:113 + m], None, OP.mult)
            self.w(te, "dve")
            ins = te.matmul(self.pb[5][:, 0:256], t["ones8"][:], t["sm8b"][:, 0:256],
                            start=True, stop=True)
            self.inc(ins, "pe", 1)
            self.w(v, "pe")
            v.tensor_copy(t["omf"][:, _ts(m, 256)], self.pb[5][:, 0:256])
            self.w(te, "dve")

    def p0_lnx(self):
        t, dr = self.t, self.dr
        v, s, te = self.v, self.s, self.te
        stat = t["stat"]
        rel = {}
        for i in range(NCH):
            par = i % 2
            if i >= 2:
                self.sy.wait_ge(self.sems["act"], rel[i - 2])
            self.dma(self.sy, "ld", t["x0t"][:, _ts(par, D)], dr["x0"][_ts(i, 128), :])
            x = t["x0t"][:, _ts(par, D)]
            self.w(v, "ld")
            v.tensor_reduce(stat[:, i:i + 1], x, axis=mybir.AxisListType.X, op=OP.add)
            self.w(s, "ld"); self.w(s, "dve")
            s.activation(t["sqscr"][:], x, AF.Square,
                                  accum_out=stat[:, NCH + i:NCH + i + 1])
            rel[i] = self.cnt["act"]
        v_ = v
        self.w(v_, "act")
        v_.tensor_scalar(stat[:, 2 * NCH:3 * NCH], stat[:, 0:NCH], 1.0 / D, None, OP.mult)
        v_.tensor_scalar(stat[:, 3 * NCH:4 * NCH], stat[:, NCH:2 * NCH], 1.0 / D, None, OP.mult)
        v_.tensor_tensor(stat[:, 4 * NCH:5 * NCH], stat[:, 2 * NCH:3 * NCH],
                         stat[:, 2 * NCH:3 * NCH], OP.mult)
        v_.tensor_tensor(stat[:, 4 * NCH:5 * NCH], stat[:, 3 * NCH:4 * NCH],
                                  stat[:, 4 * NCH:5 * NCH], OP.subtract)
        self.w(s, "dve")
        s.activation(stat[:, 5 * NCH:6 * NCH], stat[:, 4 * NCH:5 * NCH], AF.Sqrt,
                              bias=self.t["cconst"][:, 0:1])
        self.w(v, "act")
        v.reciprocal(stat[:, 6 * NCH:7 * NCH], stat[:, 5 * NCH:6 * NCH])
        self.w(self.g, "ld"); self.w(self.g, "act"); self.w(self.g, "dve")
        relb = {}
        for i in range(NCH):
            par = i % 2
            if i >= 2:
                self.g.wait_ge(self.sems["dve"], relb[i - 2])
            self.dma(self.g, "ldw", t["x0t"][:, _ts(par, D)], dr["x0"][_ts(i, 128), :])
            self.w(v, "ldw")
            v.tensor_scalar(t["lnt"][:, _ts(par, D)], t["x0t"][:, _ts(par, D)],
                                     stat[:, 2 * NCH + i:2 * NCH + i + 1],
                                     stat[:, 6 * NCH + i:6 * NCH + i + 1],
                                     OP.subtract, OP.mult)
            relb[i] = self.cnt["dve"]
            self.w(te, "dve")
            for half in range(2):
                bank = self.ptrA if half == 0 else self.ptrB
                for j in range(4):
                    kk = half * 4 + j
                    ins = te.transpose(bank[:, _ts(j, 128)],
                                       t["lnt"][:, par * D + kk * 128: par * D + (kk + 1) * 128],
                                       t["identb"][:])
                self.inc(ins, "pe", 1)
                self.w(v, "pe")
                for j in range(4):
                    kk = half * 4 + j
                    ins = v.tensor_copy(
                        t["lnx_fm"][:, kk * NT_DN + i * 128: kk * NT_DN + (i + 1) * 128],
                        bank[:, _ts(j, 128)])
                ins
                self.w(te, "dve")

    def p1b_proj(self):
        t = self.t
        v, s, te, g = self.v, self.s, self.te, self.g
        pb = self.pb
        self.w(te, "dve")

        def loadw(src, cols):
            self.w(g, "pe")
            self.dma(g, "ldw", t["wbuf"][:, 0:8 * cols].rearrange("p (k c) -> p k c", c=cols),
                     self.dr[src].rearrange("(k p) c -> p k c", p=128))
            self.w(te, "ldw")

        def mm8(dst_bank, cols, i):
            for k in range(8):
                ins = te.matmul(dst_bank[:, 0:cols],
                                t["lnx_fm"][:, k * NT_DN + i * 128: k * NT_DN + (i + 1) * 128],
                                t["wbuf"][:, k * cols:(k + 1) * cols],
                                start=(k == 0), stop=(k == 7))
            self.inc(ins, "pe", 1)

        for wname, dst, rsq in [("Wq", "q_tm", "rsq_q"), ("Wk", "k_tm", "rsq_k")]:
            loadw(wname, 512)
            for i in range(NCH):
                mm8(pb[i % 2], 512, i)
                self.w(v, "pe")
                v.tensor_copy(t[dst][:, _ts(i, 512)], pb[i % 2][:, 0:512])
                self.w(s, "dve")
                s.activation(t["sqscr"][:, 0:512], t[dst][:, _ts(i, 512)],
                                      AF.Square)
                self.w(v, "act")
                v.tensor_reduce(t[rsq][:, _ts(i, 8)],
                                         t["sqscr"][:, 0:512].rearrange("p (h f) -> p h f", h=8),
                                         axis=mybir.AxisListType.X, op=OP.add)
                self.w(te, "dve")
        loadw("Wv", 512)
        for i in range(NCH):
            mm8(pb[i % 2], 512, i)
            self.w(v, "pe")
            v.tensor_copy(t["v_tm"][:, _ts(i, 512)], pb[i % 2][:, 0:512])
            if i >= 1:
                self.w(te, "dve")
        loadw("Wbt", 128)
        for i in range(NCH):
            mm8(pb[i % 2], 128, i)
            self.w(s, "pe")
            s.activation(t["beta"][:, _ts(i, 64)], pb[i % 2][:, 0:64], AF.Sigmoid)
            s.activation(t["sm64"][:], pb[i % 2][:, 64:128], AF.Sigmoid)
            self.w(v, "act")
            v.tensor_tensor(t["sm64b"][:], t["sm64"][:], t["rhow"][:], OP.mult)
            v.tensor_scalar(t["sm64b"][:], t["sm64b"][:], CLIP, None, OP.max)
            self.w(s, "dve")
            s.activation(t["g_tm"][:, _ts(i, 64)], t["sm64b"][:], AF.Ln)
            self.w(te, "act")
        loadw("Wg", 512)
        for i in range(NCH):
            mm8(pb[i % 2], 512, i)
            self.w(s, "pe")
            s.activation(t["gate"][:, _ts(i, 512)], pb[i % 2][:, 0:512], AF.Sigmoid)
            if i >= 1:
                self.w(te, "act")
        self.w(s, "dve")
        s.activation(t["rsq_q"][:], t["rsq_q"][:], AF.Sqrt,
                              bias=self.t["cconst"][:, 1:2])
        s.activation(t["rsq_k"][:], t["rsq_k"][:], AF.Sqrt,
                              bias=self.t["cconst"][:, 1:2])
        self.w(v, "act")
        v.reciprocal(t["rsq_q"][:], t["rsq_q"][:])
        v.reciprocal(t["rsq_k"][:], t["rsq_k"][:])

    def p1c_cumsum(self):
        t = self.t
        v, te = self.v, self.te
        pb5 = self.pb[5]
        for i in range(NCH):
            self.w(te, "act"); self.w(te, "dve")
            ins = te.matmul(pb5[:, 0:64], t["cumsum"][:], t["g_tm"][:, _ts(i, 64)],
                            start=True, stop=True)
            self.inc(ins, "pe", 1)
            self.w(v, "pe")
            v.tensor_copy(t["g_tm"][:, _ts(i, 64)], pb5[:, 0:64])
            self.w(te, "dve")

    def p2_deltanet(self):
        t = self.t
        v, te = self.v, self.te
        v.memset(t["S"][:], 0.0)
        v.memset(t["S_bf"][:], 0.0)
        for c in range(min(NCH, DN_CHUNKS)):
            v.tensor_copy(t["ghi"][:], t["g_tm"][:, _ts(c, 64)])
            v.tensor_tensor(t["glo"][:], t["g_tm"][:, _ts(c, 64)], t["ghi"][:],
                                     OP.subtract)
            for m in range(NM):
                self.w(te, "dve")
                srh = t["ghi"][:].rearrange("p (h m) -> p m h", m=8)[:, m, :]
                srl = t["glo"][:].rearrange("p (h m) -> p m h", m=8)[:, m, :]
                te.transpose(self.ptrA[0:8, 0:128], srh, t["identb"][:])
                ins = te.transpose(self.ptrA[0:8, 128:256], srl, t["identb"][:])
                self.inc(ins, "pe", 1)
                self.w(v, "pe")
                v.tensor_copy(t["gmc"][0:8, _ts(m, 128)], self.ptrA[0:8, 0:128])
                v.tensor_tensor(t["gmc"][0:8, _ts(m, 128)],
                                         t["gmc"][0:8, _ts(m, 128)],
                                         self.ptrA[0:8, 128:256], OP.add)
                self.w(te, "dve")
            for m in range(NM):
                self.dn_cm(c, m)
            self.w(v, "pe")
            v.tensor_tensor(t["og"][:, _ts(c, 512)], self.pb[5][:, 0:512],
                                     t["gate"][:, _ts(c, 512)], OP.mult)
            self.w(te, "dve")

    def dn_cm(self, c, m):
        t = self.t
        v, s, te = self.v, self.s, self.te
        pb = self.pb
        gchunk = t["gmc"][0:8, _ts(m, 128)]
        c64 = t["beta"][:, _ts(c, 64)].rearrange("p (h m) -> p m h", m=8)
        bslice = c64[:, m, :]
        gslice = t["g_tm"][:, _ts(c, 64)].rearrange("p (h m) -> p m h", m=8)[:, m, :]
        ws = t["wrow"][:].rearrange("p (h m) -> p m h", m=8)[:, m, :]
        f8t = t["f8"][:]

        def hsl(name, h):
            return t[name][:, h * 128:(h + 1) * 128]

        def fmh(name, h):
            return t[name][0:64, h * 128:(h + 1) * 128]

        ang = t["trg"][:]; t2 = t["trg2"][:]; t3 = t["trg3"][:]
        sv_ = t["sinv"][:]; cv_ = t["cosv"][:]
        v.tensor_scalar(ang, t["omf"][:, _ts(m, 256)], t["pos"][:, c:c + 1], None,
                                 OP.mult)
        v.tensor_scalar(t2, ang, INV_2PI, None, OP.mult)
        v.tensor_scalar(t2, t2, MAGIC, None, OP.add)
        v.tensor_scalar(t2, t2, MAGIC, None, OP.subtract)
        v.scalar_tensor_tensor(t3, t2, -RR_C1, ang, OP.mult, OP.add)
        v.scalar_tensor_tensor(t3, t2, -RR_C2, t3, OP.mult, OP.add)
        v.tensor_scalar(t3, t3, -PI, PI, OP.max, OP.min)
        self.w(s, "dve")
        s.activation(sv_, t3, AF.Sin)
        v.tensor_scalar(t2, t3, PI / 2, None, OP.is_gt)
        v.scalar_tensor_tensor(t2, t2, -TWO_PI, t3, OP.mult, OP.add)
        v.tensor_scalar(t2, t2, -PI / 2 - PI, PI - PI / 2, OP.max, OP.min)
        self.w(s, "dve")
        s.activation(cv_, t2, AF.Sin, bias=t["cconst"][:, 2:3])
        s.activation(f8t[:, 0:8], gslice, AF.Exp)

        if DN_STAGE < 2:
            return
        self.w(v, "act")
        rot = t["rot"][:]
        r13 = t["rot1"][:].rearrange("p (h f) -> p h f", h=8)
        r23 = t["rot2"][:].rearrange("p (h f) -> p h f", h=8)
        c3 = cv_.rearrange("p (h f) -> p h f", h=8)
        s3 = sv_.rearrange("p (h f) -> p h f", h=8)
        rot3 = rot.rearrange("p (h f) -> p h f", h=8)
        for srcn, dstn, isq in [("q_tm", "qe_tm", True), ("k_tm", "kr_tm", False)]:
            src3 = t[srcn][:, _ts(c, 512)].rearrange("p (h f) -> p h f", h=8)
            lo = src3[:, :, 0:32]; hi = src3[:, :, 32:64]
            v.tensor_tensor(r13, lo, c3, OP.mult)
            v.tensor_tensor(r23, hi, s3, OP.mult)
            v.tensor_tensor(rot3[:, :, 0:32], r13, r23, OP.add)
            v.tensor_tensor(r13, hi, c3, OP.mult)
            v.tensor_tensor(r23, lo, s3, OP.mult)
            v.tensor_tensor(rot3[:, :, 32:64], r13, r23, OP.subtract)
            bx = t["bexp"][:]
            if isq:
                rq = t["rsq_q"][:, _ts(c, 8)]
                v.tensor_tensor(f8t[:, 8:16], rq, ws, OP.mult)
                v.tensor_copy(bx.rearrange("p (h f) -> p h f", h=8),
                                       f8t[:, 8:16].unsqueeze(-1).broadcast_to([128, 8, 64]))
                v.tensor_tensor(t["qn_tm"][:], rot, bx, OP.mult)
                v.tensor_tensor(f8t[:, 8:16], f8t[:, 8:16], f8t[:, 0:8], OP.mult)
                fcol = f8t[:, 8:16]
            else:
                fcol = t["rsq_k"][:, _ts(c, 8)]
            v.tensor_copy(bx.rearrange("p (h f) -> p h f", h=8),
                                   fcol.unsqueeze(-1).broadcast_to([128, 8, 64]))
            v.tensor_tensor(t[dstn][:], rot, bx, OP.mult)

        if DN_STAGE < 3:
            return
        for srcn, dstn, bank in [("qe_tm", "qe_fm", self.ptrA), ("kr_tm", "kr_fm", self.ptrB),
                                 ("qn_tm", "qn_fm", self.ptrA)]:
            self.w(te, "dve")
            for half in range(2):
                for j in range(4):
                    hh = half * 4 + j
                    ins = te.transpose(bank[0:64, _ts(j, 128)],
                                       t[srcn][:, hh * 64:(hh + 1) * 64], t["identb"][:])
                self.inc(ins, "pe", 1)
                self.w(v, "pe")
                v.tensor_copy(t[dstn][0:64, half * 512:(half + 1) * 512],
                                       bank[0:64, 0:512])
                self.w(te, "dve")
        self.w(te, "dve")

        if DN_STAGE < 4:
            return
        fx = t["fexp"][:]
        v.tensor_copy(fx[0:8, :].rearrange("p (h f) -> p h f", h=8),
                               gchunk.unsqueeze(1).broadcast_to([8, 8, 128]))
        v.tensor_tensor(fx[0:8, :], fx[0:8, :], t["e8bc"][:], OP.mult)
        v.tensor_scalar(fx[0:8, :], fx[0:8, :], -1.0, None, OP.mult)
        self.w(te, "dve")
        for half in range(2):
            bank = pb[half]
            te.matmul(bank[:, 0:512], gchunk, t["e8bc"][:, _ts(half, 512)],
                      start=True, stop=False)
            ins = te.matmul(bank[:, 0:512], t["ones8"][:], fx[0:8, _ts(half, 512)],
                            start=False, stop=True)
        self.inc(ins, "pe", 1)
        self.w(v, "pe")
        ea = t["eargp"][:]; e2 = t["earg2"][:]
        fx2 = t["fexp"][:]
        v.tensor_scalar(fx2[:, 0:512], pb[0][:, 0:512], 0.0, None, OP.min)
        v.tensor_scalar(fx2[:, 512:1024], pb[1][:, 0:512], 0.0, None, OP.min)
        v.tensor_scalar(t["btril"][:].bitcast(F32)[:, 0:256] if False else t["rot"][:],
                        pb[0][:, 0:512], 0.0, None, OP.max)
        v.tensor_scalar(t["bexp"][:], pb[1][:, 0:512], 0.0, None, OP.max)
        self.w(s, "dve")
        s.activation(ea[:, 0:512], fx2[:, 0:512], AF.Exp)
        s.activation(ea[:, 512:1024], fx2[:, 512:1024], AF.Exp)
        s.activation(e2[:, 0:512], t["rot"][:], AF.Exp, scale=-1.0)
        s.activation(e2[:, 512:1024], t["bexp"][:], AF.Exp, scale=-1.0)
        self.w(te, "act")

        if DN_STAGE < 5:
            return
        for h in range(8):
            bank = pb[0] if h < 4 else pb[1]
            ins = te.matmul(bank[:, _ts(h % 4, 128)], fmh("kr_fm", h), fmh("kr_fm", h),
                            start=True, stop=True)
        self.inc(ins, "pe", 1)
        for h in range(8):
            bank = pb[2] if h < 4 else pb[3]
            ins = te.matmul(bank[:, _ts(h % 4, 128)], fmh("kr_fm", h), fmh("qn_fm", h),
                            start=True, stop=True)
        self.inc(ins, "pe", 1)

        if DN_STAGE < 6:
            return
        self.w(v, "act"); self.w(v, "pe")
        v.tensor_copy(fx.rearrange("p (h f) -> p h f", h=8),
                               bslice.unsqueeze(-1).broadcast_to([128, 8, 128]))
        v.tensor_tensor(t["btril"][:], fx, t["tril01"][:], OP.mult)
        v.tensor_tensor(ea, ea, t["btril"][:], OP.mult)
        v.tensor_tensor(e2, e2, t["diag01"][:], OP.mult)
        for half in range(2):
            ab = pb[0] if half == 0 else pb[1]
            qb = pb[2] if half == 0 else pb[3]
            v.tensor_tensor(t["x0m"][:, _ts(half, 512)], ea[:, _ts(half, 512)],
                            ab[:, 0:512], OP.mult)
            ins = v.tensor_tensor(t["ptm"][:, _ts(half, 512)], e2[:, _ts(half, 512)],
                                  qb[:, 0:512], OP.mult)
        ins

        if DN_STAGE < 7:
            return
        self.w(te, "dve")
        for half in range(2):
            bank = self.ptrA if half == 0 else self.ptrB
            for j in range(4):
                hh = half * 4 + j
                ins = te.transpose(bank[:, _ts(j, 128)], hsl("x0m", hh), t["identb"][:])
            self.inc(ins, "pe", 1)
            self.w(v, "pe")
            v.tensor_copy(t["y0m"][:, _ts(half, 512)], bank[:, 0:512])

        if DN_STAGE < 8:
            return
        self.w(te, "dve")
        for h in range(8):
            ins = te.matmul(pb[0][:, h * 64:(h + 1) * 64], fmh("kr_fm", h),
                            t["S_bf"][0:64, (h * 8 + m) * 64:(h * 8 + m + 1) * 64],
                            start=True, stop=True)
        self.inc(ins, "pe", 1)
        bx = t["bexp"][:]
        v.tensor_copy(bx.rearrange("p (h f) -> p h f", h=8),
                               f8t[:, 0:8].unsqueeze(-1).broadcast_to([128, 8, 64]))
        self.w(v, "pe")
        v.tensor_tensor(t["dfp"][:], bx, pb[0][:, 0:512], OP.mult)
        v.tensor_tensor(t["dfp"][:], t["v_tm"][:, _ts(c, 512)], t["dfp"][:],
                                 OP.subtract)
        v.tensor_copy(t["bexp2"][:].rearrange("p (h f) -> p h f", h=8),
                               bslice.unsqueeze(-1).broadcast_to([128, 8, 64]))
        v.tensor_tensor(t["dfp"][:], t["dfp"][:], t["bexp2"][:], OP.mult)
        v.tensor_copy(t["dsol"][:], t["dfp"][:])
        self.w(te, "dve")

        if DN_STAGE < 9:
            return
        def xy(j):
            if j == 0:
                return ("x0m", 0), ("y0m", 0)
            sl = (j - 1) % 2
            return ("xpow", sl * 1024), ("ypow", sl * 1024)

        for j in range(7):
            (xn, xo), (yn, yo) = xy(j)
            for h in range(8):
                ins = te.matmul(pb[0][:, h * 64:(h + 1) * 64],
                                t[yn][:, yo + h * 128: yo + (h + 1) * 128],
                                t["dsol"][:, h * 64:(h + 1) * 64],
                                start=True, stop=True)
            self.inc(ins, "pe", 1)
            if j < 6:
                for h in range(8):
                    bank = pb[1] if h < 4 else pb[2]
                    ins = te.matmul(bank[:, _ts(h % 4, 128)],
                                    t[yn][:, yo + h * 128: yo + (h + 1) * 128],
                                    t[xn][:, xo + h * 128: xo + (h + 1) * 128],
                                    start=True, stop=True)
                self.inc(ins, "pe", 1)
                for h in range(8):
                    bank = pb[3] if h < 4 else pb[4]
                    ins = te.matmul(bank[:, _ts(h % 4, 128)],
                                    t[xn][:, xo + h * 128: xo + (h + 1) * 128],
                                    t[yn][:, yo + h * 128: yo + (h + 1) * 128],
                                    start=True, stop=True)
                self.inc(ins, "pe", 1)
            self.w(v, "pe")
            op = OP.subtract if j == 0 else OP.add
            v.tensor_tensor(t["dfp"][:], t["dfp"][:], pb[0][:, 0:512], op)
            v.tensor_copy(t["dsol"][:], t["dfp"][:])
            if j < 6:
                sl = j % 2
                v.tensor_copy(t["xpow"][:, sl * 1024: sl * 1024 + 512], pb[1][:, 0:512])
                v.tensor_copy(t["xpow"][:, sl * 1024 + 512: sl * 1024 + 1024], pb[2][:, 0:512])
                v.tensor_copy(t["ypow"][:, sl * 1024: sl * 1024 + 512], pb[3][:, 0:512])
                v.tensor_copy(t["ypow"][:, sl * 1024 + 512: sl * 1024 + 1024],
                                       pb[4][:, 0:512])
            self.w(te, "dve")

        if DN_STAGE < 10:
            return
        first = (m == 0)
        for h in range(8):
            te.matmul(pb[5][:, h * 64:(h + 1) * 64], fmh("qe_fm", h),
                      t["S_bf"][0:64, (h * 8 + m) * 64:(h * 8 + m + 1) * 64],
                      start=(first and h == 0), stop=False, skip_group_check=True)
            ins = te.matmul(pb[5][:, h * 64:(h + 1) * 64], hsl("ptm", h),
                            t["dsol"][:, h * 64:(h + 1) * 64],
                            start=False, stop=(m == NM - 1 and h == 7), skip_group_check=True)
        self.inc(ins, "pe", 1)

        if DN_STAGE < 11:
            return
        gc = gchunk[:, 127:128]
        v.tensor_tensor(t["sm8b"][:, 0:8], t["e8diag"][:],
                                 gc.broadcast_to([8, 8]), OP.mult)
        self.w(te, "dve")
        ins = te.matmul(pb[4][:, 0:8], t["ones8"][:], t["sm8b"][:, 0:8], start=True, stop=True)
        self.inc(ins, "pe", 1)
        self.w(v, "pe")
        v.tensor_tensor(f8t[:, 16:24], pb[4][:, 0:8], gslice, OP.subtract)
        self.w(s, "dve"); self.w(s, "pe")
        s.activation(f8t[:, 24:32], f8t[:, 16:24], AF.Exp)
        s.activation(t["sups"][0:64, 1024:1032], pb[4][0:64, 0:8], AF.Exp)
        self.w(v, "act")
        v.tensor_copy(bx.rearrange("p (h f) -> p h f", h=8),
                               f8t[:, 24:32].unsqueeze(-1).broadcast_to([128, 8, 64]))
        v.tensor_tensor(t["d2bf"][:], t["dfp"][:], bx, OP.mult)
        self.w(te, "dve")
        for h in range(8):
            ins = te.matmul(pb[4][0:64, h * 64:(h + 1) * 64],
                            t["kr_tm"][:, h * 64:(h + 1) * 64],
                            t["d2bf"][:, h * 64:(h + 1) * 64],
                            start=True, stop=True)
        self.inc(ins, "pe", 1)
        S3 = t["S"][:].rearrange("k (h m v) -> k h m v", h=8, m=8)[:, :, m, :]
        S3b = t["S_bf"][0:64].rearrange("k (h m v) -> k h m v", h=8, m=8)[:, :, m, :]
        eg = t["sups"][0:64, 1024:1032]
        egx = t["sups"][0:64, 0:512]
        sc2 = t["sups"][0:64, 512:1024]
        v.tensor_copy(egx.rearrange("k (h f) -> k h f", h=8),
                               eg.unsqueeze(-1).broadcast_to([64, 8, 64]))
        v.tensor_tensor(sc2.rearrange("k (h f) -> k h f", h=8), S3,
                                 egx.rearrange("k (h f) -> k h f", h=8), OP.mult)
        self.w(v, "pe")
        v.tensor_tensor(S3, sc2.rearrange("k (h f) -> k h f", h=8),
                                 pb[4][0:64, 0:512].rearrange("k (h f) -> k h f", h=8), OP.add)
        v.tensor_copy(S3b, S3)
        self.w(te, "dve")

    def p3_attn_out(self):
        t, dr = self.t, self.dr
        v, s, te = self.v, self.s, self.te
        pb = self.pb
        self.dma(self.g, "ldw", t["wo"][:].rearrange("p (k c) -> p k c", c=1024),
                 dr["Wo"].rearrange("(k p) c -> p k c", p=128))
        self.w(te, "ldw")
        for a in range(NTL_AT):
            c = a + 1
            self.w(te, "dve")
            for j in range(4):
                ins = te.transpose(self.ptrA[:, _ts(j, 128)],
                                   t["og"][:, c * 512 + j * 128: c * 512 + (j + 1) * 128],
                                   t["identb"][:])
            self.inc(ins, "pe", 1)
            self.w(v, "pe")
            v.tensor_copy(t["aofm"][:], self.ptrA[:, 0:512])
            self.w(te, "dve")
            for n in range(2):
                for k in range(4):
                    ins = te.matmul(pb[n][:, 0:512], t["aofm"][:, _ts(k, 128)],
                                    t["wo"][:, k * 1024 + n * 512: k * 1024 + (n + 1) * 512],
                                    start=(k == 0), stop=(k == 3))
                self.inc(ins, "pe", 1)
            self.w(v, "pe"); self.w(s, "pe")
            st = t["stat"]
            v.tensor_reduce(st[:, 0:1], pb[0][:, 0:512], axis=mybir.AxisListType.X, op=OP.add)
            v.tensor_reduce(st[:, 1:2], pb[1][:, 0:512], axis=mybir.AxisListType.X, op=OP.add)
            v.tensor_tensor(st[:, 0:1], st[:, 0:1], st[:, 1:2], OP.add)
            v.tensor_scalar(st[:, 2:3], st[:, 0:1], 1.0 / D, None, OP.mult)
            self.w(s, "dve")
            s.activation(t["sqscr3"][:, 0:512], pb[0][:, 0:512], AF.Square,
                                  accum_out=st[:, 3:4])
            s.activation(t["sqscr3"][:, 512:1024], pb[1][:, 0:512], AF.Square,
                                  accum_out=st[:, 4:5])
            self.w(v, "act")
            v.tensor_tensor(st[:, 3:4], st[:, 3:4], st[:, 4:5], OP.add)
            v.tensor_scalar(st[:, 5:6], st[:, 3:4], 1.0 / D, None, OP.mult)
            v.tensor_tensor(st[:, 6:7], st[:, 2:3], st[:, 2:3], OP.mult)
            v.tensor_tensor(st[:, 6:7], st[:, 5:6], st[:, 6:7], OP.subtract)
            self.w(s, "dve")
            s.activation(st[:, 7:8], st[:, 6:7], AF.Sqrt, bias=t["cconst"][:, 0:1])
            self.w(v, "act")
            v.reciprocal(st[:, 8:9], st[:, 7:8])
            par = a % 2
            self.w(self.sy, "dve")
            self.dma(self.sy, "ld", t["x0t3"][:, _ts(par, D)],
                     dr["x0"][CHK + a * CHK: CHK + (a + 1) * CHK, :])
            self.w(v, "ld")
            for n in range(2):
                v.tensor_scalar(t["x1t"][:, par * D + n * 512: par * D + (n + 1) * 512],
                                pb[n][:, 0:512], st[:, 2:3], st[:, 8:9], OP.subtract, OP.mult)
            v.tensor_tensor(t["x1t"][:, _ts(par, D)], t["x1t"][:, _ts(par, D)],
                                     t["x0t3"][:, _ts(par, D)], OP.add)
            self.w(te, "dve")
            self.w(self.g, "dve")
            self.dma(self.g, "sto", dr["x1_spill"][_ts(a, 128), :], t["x1t"][:, _ts(par, D)])
            if a >= 1 and BUILD_PHASES < 4:
                self.dma(self.g, "sto", dr["out"][_ts(a - 1, 128), :], t["x1t"][:, _ts(par, D)])
            if a >= 1:
                self.w(v, "sto")

    def emit_ffn_block(self):
        with ExitStack() as scB:
            sbB = lambda n, sh, dt=F32: self.t.__setitem__(
                n, scB.enter_context(self.nc.sbuf_tensor(n, sh, dt)))
            sbB("h2_fm", [128, 8 * OWN], BF16)
            sbB("ce_all", [128, NTL_OWN * NE], F32)
            sbB("ones8b", [8, 128], BF16)
            v = self.v
            self.w(v, "ld")
            v.tensor_copy(self.t["ones8b"][0:8, :], self.t["ones8"][0:8, :])
            self.p4_window_attn()
            if BUILD_PHASES >= 5:
                self.p5_gate()
            if BUILD_PHASES >= 6:
                self.p6_moe()
            self.p_taps(7)
            self.phase_barrier()

    def p4_window_attn(self):
        t, dr = self.t, self.dr
        v, s, te, g, sy = self.v, self.s, self.te, self.g, self.sy
        pb, st = self.pb, self.t["stat"]
        with ExitStack() as sc4:
            sb4 = lambda n, sh, dt=F32: self.t.__setitem__(
                n, sc4.enter_context(self.nc.sbuf_tensor(n, sh, dt)))
            sb4("wqkv", [128, 8 * 1536], BF16)
            sb4("projw", [128, 8 * 1024], BF16)
            sb4("awf", [128, 8], F32)
            sb4("wffm", [128, 8], F32)
            sb4("qkvb8f", [8, 1536], F32); sb4("qkvb8", [8, 1536], BF16)
            sb4("projb8f", [8, 1024], F32); sb4("projb8", [8, 1024], BF16)
            sb4("snk8", [8, 16], F32); sb4("snk", [128, 16], F32)
            sb4("mb", [128, 256], F32); sb4("mf", [128, 256], F32)
            sb4("x1ld", [128, 2 * D], F32)
            sb4("ropc", [128, 2 * D], F32)
            sb4("rops", [128, 2 * D], F32)
            sb4("y1t", [128, D], F32)
            sb4("hnb", [128, D], BF16)
            sb4("hfm", [128, 8 * 128], BF16)
            sb4("qkvt", [128, 1536], F32)
            sb4("shuf", [128, D], F32)
            sb4("rr", [128, D], F32)
            sb4("qrb", [128, D], BF16)
            sb4("krb", [128, 256], BF16)
            sb4("qhf", [64, 16 * 128], BF16)
            sb4("khf", [64, 2 * 4 * 128], BF16)
            sb4("vrng", [128, 2 * 256], BF16)
            sb4("lg", [128, 256], F32)
            sb4("pexp", [128, 256], BF16)
            sb4("att", [128, 256], BF16)
            sb4("ytm", [128, D], BF16)
            sb4("yfm", [128, 8 * 128], BF16)
            sb4("x2t", [128, 2 * D], F32)
            sb4("x2nb", [128, D], BF16)
            sb4("sq4", [128, D], F32)
            self.dma(g, "ldw", t["wqkv"][:].rearrange("p (k c) -> p k c", c=1536),
                     dr["qkv_w"].rearrange("(k p) c -> p k c", p=128))
            self.dma(g, "ldw", t["projw"][:].rearrange("p (k c) -> p k c", c=1024),
                     dr["proj_w"].rearrange("(k p) c -> p k c", p=128))
            self.dma(g, "ldw", t["awf"][:],
                     dr["attn_norm_w"].rearrange("o (k p) -> (o p) k", p=128),
                     allow_slow_non_contiguous=True)
            self.dma(g, "ldw", t["wffm"][:],
                     dr["ffn_norm_w"].rearrange("o (k p) -> (o p) k", p=128),
                     allow_slow_non_contiguous=True)
            self.dma(g, "ldw", t["mb"][:], dr["maskband"])
            self.dma(g, "ldw", t["mf"][:], dr["maskfirst"])
            for p in range(8):
                self.dma(sy, "ld", t["qkvb8f"][p:p + 1, :], dr["qkv_b"])
                self.dma(sy, "ld", t["projb8f"][p:p + 1, :], dr["proj_b"])
                self.dma(sy, "ld", t["snk8"][p:p + 1, :], dr["attn_sink"])
            self.w(v, "ld")
            v.tensor_scalar(t["qkvb8"][0:8, :], t["qkvb8f"][0:8, :], 0.125, None, OP.mult)
            v.tensor_scalar(t["projb8"][0:8, :], t["projb8f"][0:8, :], 0.125, None, OP.mult)
            v.tensor_scalar(t["snk8"][0:8, :], t["snk8"][0:8, :], 0.125, None, OP.mult)
            self.w(te, "dve")
            ins = te.matmul(pb[5][:, 0:16], t["ones8"][0:8, :], t["snk8"][0:8, :],
                            start=True, stop=True)
            self.inc(ins, "pe", 1)
            self.w(v, "pe")
            v.tensor_copy(t["snk"][:], pb[5][:, 0:16])
            self.w(v, "ldw")
            for k in range(8):
                v.tensor_scalar(t["wqkv"][:, _ts(k, 1536)], t["wqkv"][:, _ts(k, 1536)],
                                t["awf"][:, k:k + 1], None, OP.mult)
            self.w(te, "dve")
            for a in range(NTL_AT):
                par = a % 2
                prev, cur = 1 - par, par
                self.w(sy, "dve")
                self.dma(sy, "ld", t["x1ld"][:, _ts(par, D)], dr["x1_spill"][_ts(a, 128), :])
                self.dma(sy, "ld", t["ropc"][:, _ts(par, D)], dr["ropecos"][_ts(a, 128), :])
                self.dma(sy, "ld", t["rops"][:, _ts(par, D)], dr["ropesin"][_ts(a, 128), :])
                self.w(v, "ld"); self.w(s, "ld")
                x1 = t["x1ld"][:, _ts(par, D)]
                v.tensor_reduce(st[:, 0:1], x1, axis=mybir.AxisListType.X, op=OP.add)
                v.tensor_scalar(st[:, 1:2], st[:, 0:1], 1.0 / D, None, OP.mult)
                self.w(s, "dve")
                s.activation(t["sq4"][:], x1, AF.Square, accum_out=st[:, 2:3])
                self.w(v, "act")
                v.tensor_scalar(st[:, 3:4], st[:, 2:3], 1.0 / D, None, OP.mult)
                v.tensor_tensor(st[:, 4:5], st[:, 1:2], st[:, 1:2], OP.mult)
                v.tensor_tensor(st[:, 4:5], st[:, 3:4], st[:, 4:5], OP.subtract)
                self.w(s, "dve")
                s.activation(st[:, 5:6], st[:, 4:5], AF.Sqrt, bias=t["cconst"][:, 0:1])
                self.w(v, "act")
                v.reciprocal(st[:, 6:7], st[:, 5:6])
                v.tensor_scalar(t["y1t"][:], x1, st[:, 1:2], st[:, 6:7],
                                OP.subtract, OP.mult)
                self.w(s, "dve")
                s.activation(t["sq4"][:], t["y1t"][:], AF.Square, accum_out=st[:, 7:8])
                self.w(v, "act")
                v.tensor_scalar(st[:, 8:9], st[:, 7:8], 1.0 / D, None, OP.mult)
                self.w(s, "dve")
                s.activation(st[:, 9:10], st[:, 8:9], AF.Sqrt, bias=t["cconst"][:, 0:1])
                self.w(v, "act")
                v.reciprocal(st[:, 10:11], st[:, 9:10])
                v.tensor_scalar(t["hnb"][:], t["y1t"][:], st[:, 10:11], None, OP.mult)
                self.w(te, "dve")
                for j in range(8):
                    ptr = self.ptrA if j < 4 else self.ptrB
                    ins = te.transpose(ptr[:, _ts(j % 4, 128)], t["hnb"][:, _ts(j, 128)],
                                       t["identb"][:])
                self.inc(ins, "pe", 1)
                self.w(v, "pe")
                v.tensor_copy(t["hfm"][:, 0:512], self.ptrA[:, 0:512])
                v.tensor_copy(t["hfm"][:, 512:1024], self.ptrB[:, 0:512])
                self.w(te, "dve"); self.w(te, "ldw")
                for n in range(3):
                    for k in range(8):
                        te.matmul(pb[n][:, 0:512], t["hfm"][:, _ts(k, 128)],
                                  t["wqkv"][:, k * 1536 + n * 512: k * 1536 + (n + 1) * 512],
                                  start=(k == 0), stop=False, skip_group_check=True)
                    ins = te.matmul(pb[n][:, 0:512], t["ones8b"][0:8, :],
                                    t["qkvb8"][0:8, _ts(n, 512)],
                                    start=False, stop=True, skip_group_check=True)
                    self.inc(ins, "pe", 1)
                self.w(v, "pe")
                for n in range(3):
                    v.tensor_copy(t["qkvt"][:, _ts(n, 512)], pb[n][:, 0:512])
                self.w(te, "dve")
                cosT = t["ropc"][:, _ts(par, D)]
                sinT = t["rops"][:, _ts(par, D)]
                kk = t["qkvt"][:, 1024:1280]
                shkA = t["shuf"][:, 0:256].rearrange("p (i two) -> p i two", two=2)
                kkA = kk.rearrange("p (i two) -> p i two", two=2)
                v.tensor_copy(shkA[:, :, 0:1], kkA[:, :, 1:2])
                v.tensor_copy(shkA[:, :, 1:2], kkA[:, :, 0:1])
                v.tensor_tensor(t["rr"][:, 0:256], kk, cosT[:, 0:256], OP.mult)
                v.tensor_tensor(t["shuf"][:, 0:256], t["shuf"][:, 0:256], sinT[:, 0:256],
                                OP.mult)
                v.tensor_tensor(t["rr"][:, 0:256], t["rr"][:, 0:256], t["shuf"][:, 0:256],
                                OP.add)
                v.tensor_copy(t["krb"][:], t["rr"][:, 0:256])
                v.tensor_copy(t["vrng"][:, _ts(par, 256)], t["qkvt"][:, 1280:1536])
                self.w(te, "dve")
                for gi in range(4):
                    ins = te.transpose(self.ptrB[0:64, _ts(gi, 128)],
                                       t["krb"][:, _ts(gi, 64)], t["identb"][:])
                self.inc(ins, "pe", 1)
                self.w(v, "pe")
                v.tensor_copy(t["khf"][0:64, _ts(par, 512)], self.ptrB[0:64, 0:512])
                self.w(te, "dve")
                if a == 0:
                    continue
                qq = t["qkvt"][:, 0:1024]
                shA = t["shuf"][:].rearrange("p (i two) -> p i two", two=2)
                qqA = qq.rearrange("p (i two) -> p i two", two=2)
                v.tensor_copy(shA[:, :, 0:1], qqA[:, :, 1:2])
                v.tensor_copy(shA[:, :, 1:2], qqA[:, :, 0:1])
                v.tensor_tensor(t["rr"][:], qq, cosT, OP.mult)
                v.tensor_tensor(t["shuf"][:], t["shuf"][:], sinT, OP.mult)
                v.tensor_tensor(t["rr"][:], t["rr"][:], t["shuf"][:], OP.add)
                v.tensor_copy(t["qrb"][:], t["rr"][:])
                for hh in range(4):
                    self.w(te, "dve")
                    for j in range(4):
                        h = hh * 4 + j
                        ins = te.transpose(self.ptrA[0:64, _ts(j, 128)],
                                           t["qrb"][:, _ts(h, 64)], t["identb"][:])
                    self.inc(ins, "pe", 1)
                    self.w(v, "pe")
                    v.tensor_copy(t["qhf"][0:64, _ts(hh, 512)], self.ptrA[0:64, 0:512])
                mask = t["mf"] if a == 1 else t["mb"]
                for h in range(FH):
                    gi = h // 4
                    self.w(te, "dve"); self.w(te, "act")
                    for kt, slot in enumerate((prev, cur)):
                        ins = te.matmul(
                            pb[3][:, kt * 128:(kt + 1) * 128],
                            t["qhf"][0:64, _ts(h, 128)],
                            t["khf"][0:64, slot * 512 + gi * 128: slot * 512 + (gi + 1) * 128],
                            start=True, stop=True, skip_group_check=True)
                    self.inc(ins, "pe", 1)
                    self.w(v, "pe")
                    v.scalar_tensor_tensor(t["lg"][:], pb[3][:, 0:256], SCALE_AT, mask[:],
                                           OP.mult, OP.add)
                    v.tensor_reduce(st[:, 11:12], t["lg"][:], axis=mybir.AxisListType.X,
                                    op=OP.max)
                    v.tensor_scalar(st[:, 11:12], st[:, 11:12], t["snk"][:, h:h + 1],
                                    None, OP.max)
                    v.tensor_scalar(st[:, 12:13], st[:, 11:12], -1.0, None, OP.mult)
                    self.w(s, "dve")
                    s.activation(t["pexp"][:], t["lg"][:], AF.Exp, bias=st[:, 12:13],
                                 accum_out=st[:, 13:14])
                    s.activation(st[:, 14:15], t["snk"][:, h:h + 1], AF.Exp,
                                 bias=st[:, 12:13])
                    self.w(v, "act")
                    v.tensor_tensor(st[:, 13:14], st[:, 13:14], st[:, 14:15], OP.add)
                    v.reciprocal(st[:, 15:16], st[:, 13:14])
                    self.w(te, "act"); self.w(te, "dve")
                    for kt in range(2):
                        ins = te.transpose(self.ptrB[:, kt * 128:(kt + 1) * 128],
                                           t["pexp"][:, kt * 128:(kt + 1) * 128],
                                           t["identb"][:])
                    self.inc(ins, "pe", 1)
                    self.w(v, "pe")
                    v.tensor_copy(t["att"][:], self.ptrB[:, 0:256])
                    self.w(te, "dve")
                    for kt, slot in enumerate((prev, cur)):
                        ins = te.matmul(
                            pb[4][:, _ts(h % 8, 64)],
                            t["att"][:, kt * 128:(kt + 1) * 128],
                            t["vrng"][:, slot * 256 + gi * 64: slot * 256 + (gi + 1) * 64],
                            start=(kt == 0), stop=(kt == 1), skip_group_check=True)
                    self.inc(ins, "pe", 1)
                    self.w(v, "pe")
                    v.tensor_scalar(t["ytm"][:, _ts(h, 64)], pb[4][:, _ts(h % 8, 64)],
                                    st[:, 15:16], None, OP.mult)
                self.w(te, "dve")
                for j in range(8):
                    ptr = self.ptrA if j < 4 else self.ptrB
                    ins = te.transpose(ptr[:, _ts(j % 4, 128)], t["ytm"][:, _ts(j, 128)],
                                       t["identb"][:])
                self.inc(ins, "pe", 1)
                self.w(v, "pe")
                v.tensor_copy(t["yfm"][:, 0:512], self.ptrA[:, 0:512])
                v.tensor_copy(t["yfm"][:, 512:1024], self.ptrB[:, 0:512])
                self.w(te, "dve")
                for n in range(2):
                    for k in range(8):
                        te.matmul(pb[n][:, 0:512], t["yfm"][:, _ts(k, 128)],
                                  t["projw"][:, k * 1024 + n * 512: k * 1024 + (n + 1) * 512],
                                  start=(k == 0), stop=False, skip_group_check=True)
                    ins = te.matmul(pb[n][:, 0:512], t["ones8b"][0:8, :],
                                    t["projb8"][0:8, _ts(n, 512)],
                                    start=False, stop=True, skip_group_check=True)
                    self.inc(ins, "pe", 1)
                self.w(v, "pe")
                if a >= 2:
                    self.w(v, "sto")
                x2 = t["x2t"][:, _ts(par, D)]
                for n in range(2):
                    v.tensor_tensor(x2[:, _ts(n, 512)], pb[n][:, 0:512],
                                    t["y1t"][:, _ts(n, 512)], OP.add)
                self.w(te, "dve")
                self.w(g, "dve")
                self.dma(g, "sto", dr["x2_spill"][_ts(a - 1, 128), :], x2)
                self.w(s, "dve")
                s.activation(t["sq4"][:], x2, AF.Square, accum_out=st[:, 16:17])
                self.w(v, "act")
                v.tensor_scalar(st[:, 17:18], st[:, 16:17], 1.0 / D, None, OP.mult)
                self.w(s, "dve")
                s.activation(st[:, 18:19], st[:, 17:18], AF.Sqrt, bias=t["cconst"][:, 0:1])
                self.w(v, "act")
                v.reciprocal(st[:, 19:20], st[:, 18:19])
                v.tensor_scalar(t["x2nb"][:], x2, st[:, 19:20], None, OP.mult)
                self.w(te, "dve")
                for j in range(8):
                    ptr = self.ptrA if j < 4 else self.ptrB
                    ins = te.transpose(ptr[:, _ts(j % 4, 128)], t["x2nb"][:, _ts(j, 128)],
                                       t["identb"][:])
                self.inc(ins, "pe", 1)
                self.w(v, "pe")
                for j in range(8):
                    ptr = self.ptrA if j < 4 else self.ptrB
                    v.tensor_scalar(t["h2_fm"][:, j * OWN + (a - 1) * 128: j * OWN + a * 128],
                                    ptr[:, _ts(j % 4, 128)], t["wffm"][:, j:j + 1],
                                    None, OP.mult)
                self.w(te, "dve")
            self.p_taps(4)
            self.phase_barrier()

    def p5_gate(self):
        t, dr = self.t, self.dr
        v, s, te, g, sy = self.v, self.s, self.te, self.g, self.sy
        pb, st = self.pb, self.t["stat"]
        with ExitStack() as sc5:
            sb5 = lambda n, sh, dt=F32: self.t.__setitem__(
                n, sc5.enter_context(self.nc.sbuf_tensor(n, sh, dt)))
            sb5("gatew", [128, 8 * 32], BF16)
            sb5("gateb8f", [8, 32], F32)
            sb5("gateb8", [8, 32], BF16)
            sb5("glogt", [128, 32], F32)
            sb5("ebuf", [128, 4 * 32], F32)
            sb5("mvals", [128, 4], F32)
            sb5("tmp5", [128, 32], F32)
            self.dma(g, "ldw", t["gatew"][:].rearrange("p (k c) -> p k c", c=32),
                     dr["gate_w"].rearrange("(k p) c -> p k c", p=128))
            for p in range(8):
                self.dma(sy, "ld", t["gateb8f"][p:p + 1, :], dr["gate_b"])
            self.w(v, "ld")
            v.tensor_scalar(t["gateb8"][0:8, :], t["gateb8f"][0:8, :], 0.125, None, OP.mult)
            self.w(te, "ldw"); self.w(te, "dve")
            for tt in range(NTL_OWN):
                for k in range(8):
                    te.matmul(pb[5][:, 0:32],
                              t["h2_fm"][:, k * OWN + tt * 128: k * OWN + (tt + 1) * 128],
                              t["gatew"][:, k * 32:(k + 1) * 32],
                              start=(k == 0), stop=False, skip_group_check=True)
                ins = te.matmul(pb[5][:, 0:32], t["ones8b"][0:8, :], t["gateb8"][0:8, :],
                                start=False, stop=True, skip_group_check=True)
                self.inc(ins, "pe", 1)
                self.w(v, "pe")
                v.tensor_copy(t["glogt"][:], pb[5][:, 0:32])
                self.w(te, "dve")
                for r in range(TOPK):
                    v.tensor_reduce(t["mvals"][:, r:r + 1], t["glogt"][:],
                                    axis=mybir.AxisListType.X, op=OP.max)
                    v.tensor_scalar(t["ebuf"][:, _ts(r, 32)], t["glogt"][:],
                                    t["mvals"][:, r:r + 1], None, OP.is_equal)
                    if r < TOPK - 1:
                        v.scalar_tensor_tensor(t["glogt"][:], t["ebuf"][:, _ts(r, 32)],
                                               -1e30, t["glogt"][:], OP.mult, OP.add)
                v.tensor_copy(st[:, 20:21], t["mvals"][:, 0:1])
                v.tensor_scalar(t["mvals"][:], t["mvals"][:], st[:, 20:21], None,
                                OP.subtract)
                self.w(s, "dve")
                s.activation(t["mvals"][:], t["mvals"][:], AF.Exp, accum_out=st[:, 21:22])
                self.w(v, "act")
                v.reciprocal(st[:, 22:23], st[:, 21:22])
                v.tensor_scalar(t["mvals"][:], t["mvals"][:], st[:, 22:23], None, OP.mult)
                ce = t["ce_all"][:, _ts(tt, 32)]
                v.tensor_scalar(ce, t["ebuf"][:, 0:32], t["mvals"][:, 0:1], None, OP.mult)
                for r in range(1, TOPK):
                    v.tensor_scalar(t["tmp5"][:], t["ebuf"][:, _ts(r, 32)],
                                    t["mvals"][:, r:r + 1], None, OP.mult)
                    v.tensor_tensor(ce, ce, t["tmp5"][:], OP.add)
            self.p_taps(5)
            self.phase_barrier()

    def p6_moe(self):
        t, dr = self.t, self.dr
        v, s, te, g, sy = self.v, self.s, self.te, self.g, self.sy
        pb, st = self.pb, self.t["stat"]
        with ExitStack() as sc6:
            sb6 = lambda n, sh, dt=F32: self.t.__setitem__(
                n, sc6.enter_context(self.nc.sbuf_tensor(n, sh, dt)))
            sb6("wup", [128, 2 * 8 * 256], BF16)
            sb6("wdn", [128, 2 * 8 * 1024], BF16)
            sb6("act6", [128, 8 * 1024], BF16)
            sb6("sg", [128, 512], F32)
            sb6("tmp6", [128, 512], F32)
            sb6("y2b", [128, 8 * 1024], F32)
            sb6("x2ld", [128, 2 * D], F32)
            sb6("x1l7", [128, 2 * D], F32)
            sb6("ffnt", [128, 2 * D], F32)
            sb6("sq6", [128, D], F32)

            def up_slice_dma(b, e, m):
                mp = m % 2
                wu3 = dr["W_up"][e].rearrange("(k p) c -> p k c", p=128)
                self.w(g, "pe")
                self.dma(g, "ldw",
                         t["wup"][:].rearrange("p (ms k c) -> p ms k c", ms=2, c=256)
                         [:, mp, :, 0:128],
                         wu3[:, :, m * 128:(m + 1) * 128])
                self.dma(g, "ldw",
                         t["wup"][:].rearrange("p (ms k c) -> p ms k c", ms=2, c=256)
                         [:, mp, :, 128:256],
                         wu3[:, :, DFF + m * 128: DFF + (m + 1) * 128])

            triples = [(b, e, m) for b in range(2) for e in range(NE) for m in range(8)]
            up_slice_dma(*triples[0])
            dve_mark = {}
            for b in range(2):
                self.w(v, "pe")
                v.memset(t["y2b"][:], 0.0)
                for e in range(NE):
                    em = e % 2
                    self.w(g, "pe")
                    self.dma(g, "ldd",
                             t["wdn"][:].rearrange("p (es k c) -> p es k c", es=2, c=1024)
                             [:, em, :, :],
                             dr["W_down"][e].rearrange("(k p) c -> p k c", p=128))
                    for m in range(8):
                        mp = m % 2
                        ti = b * NE * 8 + e * 8 + m
                        if ti + 1 < len(triples):
                            up_slice_dma(*triples[ti + 1])
                        self.w(te, "ldw")
                        for nb2 in range(2):
                            bu, bg = pb[2 * nb2], pb[2 * nb2 + 1]
                            key_u, key_g = ("up", nb2, 0), ("up", nb2, 1)
                            te.wait_ge(self.sems["dve"], dve_mark.get(key_u, 0))
                            te.wait_ge(self.sems["dve"], dve_mark.get(key_g, 0))
                            te.wait_ge(self.sems["act"], dve_mark.get(("upa", nb2), 0))
                            tok = b * 1024 + nb2 * 512
                            for k in range(8):
                                te.matmul(bu[:, 0:512],
                                          t["wup"][:, (mp * 8 + k) * 256:
                                                   (mp * 8 + k) * 256 + 128],
                                          t["h2_fm"][:, k * OWN + tok: k * OWN + tok + 512],
                                          start=(k == 0), stop=(k == 7),
                                          skip_group_check=True)
                            for k in range(8):
                                ins = te.matmul(bg[:, 0:512],
                                                t["wup"][:, (mp * 8 + k) * 256 + 128:
                                                         (mp * 8 + k) * 256 + 256],
                                                t["h2_fm"][:, k * OWN + tok:
                                                           k * OWN + tok + 512],
                                                start=(k == 0), stop=(k == 7),
                                                skip_group_check=True)
                            self.inc(ins, "pe", 2)
                            self.w(s, "pe")
                            s.activation(t["sg"][:], bg[:, 0:512], AF.Silu)
                            dve_mark[("upa", nb2)] = self.cnt["act"]
                            self.w(v, "act"); self.w(v, "pe")
                            v.tensor_tensor(
                                t["act6"][:, m * 1024 + nb2 * 512: m * 1024 + nb2 * 512 + 512],
                                t["sg"][:], bu[:, 0:512], OP.mult)
                            dve_mark[key_u] = self.cnt["dve"]
                            dve_mark[key_g] = self.cnt["dve"]
                    self.w(te, "ldd")
                    for half in range(2):
                        for tt8 in range(8):
                            bank = pb[4 + tt8 % 2]
                            key = ("dn", tt8 % 2)
                            te.wait_ge(self.sems["dve"], dve_mark.get(key, 0))
                            for k in range(8):
                                ins = te.matmul(
                                    bank[:, 0:512],
                                    t["act6"][:, k * 1024 + tt8 * 128:
                                              k * 1024 + (tt8 + 1) * 128],
                                    t["wdn"][:, (em * 8 + k) * 1024 + half * 512:
                                             (em * 8 + k) * 1024 + half * 512 + 512],
                                    start=(k == 0), stop=(k == 7), skip_group_check=True)
                            self.inc(ins, "pe", 1)
                            self.w(v, "pe")
                            cecol = t["ce_all"][:, (b * 8 + tt8) * 32 + e:
                                               (b * 8 + tt8) * 32 + e + 1]
                            v.tensor_scalar(t["tmp6"][:], bank[:, 0:512], cecol, None,
                                            OP.mult)
                            v.tensor_tensor(
                                t["y2b"][:, tt8 * 1024 + half * 512:
                                         tt8 * 1024 + half * 512 + 512],
                                t["y2b"][:, tt8 * 1024 + half * 512:
                                         tt8 * 1024 + half * 512 + 512],
                                t["tmp6"][:], OP.add)
                            dve_mark[key] = self.cnt["dve"]
                for tt8 in range(8):
                    par7 = tt8 % 2
                    self.w(sy, "dve")
                    self.dma(sy, "ld", t["x2ld"][:, _ts(par7, D)],
                             dr["x2_spill"][_ts(b * 8 + tt8, 128), :])
                    self.dma(sy, "ld", t["x1l7"][:, _ts(par7, D)],
                             dr["x1_spill"][_ts(1 + b * 8 + tt8, 128), :])
                    self.w(v, "ld")
                    if b * 8 + tt8 >= 2:
                        self.w(v, "sto")
                    ff = t["ffnt"][:, _ts(par7, D)]
                    v.tensor_tensor(ff, t["y2b"][:, _ts(tt8, D)],
                                    t["x2ld"][:, _ts(par7, D)], OP.add)
                    v.tensor_reduce(st[:, 24:25], ff, axis=mybir.AxisListType.X, op=OP.add)
                    v.tensor_scalar(st[:, 25:26], st[:, 24:25], 1.0 / D, None, OP.mult)
                    self.w(s, "dve")
                    s.activation(t["sq6"][:], ff, AF.Square, accum_out=st[:, 26:27])
                    self.w(v, "act")
                    v.tensor_scalar(st[:, 27:28], st[:, 26:27], 1.0 / D, None, OP.mult)
                    v.tensor_tensor(st[:, 28:29], st[:, 25:26], st[:, 25:26], OP.mult)
                    v.tensor_tensor(st[:, 28:29], st[:, 27:28], st[:, 28:29], OP.subtract)
                    self.w(s, "dve")
                    s.activation(st[:, 29:30], st[:, 28:29], AF.Sqrt,
                                 bias=t["cconst"][:, 0:1])
                    self.w(v, "act")
                    v.reciprocal(st[:, 30:31], st[:, 29:30])
                    v.tensor_scalar(ff, ff, st[:, 25:26], st[:, 30:31],
                                    OP.subtract, OP.mult)
                    v.tensor_tensor(ff, ff, t["x1l7"][:, _ts(par7, D)], OP.add)
                    self.w(g, "dve")
                    self.dma(g, "sto", dr["out"][_ts(b * 8 + tt8, 128), :], ff)
            self.p_taps(6)

    def p_taps(self, phase):
        t, dr = self.t, self.dr
        g = self.g
        for name, tile, lo, hi, ph in DEBUG_TAPS:
            if ph != phase:
                continue
            self.w(g, "dve"); self.w(g, "act"); self.w(g, "pe")
            ap = t[tile][:, lo:hi]
            self.dma(g, "sto", dr["tap_" + name][0:ap.shape[0], :], ap)

    def p_finish(self):
        g = self.g
        self.w(g, "dve"); self.w(g, "act"); self.w(g, "pe")
        g.wait_ge(self.sems["sto"], self.cnt["sto"])



_CACHED = {}


def _build_cached():
    key = (BUILD_PHASES, tuple(n for n, *_ in DEBUG_TAPS))
    if key not in _CACHED:
        kb = KB()
        _CACHED[key] = kb.build()
    return _CACHED[key]


def _constants():
    c = {}
    c["identsb"] = np.eye(128, dtype=BF)
    c["identsf"] = np.eye(128, dtype=np.float32)
    c["ones8c"] = np.ones((8, 128), np.float32)
    c["ones1r"] = np.ones((1, 128), np.float32)
    cc = np.zeros((128, 4), np.float32)
    cc[:, 0] = LN_EPS; cc[:, 1] = L2_EPS; cc[:, 2] = PI / 2; cc[:, 3] = 1.0
    c["cconstc"] = cc
    tt2 = np.arange(128)
    c["cumsumc"] = (tt2[:, None] <= tt2[None, :]).astype(np.float32)
    e8 = np.zeros((8, 8 * CHK), np.float32)
    for hp in range(8):
        e8[hp, hp * CHK:(hp + 1) * CHK] = 1.0
    c["e8bcc"] = e8
    c["e8negc"] = -e8
    c["e8diagc"] = np.eye(8, dtype=np.float32)
    i_idx = np.arange(CHK)[:, None]
    s_idx = np.arange(CHK)[None, :]
    tril = (s_idx < i_idx).astype(np.float32)
    diag = (i_idx <= s_idx).astype(np.float32)
    c["tril01c"] = np.tile(tril, (1, 8)).astype(BF)
    c["diag01c"] = np.tile(diag, (1, 8)).astype(BF)
    invf = (1.0 / 10000.0 ** (np.arange(0, DK, 2, dtype=np.float32) / DK)).astype(np.float32)
    iv8 = np.zeros((8, 8 * 32), np.float32)
    for hp in range(8):
        iv8[hp, hp * 32:(hp + 1) * 32] = invf
    c["invf8c"] = iv8
    r = np.arange(CHK)[:, None]
    j = np.arange(2 * CHK)[None, :]
    band = np.where((j >= r) & (j <= r + CHK), 0.0, -1e30).astype(np.float32)
    first = np.where((j >= CHK) & (j <= r + CHK) & (j >= r), 0.0, -1e30).astype(np.float32)
    c["maskband"] = band
    c["maskfirst_normal"] = band
    c["maskfirst_first"] = first
    pos = np.arange(T, dtype=np.float64) / 32.0
    inv2 = 1.0 / (150000.0 ** (np.arange(0, FDH, 2, dtype=np.float64) / FDH))
    fr = pos[:, None] * inv2[None, :]
    cos = np.cos(fr); sin = np.sin(fr)
    cos2 = np.repeat(cos, 2, axis=1)
    sin2 = np.repeat(sin, 2, axis=1)
    sin2[:, 0::2] *= -1.0
    c["ropecos_full"] = np.tile(cos2, (1, FH)).astype(np.float32)
    c["ropesin_full"] = np.tile(sin2, (1, FH)).astype(np.float32)
    return c


def _stage_in_maps(inputs):
    cst = _constants()
    xfull = np.ascontiguousarray(inputs["L0_attn_hidden_states"])
    wbt = np.concatenate([np.asarray(inputs["L0_attn_Wbeta"]),
                          np.asarray(inputs["L0_attn_Wtg"])], axis=1)
    shared = {
        "Wq": inputs["L0_attn_Wq"], "Wk": inputs["L0_attn_Wk"], "Wv": inputs["L0_attn_Wv"],
        "Wbt": wbt, "Wg": inputs["L0_attn_Wg"], "Wo": inputs["L0_attn_Wo"],
        "log_decay": inputs["L0_attn_log_decay"],
        "mode_logits": inputs["L0_attn_mode_logits"],
        "omega_ls": inputs["L0_attn_omega_log_scale"],
        "attn_norm_w": np.asarray(inputs["L0_ffn_attn_norm_w"])[None, :],
        "ffn_norm_w": np.asarray(inputs["L0_ffn_ffn_norm_w"])[None, :],
        "qkv_w": inputs["L0_ffn_qkv_w"], "qkv_b": np.asarray(inputs["L0_ffn_qkv_b"])[None, :],
        "proj_w": inputs["L0_ffn_proj_w"], "proj_b": np.asarray(inputs["L0_ffn_proj_b"])[None, :],
        "attn_sink": np.asarray(inputs["L0_ffn_attn_sink"])[None, :],
        "gate_w": inputs["L0_ffn_gate_w"], "gate_b": np.asarray(inputs["L0_ffn_gate_b"])[None, :],
        "W_up": inputs["L0_ffn_W_up"], "W_down": inputs["L0_ffn_W_down"],
        "identsb": cst["identsb"], "identsf": cst["identsf"],
        "ones8c": cst["ones8c"], "ones1r": cst["ones1r"],
        "e8bcc": cst["e8bcc"], "e8negc": cst["e8negc"], "e8diagc": cst["e8diagc"],
        "tril01c": cst["tril01c"], "diag01c": cst["diag01c"], "invf8c": cst["invf8c"],
        "cconstc": cst["cconstc"], "cumsumc": cst["cumsumc"],
        "maskband": cst["maskband"],
    }
    shared = {k: np.ascontiguousarray(np.asarray(val)) for k, val in shared.items()}
    in_maps = []
    for core in range(NCORE):
        seq, p = core // 4, core % 4
        base = p * OWN
        x0 = np.zeros((NT_DN, D), np.float32)
        lo = base - 256
        if lo < 0:
            x0[256:] = xfull[seq, base:base + OWN]
        else:
            x0[:] = xfull[seq, lo:base + OWN]
        posmat = (np.arange(128)[:, None] + np.arange(NCH)[None, :] * 128 + (base - 256)
                  ).astype(np.float32)
        at_lo = base - 128
        if at_lo < 0:
            ropec = np.zeros((NT_AT, D), np.float32)
            ropes = np.zeros((NT_AT, D), np.float32)
            ropec[128:] = cst["ropecos_full"][base:base + OWN]
            ropes[128:] = cst["ropesin_full"][base:base + OWN]
        else:
            ropec = cst["ropecos_full"][at_lo:base + OWN]
            ropes = cst["ropesin_full"][at_lo:base + OWN]
        m = dict(shared)
        m["x0"] = x0
        m["pos_tc"] = posmat
        m["maskfirst"] = cst["maskfirst_first"] if p == 0 else cst["maskfirst_normal"]
        m["ropecos"] = np.ascontiguousarray(ropec)
        m["ropesin"] = np.ascontiguousarray(ropes)
        in_maps.append(m)
    return in_maps


def kernel(**inputs):
    nc = _build_cached()
    in_maps = _stage_in_maps(inputs)
    res = run_bass_kernel_spmd(nc, in_maps, core_ids=list(range(NCORE)))
    outs = [res.results[i]["out"] for i in range(NCORE)]
    full = np.concatenate(outs, axis=0).reshape(2, T, D).astype(np.float32)
    if DEBUG_TAPS:
        kernel.taps = [{k: v for k, v in r.items() if k.startswith("tap_")}
                       for r in res.results]
    return full


# version 02
# speedup vs baseline: 1.4867x; 1.4867x over previous
import numpy as np
import ml_dtypes
import concourse.bass as bass
import concourse.mybir as mybir
from concourse.bass_utils import run_bass_kernel_spmd
from concourse.alu_op_type import AluOpType as OP
from contextlib import ExitStack

F32 = mybir.dt.float32
BF16 = mybir.dt.bfloat16
AF = mybir.ActivationFunctionType
BF = ml_dtypes.bfloat16

D = 1024
T = 8192
NCORE = 8
OWN = 2048
CHK = 128
NT_DN = OWN + 256
NT_AT = OWN + 128
NCH = NT_DN // CHK
NTL_AT = NT_AT // CHK
NTL_OWN = OWN // CHK
H, DK, DV, NM = 8, 64, 64, 8
FH, FKV, FDH = 16, 4, 64
NE, DFF = 32, 1024
TOPK = 4
LN_EPS = 1e-5
L2_EPS = 1e-6
CLIP = 1e-6
SCALE_DN = float(DK) ** -0.5
SCALE_AT = float(FDH) ** -0.5
MAGIC = 12582912.0
TWO_PI = float(2 * np.pi)
INV_2PI = float(1.0 / TWO_PI)
RR_C1 = 6.28125
RR_C2 = float(np.float32(TWO_PI) - np.float32(6.28125))
PI = float(np.pi)

DEBUG_TAPS = []
DN_STAGE = 99
DN_CHUNKS = 99
BUILD_PHASES = 99


def _ts(i, n):
    return slice(i * n, (i + 1) * n)


_CHAIN_OPS = {
    "tensor_copy", "tensor_tensor", "tensor_scalar", "tensor_single_scalar",
    "scalar_tensor_tensor", "tensor_reduce", "reduce_max", "reduce_sum", "memset",
    "memzero", "reciprocal", "tensor_tensor_scan", "max_index", "max_with_indices",
    "select", "activation", "copy", "add", "mul", "sqrt", "square", "sign",
    "tensor_add", "tensor_mul", "tensor_sub", "tensor_max", "tensor_relu",
    "tensor_scalar_add", "tensor_scalar_mul", "tensor_scalar_sub", "tensor_scalar_max",
    "tensor_scalar_min",
}


class _Chain:

    def __init__(self, kb, eng, sem):
        self._kb = kb
        self._eng = eng
        self._sem = sem

    def __getattr__(self, name):
        real = getattr(self._eng, name)
        if name not in _CHAIN_OPS:
            return real
        kb, eng, sem = self._kb, self._eng, self._sem

        def call(*a, **k):
            eng.wait_ge(kb.sems[sem], kb.cnt[sem])
            ins = real(*a, **k)
            kb.cnt[sem] += 1
            ins.then_inc(kb.sems[sem], 1)
            return ins
        return call


class KB:
    def __init__(self):
        self.nc = bass.Bass("TRN2", target_bir_lowering=False, debug=False,
                            detect_race_conditions=False)
        self.t = {}
        self.dr = {}
        self.cnt = {}
        self.sems = {}

    def sem(self, name):
        self.sems[name] = self.stack.enter_context(self.nc.semaphore(name))
        self.cnt[name] = 0

    def inc(self, instr, name, n=1):
        self.cnt[name] += n
        instr.then_inc(self.sems[name], n)

    def w(self, eng, name):
        eng.wait_ge(self.sems[name], self.cnt[name])

    def dma(self, eng, sem, out, in_, **kw):
        eng.wait_ge(self.sems[sem], self.cnt[sem])
        self.inc(eng.dma_start(out=out, in_=in_, **kw), sem, 16)

    def mk_chains(self):
        self._vchain = _Chain(self, self.nc.vector, "dve")
        self._schain = _Chain(self, self.nc.scalar, "act")

    def din(self, name, shape, dt=F32):
        self.dr[name] = self.nc.dram_tensor(name, shape, dt, kind="ExternalInput").ap()

    def sb(self, name, sh, dt=F32):
        self.t[name] = self.stack.enter_context(self.nc.sbuf_tensor(name, sh, dt))

    @property
    def v(self): return self._vchain
    @property
    def s(self): return self._schain
    @property
    def te(self): return self.nc.tensor
    @property
    def g(self): return self.nc.gpsimd
    @property
    def sy(self): return self.nc.sync

    def build(self):
        nc = self.nc
        for name, shape, dt in [
            ("x0", [NT_DN, D], F32),
            ("Wq", [D, 512], F32), ("Wk", [D, 512], F32), ("Wv", [D, 512], F32),
            ("Wbt", [D, 128], F32), ("Wg", [D, 512], F32), ("Wo", [512, D], F32),
            ("log_decay", [8, 8], F32), ("mode_logits", [8, 8], F32), ("omega_ls", [8, 8], F32),
            ("attn_norm_w", [1, D], F32), ("ffn_norm_w", [1, D], F32),
            ("qkv_w", [D, 1536], F32), ("qkv_b", [1, 1536], F32),
            ("proj_w", [D, D], F32), ("proj_b", [1, D], F32),
            ("attn_sink", [1, FH], F32),
            ("gate_w", [D, NE], F32), ("gate_b", [1, NE], F32),
            ("W_up", [NE, D, 2 * DFF], F32), ("W_down", [NE, DFF, D], F32),
            ("identsb", [128, 128], BF16), ("identsf", [128, 128], F32),
            ("ones8c", [8, 128], F32), ("ones1r", [1, 128], F32),
            ("e8bcc", [8, 8 * CHK], F32), ("e8negc", [8, 8 * CHK], F32),
            ("e8diagc", [8, 8], F32),
            ("tril01c", [128, 8 * CHK], BF16), ("diag01c", [128, 8 * CHK], BF16),
            ("pos_tc", [128, NCH], F32), ("invf8c", [8, 8 * 32], F32),
            ("cconstc", [128, 4], F32),
            ("cumsumc", [128, 128], F32),
            ("maskband", [128, 2 * CHK], F32), ("maskfirst", [128, 2 * CHK], F32),
            ("ropecos", [NT_AT, D], F32), ("ropesin", [NT_AT, D], F32),
        ]:
            self.din(name, shape, dt)
        self.dr["out"] = nc.dram_tensor("out", [OWN, D], F32, kind="ExternalOutput").ap()
        self.dr["x1_spill"] = nc.dram_tensor("x1_spill", [NT_AT, D], F32, kind="Internal").ap()
        self.dr["x2_spill"] = nc.dram_tensor("x2_spill", [OWN, D], F32, kind="Internal").ap()
        self.dr["flat_scr"] = nc.dram_tensor("flat_scr", [16, 64], F32, kind="Internal").ap()
        for name, tile, lo, hi, ph in DEBUG_TAPS:
            self.dr["tap_" + name] = nc.dram_tensor(
                "tap_" + name, [128, hi - lo], F32, kind="ExternalOutput").ap()

        with ExitStack() as stack:
            self.stack = stack
            for sname in ["ld", "ldw", "ldw2", "ldd", "pe", "dve", "act", "sto"]:
                self.sem(sname)
            self.mk_chains()
            sb = self.sb
            ps = lambda name, sh, dt=F32: self.t.__setitem__(
                name, stack.enter_context(nc.psum_tensor(name, sh, dt)))
            sb("identb", [128, 128], BF16); sb("identf", [128, 128], F32)
            sb("ones8", [8, 128], F32); sb("ones1", [1, 128], F32)
            sb("stat", [128, 8 * NCH], F32)
            sb("cconst", [128, 4], F32)
            self.pb = []
            for i in range(6):
                ps(f"pb{i}", [128, 512], F32)
                self.pb.append(self.t[f"pb{i}"])
            ps("ptrA", [128, 512], BF16)
            ps("ptrB", [128, 512], BF16)
            self.ptrA = self.t["ptrA"]; self.ptrB = self.t["ptrB"]

            blk = stack.enter_context(nc.Block())

            @blk.vector
            def _(_v):
                self.emit()
        return nc

    def emit(self):
        sb = self.sb
        self.p_const_smalls()
        with ExitStack() as scA:
            sbA = lambda n, sh, dt=F32: self.t.__setitem__(
                n, scA.enter_context(self.nc.sbuf_tensor(n, sh, dt)))
            sbA("q_tm", [128, NCH * 512], BF16)
            sbA("k_tm", [128, NCH * 512], BF16)
            sbA("v_tm", [128, NCH * 512], BF16)
            sbA("beta", [128, NCH * 64], F32)
            sbA("g_tm", [128, NCH * 64], F32)
            sbA("gate", [128, NCH * 512], BF16)
            sbA("og", [128, NCH * 512], BF16)
            sbA("rsq_q", [128, NCH * 8], F32)
            sbA("rsq_k", [128, NCH * 8], F32)
            sbA("wrow", [128, 64], F32)
            sbA("rhow", [128, 64], F32)
            sbA("pos", [128, NCH], F32)
            sbA("omf", [128, NM * 256], F32)
            sbA("e8diag", [8, 8], F32)
            sbA("e8bc", [8, 8 * CHK], F32)
            sbA("invf8", [8, 8 * 32], F32)
            sbA("cumsum", [128, 128], F32)
            sbA("sm64", [128, 64], F32); sbA("sm64b", [128, 64], F32)
            sbA("sm8", [8, 256], F32); sbA("sm8b", [8, 256], F32)
            self.p_const_dn()
            self.emit_attn_block()
        if BUILD_PHASES >= 4:
            self.emit_ffn_block()
        self.p_finish()

    def emit_attn_block(self):
        sb = self.sb
        with ExitStack() as sc1:
            self._scope = sc1
            sb1 = lambda n, sh, dt=F32: self.t.__setitem__(
                n, sc1.enter_context(self.nc.sbuf_tensor(n, sh, dt)))
            sb1("lnx_fm", [128, 8 * NT_DN], BF16)
            sb1("wbuf", [128, 8 * 512], BF16)
            sb1("x0t", [128, 2 * D], F32)
            sb1("lnt", [128, 2 * D], BF16)
            sb1("sqscr", [128, D], F32)
            self.p1a_small()
            self.p0_lnx()
            if BUILD_PHASES >= 1:
                self.p1b_proj()
            if BUILD_PHASES >= 1.5:
                self.p1c_cumsum()
            self.p_taps(1)
            self.phase_barrier()
        if BUILD_PHASES >= 2:
            with ExitStack() as sc2:
                sb2 = lambda n, sh, dt=F32: self.t.__setitem__(
                    n, sc2.enter_context(self.nc.sbuf_tensor(n, sh, dt)))
                sb2("S", [64, 64 * DV], F32)
                sb2("S_bf", [64, 64 * DV], BF16)
                sb2("gmc", [8, 8 * 128], F32)
                sb2("ghi", [128, 64], BF16)
                sb2("glo", [128, 64], BF16)
                sb2("tril01", [128, 8 * CHK], BF16)
                sb2("diag01", [128, 8 * CHK], BF16)
                sb2("trg", [128, 256], F32); sb2("trg2", [128, 256], F32)
                sb2("trg3", [128, 256], F32)
                sb2("sinv", [128, 256], F32); sb2("cosv", [128, 256], F32)
                sb2("rot1", [128, 256], F32); sb2("rot2", [128, 256], F32)
                sb2("rot", [128, 512], F32)
                sb2("qe_tm", [128, 512], BF16); sb2("kr_tm", [128, 512], BF16)
                sb2("qn_tm", [128, 512], BF16); sb2("qn_fm", [64, 1024], BF16)
                sb2("qe_fm", [64, 1024], BF16); sb2("kr_fm", [64, 1024], BF16)
                sb2("fexp", [128, 1024], F32)
                sb2("btril", [128, 1024], BF16)
                sb2("eargp", [128, 1024], BF16)
                sb2("earg2", [128, 1024], BF16)
                sb2("x0m", [128, 1024], BF16)
                sb2("y0m", [128, 1024], BF16)
                sb2("ptm", [128, 1024], BF16)
                sb2("xpow", [128, 2 * 1024], BF16)
                sb2("ypow", [128, 2 * 1024], BF16)
                sb2("dsol", [128, 512], BF16)
                sb2("dfp", [128, 512], F32)
                sb2("bexp", [128, 512], F32)
                sb2("bexp2", [128, 512], F32)
                sb2("f8", [128, 32], F32)
                sb2("sups", [64, 1088], F32)
                sb2("d2bf", [128, 512], BF16)
                self.load_dn_consts()
                self.p2_deltanet()
                self.p_taps(2)
                self.phase_barrier()
        if BUILD_PHASES >= 2.7:
            with ExitStack() as sc3:
                sb3 = lambda n, sh, dt=F32: self.t.__setitem__(
                    n, sc3.enter_context(self.nc.sbuf_tensor(n, sh, dt)))
                sb3("wo", [128, 4 * 1024], BF16)
                sb3("x0t3", [128, 2 * D], F32)
                sb3("x1t", [128, 2 * D], F32)
                sb3("aofm", [128, 512], BF16)
                sb3("sqscr3", [128, D], F32)
                if BUILD_PHASES >= 3:
                    self.p3_attn_out()
                self.p_taps(3)
                self.phase_barrier()

    def phase_barrier(self):
        for eng in [self.v, self.s, self.te, self.g, self.sy]:
            for sname in ["ld", "ldw", "ldw2", "ldd", "pe", "dve", "act", "sto"]:
                eng.wait_ge(self.sems[sname], self.cnt[sname])

    def p_const_smalls(self):
        t, dr = self.t, self.dr
        for name, dst in [("identsb", "identb"), ("identsf", "identf"), ("ones8c", "ones8"),
                          ("ones1r", "ones1"), ("cconstc", "cconst")]:
            self.dma(self.sy, "ld", t[dst][:], dr[name])

    def p_const_dn(self):
        t, dr = self.t, self.dr
        for name, dst in [("e8diagc", "e8diag"),
                          ("invf8c", "invf8"), ("pos_tc", "pos"),
                          ("e8bcc", "e8bc"),
                          ("cumsumc", "cumsum")]:
            self.dma(self.sy, "ld", t[dst][:], dr[name])
        self.dma(self.sy, "ld", t["sm8"][:, 0:8], dr["log_decay"])
        self.dma(self.sy, "ld", t["sm8"][:, 8:16], dr["mode_logits"])
        self.dma(self.sy, "ld", t["sm8"][:, 16:24], dr["omega_ls"])

    def load_dn_consts(self):
        t, dr = self.t, self.dr
        for name, dst in [("tril01c", "tril01"), ("diag01c", "diag01")]:
            self.dma(self.sy, "ld", t[dst][:], dr[name])

    def p1a_small(self):
        t, dr = self.t, self.dr
        v, s, te, g = self.v, self.s, self.te, self.g
        sm = t["sm8"]
        self.w(v, "ld"); self.w(s, "ld")
        s.activation(sm[:, 24:32], sm[:, 0:8], AF.Abs)
        self.w(s, "act")
        s.activation(sm[:, 32:40], sm[:, 24:32], AF.Exp, scale=-1.0)
        self.w(s, "act")
        s.activation(sm[:, 40:48], sm[:, 32:40], AF.Ln, bias=self.t["cconst"][0:8, 3:4])
        self.w(v, "act")
        v.tensor_scalar(sm[:, 48:56], sm[:, 0:8], 0.0, None, OP.max)
        v.tensor_tensor(sm[:, 48:56], sm[:, 48:56], sm[:, 40:48], OP.add)
        self.w(s, "dve")
        s.activation(sm[:, 56:64], sm[:, 48:56], AF.Exp, scale=-1.0)
        v.tensor_reduce(sm[:, 64:65], sm[:, 8:16], axis=mybir.AxisListType.X,
                                 op=OP.max)
        v.tensor_scalar(sm[:, 72:80], sm[:, 8:16], sm[:, 64:65], None, OP.subtract)
        self.w(s, "dve")
        s.activation(sm[:, 80:88], sm[:, 72:80], AF.Exp, accum_out=sm[:, 88:89])
        self.w(v, "act")
        v.reciprocal(sm[:, 96:97], sm[:, 88:89])
        v.tensor_scalar(sm[:, 104:112], sm[:, 80:88], sm[:, 96:97], SCALE_DN,
                                 OP.mult, OP.mult)
        self.w(s, "dve")
        s.activation(sm[:, 112:120], sm[:, 16:24], AF.Exp)
        e8sel = self.t["e8bc"][:].rearrange("p (h s) -> p h s", h=8)[:, :, 0:8]
        smb = self.t["sm8b"]
        self.w(v, "act"); self.w(v, "dve")
        v.tensor_copy(smb[0:8, 0:64].rearrange("p (h m) -> p h m", m=8),
                      sm[:, 56:64].unsqueeze(1).broadcast_to([8, 8, 8]))
        v.tensor_tensor(smb[0:8, 0:64].rearrange("p (h m) -> p h m", m=8),
                        smb[0:8, 0:64].rearrange("p (h m) -> p h m", m=8), e8sel, OP.mult)
        v.tensor_copy(smb[0:8, 64:128].rearrange("p (h m) -> p h m", m=8),
                      sm[:, 104:112].unsqueeze(1).broadcast_to([8, 8, 8]))
        v.tensor_tensor(smb[0:8, 64:128].rearrange("p (h m) -> p h m", m=8),
                                 smb[0:8, 64:128].rearrange("p (h m) -> p h m", m=8),
                                 e8sel, OP.mult)
        self.w(te, "dve"); self.w(te, "ld")
        ins = te.matmul(self.pb[5][:, 0:128], t["ones8"][:], smb[0:8, 0:128],
                        start=True, stop=True)
        self.inc(ins, "pe", 1)
        self.w(v, "pe")
        v.tensor_copy(t["rhow"][:], self.pb[5][:, 0:64])
        v.tensor_copy(t["wrow"][:], self.pb[5][:, 64:128])
        self.w(te, "dve")
        for m in range(NM):
            v.tensor_scalar(t["sm8b"][:, 0:256], t["invf8"][:],
                                     sm[:, 112 + m:113 + m], None, OP.mult)
            self.w(te, "dve")
            ins = te.matmul(self.pb[5][:, 0:256], t["ones8"][:], t["sm8b"][:, 0:256],
                            start=True, stop=True)
            self.inc(ins, "pe", 1)
            self.w(v, "pe")
            v.tensor_copy(t["omf"][:, _ts(m, 256)], self.pb[5][:, 0:256])
            self.w(te, "dve")

    def p0_lnx(self):
        t, dr = self.t, self.dr
        v, s, te = self.v, self.s, self.te
        stat = t["stat"]
        rel = {}
        for i in range(NCH):
            par = i % 2
            if i >= 2:
                self.sy.wait_ge(self.sems["act"], rel[i - 2])
            self.dma(self.sy, "ld", t["x0t"][:, _ts(par, D)], dr["x0"][_ts(i, 128), :])
            x = t["x0t"][:, _ts(par, D)]
            self.w(v, "ld")
            v.tensor_reduce(stat[:, i:i + 1], x, axis=mybir.AxisListType.X, op=OP.add)
            self.w(s, "ld"); self.w(s, "dve")
            s.activation(t["sqscr"][:], x, AF.Square,
                                  accum_out=stat[:, NCH + i:NCH + i + 1])
            rel[i] = self.cnt["act"]
        v_ = v
        self.w(v_, "act")
        v_.tensor_scalar(stat[:, 2 * NCH:3 * NCH], stat[:, 0:NCH], 1.0 / D, None, OP.mult)
        v_.tensor_scalar(stat[:, 3 * NCH:4 * NCH], stat[:, NCH:2 * NCH], 1.0 / D, None, OP.mult)
        v_.tensor_tensor(stat[:, 4 * NCH:5 * NCH], stat[:, 2 * NCH:3 * NCH],
                         stat[:, 2 * NCH:3 * NCH], OP.mult)
        v_.tensor_tensor(stat[:, 4 * NCH:5 * NCH], stat[:, 3 * NCH:4 * NCH],
                                  stat[:, 4 * NCH:5 * NCH], OP.subtract)
        self.w(s, "dve")
        s.activation(stat[:, 5 * NCH:6 * NCH], stat[:, 4 * NCH:5 * NCH], AF.Sqrt,
                              bias=self.t["cconst"][:, 0:1])
        self.w(v, "act")
        v.reciprocal(stat[:, 6 * NCH:7 * NCH], stat[:, 5 * NCH:6 * NCH])
        self.w(self.g, "ld"); self.w(self.g, "act"); self.w(self.g, "dve")
        relb = {}
        for i in range(NCH):
            par = i % 2
            if i >= 2:
                self.g.wait_ge(self.sems["dve"], relb[i - 2])
            self.dma(self.g, "ldw", t["x0t"][:, _ts(par, D)], dr["x0"][_ts(i, 128), :])
            self.w(v, "ldw")
            v.tensor_scalar(t["lnt"][:, _ts(par, D)], t["x0t"][:, _ts(par, D)],
                                     stat[:, 2 * NCH + i:2 * NCH + i + 1],
                                     stat[:, 6 * NCH + i:6 * NCH + i + 1],
                                     OP.subtract, OP.mult)
            relb[i] = self.cnt["dve"]
            self.w(te, "dve")
            for half in range(2):
                bank = self.ptrA if half == 0 else self.ptrB
                for j in range(4):
                    kk = half * 4 + j
                    ins = te.transpose(bank[:, _ts(j, 128)],
                                       t["lnt"][:, par * D + kk * 128: par * D + (kk + 1) * 128],
                                       t["identb"][:])
                self.inc(ins, "pe", 1)
                self.w(v, "pe")
                for j in range(4):
                    kk = half * 4 + j
                    ins = v.tensor_copy(
                        t["lnx_fm"][:, kk * NT_DN + i * 128: kk * NT_DN + (i + 1) * 128],
                        bank[:, _ts(j, 128)])
                ins
                self.w(te, "dve")

    def p1b_proj(self):
        t = self.t
        v, s, te, g = self.v, self.s, self.te, self.g
        pb = self.pb
        self.w(te, "dve")

        def loadw(src, cols):
            self.w(g, "pe")
            self.dma(g, "ldw", t["wbuf"][:, 0:8 * cols].rearrange("p (k c) -> p k c", c=cols),
                     self.dr[src].rearrange("(k p) c -> p k c", p=128))
            self.w(te, "ldw")

        def mm8(dst_bank, cols, i):
            for k in range(8):
                ins = te.matmul(dst_bank[:, 0:cols],
                                t["lnx_fm"][:, k * NT_DN + i * 128: k * NT_DN + (i + 1) * 128],
                                t["wbuf"][:, k * cols:(k + 1) * cols],
                                start=(k == 0), stop=(k == 7))
            self.inc(ins, "pe", 1)

        for wname, dst, rsq in [("Wq", "q_tm", "rsq_q"), ("Wk", "k_tm", "rsq_k")]:
            loadw(wname, 512)
            for i in range(NCH):
                mm8(pb[i % 2], 512, i)
                self.w(v, "pe")
                v.tensor_copy(t[dst][:, _ts(i, 512)], pb[i % 2][:, 0:512])
                self.w(s, "dve")
                s.activation(t["sqscr"][:, 0:512], t[dst][:, _ts(i, 512)],
                                      AF.Square)
                self.w(v, "act")
                v.tensor_reduce(t[rsq][:, _ts(i, 8)],
                                         t["sqscr"][:, 0:512].rearrange("p (h f) -> p h f", h=8),
                                         axis=mybir.AxisListType.X, op=OP.add)
                self.w(te, "dve")
        loadw("Wv", 512)
        for i in range(NCH):
            mm8(pb[i % 2], 512, i)
            self.w(v, "pe")
            v.tensor_copy(t["v_tm"][:, _ts(i, 512)], pb[i % 2][:, 0:512])
            if i >= 1:
                self.w(te, "dve")
        loadw("Wbt", 128)
        for i in range(NCH):
            mm8(pb[i % 2], 128, i)
            self.w(s, "pe")
            s.activation(t["beta"][:, _ts(i, 64)], pb[i % 2][:, 0:64], AF.Sigmoid)
            s.activation(t["sm64"][:], pb[i % 2][:, 64:128], AF.Sigmoid)
            self.w(v, "act")
            v.tensor_tensor(t["sm64b"][:], t["sm64"][:], t["rhow"][:], OP.mult)
            v.tensor_scalar(t["sm64b"][:], t["sm64b"][:], CLIP, None, OP.max)
            self.w(s, "dve")
            s.activation(t["g_tm"][:, _ts(i, 64)], t["sm64b"][:], AF.Ln)
            self.w(te, "act")
        loadw("Wg", 512)
        for i in range(NCH):
            mm8(pb[i % 2], 512, i)
            self.w(s, "pe")
            s.activation(t["gate"][:, _ts(i, 512)], pb[i % 2][:, 0:512], AF.Sigmoid)
            if i >= 1:
                self.w(te, "act")
        self.w(s, "dve")
        s.activation(t["rsq_q"][:], t["rsq_q"][:], AF.Sqrt,
                              bias=self.t["cconst"][:, 1:2])
        s.activation(t["rsq_k"][:], t["rsq_k"][:], AF.Sqrt,
                              bias=self.t["cconst"][:, 1:2])
        self.w(v, "act")
        v.reciprocal(t["rsq_q"][:], t["rsq_q"][:])
        v.reciprocal(t["rsq_k"][:], t["rsq_k"][:])

    def p1c_cumsum(self):
        t = self.t
        v, te = self.v, self.te
        pb5 = self.pb[5]
        for i in range(NCH):
            self.w(te, "act"); self.w(te, "dve")
            ins = te.matmul(pb5[:, 0:64], t["cumsum"][:], t["g_tm"][:, _ts(i, 64)],
                            start=True, stop=True)
            self.inc(ins, "pe", 1)
            self.w(v, "pe")
            v.tensor_copy(t["g_tm"][:, _ts(i, 64)], pb5[:, 0:64])
            self.w(te, "dve")

    def p2_deltanet(self):
        t = self.t
        v, te = self.v, self.te
        v.memset(t["S"][:], 0.0)
        v.memset(t["S_bf"][:], 0.0)
        for c in range(min(NCH, DN_CHUNKS)):
            v.tensor_copy(t["ghi"][:], t["g_tm"][:, _ts(c, 64)])
            v.tensor_tensor(t["glo"][:], t["g_tm"][:, _ts(c, 64)], t["ghi"][:],
                                     OP.subtract)
            for m in range(NM):
                self.w(te, "dve")
                srh = t["ghi"][:].rearrange("p (h m) -> p m h", m=8)[:, m, :]
                srl = t["glo"][:].rearrange("p (h m) -> p m h", m=8)[:, m, :]
                te.transpose(self.ptrA[0:8, 0:128], srh, t["identb"][:])
                ins = te.transpose(self.ptrA[0:8, 128:256], srl, t["identb"][:])
                self.inc(ins, "pe", 1)
                self.w(v, "pe")
                v.tensor_copy(t["gmc"][0:8, _ts(m, 128)], self.ptrA[0:8, 0:128])
                v.tensor_tensor(t["gmc"][0:8, _ts(m, 128)],
                                         t["gmc"][0:8, _ts(m, 128)],
                                         self.ptrA[0:8, 128:256], OP.add)
                self.w(te, "dve")
            for m in range(NM):
                self.dn_cm(c, m)
            self.w(v, "pe")
            v.tensor_tensor(t["og"][:, _ts(c, 512)], self.pb[5][:, 0:512],
                                     t["gate"][:, _ts(c, 512)], OP.mult)
            self.w(te, "dve")

    def dn_cm(self, c, m):
        t = self.t
        v, s, te = self.v, self.s, self.te
        pb = self.pb
        gchunk = t["gmc"][0:8, _ts(m, 128)]
        c64 = t["beta"][:, _ts(c, 64)].rearrange("p (h m) -> p m h", m=8)
        bslice = c64[:, m, :]
        gslice = t["g_tm"][:, _ts(c, 64)].rearrange("p (h m) -> p m h", m=8)[:, m, :]
        ws = t["wrow"][:].rearrange("p (h m) -> p m h", m=8)[:, m, :]
        f8t = t["f8"][:]

        def hsl(name, h):
            return t[name][:, h * 128:(h + 1) * 128]

        def fmh(name, h):
            return t[name][0:64, h * 128:(h + 1) * 128]

        ang = t["trg"][:]; t2 = t["trg2"][:]; t3 = t["trg3"][:]
        sv_ = t["sinv"][:]; cv_ = t["cosv"][:]
        v.tensor_scalar(ang, t["omf"][:, _ts(m, 256)], t["pos"][:, c:c + 1], None,
                                 OP.mult)
        v.tensor_scalar(t2, ang, INV_2PI, None, OP.mult)
        v.tensor_scalar(t2, t2, MAGIC, None, OP.add)
        v.tensor_scalar(t2, t2, MAGIC, None, OP.subtract)
        v.scalar_tensor_tensor(t3, t2, -RR_C1, ang, OP.mult, OP.add)
        v.scalar_tensor_tensor(t3, t2, -RR_C2, t3, OP.mult, OP.add)
        v.tensor_scalar(t3, t3, -PI, PI, OP.max, OP.min)
        self.w(s, "dve")
        s.activation(sv_, t3, AF.Sin)
        v.tensor_scalar(t2, t3, PI / 2, None, OP.is_gt)
        v.scalar_tensor_tensor(t2, t2, -TWO_PI, t3, OP.mult, OP.add)
        v.tensor_scalar(t2, t2, -PI / 2 - PI, PI - PI / 2, OP.max, OP.min)
        self.w(s, "dve")
        s.activation(cv_, t2, AF.Sin, bias=t["cconst"][:, 2:3])
        s.activation(f8t[:, 0:8], gslice, AF.Exp)

        if DN_STAGE < 2:
            return
        self.w(v, "act")
        rot = t["rot"][:]
        r13 = t["rot1"][:].rearrange("p (h f) -> p h f", h=8)
        r23 = t["rot2"][:].rearrange("p (h f) -> p h f", h=8)
        c3 = cv_.rearrange("p (h f) -> p h f", h=8)
        s3 = sv_.rearrange("p (h f) -> p h f", h=8)
        rot3 = rot.rearrange("p (h f) -> p h f", h=8)
        for srcn, dstn, isq in [("q_tm", "qe_tm", True), ("k_tm", "kr_tm", False)]:
            src3 = t[srcn][:, _ts(c, 512)].rearrange("p (h f) -> p h f", h=8)
            lo = src3[:, :, 0:32]; hi = src3[:, :, 32:64]
            v.tensor_tensor(r13, lo, c3, OP.mult)
            v.tensor_tensor(r23, hi, s3, OP.mult)
            v.tensor_tensor(rot3[:, :, 0:32], r13, r23, OP.add)
            v.tensor_tensor(r13, hi, c3, OP.mult)
            v.tensor_tensor(r23, lo, s3, OP.mult)
            v.tensor_tensor(rot3[:, :, 32:64], r13, r23, OP.subtract)
            bx = t["bexp"][:]
            if isq:
                rq = t["rsq_q"][:, _ts(c, 8)]
                v.tensor_tensor(f8t[:, 8:16], rq, ws, OP.mult)
                v.tensor_copy(bx.rearrange("p (h f) -> p h f", h=8),
                                       f8t[:, 8:16].unsqueeze(-1).broadcast_to([128, 8, 64]))
                v.tensor_tensor(t["qn_tm"][:], rot, bx, OP.mult)
                v.tensor_tensor(f8t[:, 8:16], f8t[:, 8:16], f8t[:, 0:8], OP.mult)
                fcol = f8t[:, 8:16]
            else:
                fcol = t["rsq_k"][:, _ts(c, 8)]
            v.tensor_copy(bx.rearrange("p (h f) -> p h f", h=8),
                                   fcol.unsqueeze(-1).broadcast_to([128, 8, 64]))
            v.tensor_tensor(t[dstn][:], rot, bx, OP.mult)

        if DN_STAGE < 3:
            return
        for srcn, dstn, bank in [("qe_tm", "qe_fm", self.ptrA), ("kr_tm", "kr_fm", self.ptrB),
                                 ("qn_tm", "qn_fm", self.ptrA)]:
            self.w(te, "dve")
            for half in range(2):
                for j in range(4):
                    hh = half * 4 + j
                    ins = te.transpose(bank[0:64, _ts(j, 128)],
                                       t[srcn][:, hh * 64:(hh + 1) * 64], t["identb"][:])
                self.inc(ins, "pe", 1)
                self.w(v, "pe")
                v.tensor_copy(t[dstn][0:64, half * 512:(half + 1) * 512],
                                       bank[0:64, 0:512])
                self.w(te, "dve")
        self.w(te, "dve")

        if DN_STAGE < 4:
            return
        fx = t["fexp"][:]
        v.tensor_copy(fx[0:8, :].rearrange("p (h f) -> p h f", h=8),
                               gchunk.unsqueeze(1).broadcast_to([8, 8, 128]))
        v.tensor_tensor(fx[0:8, :], fx[0:8, :], t["e8bc"][:], OP.mult)
        v.tensor_scalar(fx[0:8, :], fx[0:8, :], -1.0, None, OP.mult)
        self.w(te, "dve")
        for half in range(2):
            bank = pb[half]
            te.matmul(bank[:, 0:512], gchunk, t["e8bc"][:, _ts(half, 512)],
                      start=True, stop=False)
            ins = te.matmul(bank[:, 0:512], t["ones8"][:], fx[0:8, _ts(half, 512)],
                            start=False, stop=True)
        self.inc(ins, "pe", 1)
        self.w(v, "pe")
        ea = t["eargp"][:]; e2 = t["earg2"][:]
        fx2 = t["fexp"][:]
        v.tensor_scalar(fx2[:, 0:512], pb[0][:, 0:512], 0.0, None, OP.min)
        v.tensor_scalar(fx2[:, 512:1024], pb[1][:, 0:512], 0.0, None, OP.min)
        v.tensor_scalar(t["btril"][:].bitcast(F32)[:, 0:256] if False else t["rot"][:],
                        pb[0][:, 0:512], 0.0, None, OP.max)
        v.tensor_scalar(t["bexp"][:], pb[1][:, 0:512], 0.0, None, OP.max)
        self.w(s, "dve")
        s.activation(ea[:, 0:512], fx2[:, 0:512], AF.Exp)
        s.activation(ea[:, 512:1024], fx2[:, 512:1024], AF.Exp)
        s.activation(e2[:, 0:512], t["rot"][:], AF.Exp, scale=-1.0)
        s.activation(e2[:, 512:1024], t["bexp"][:], AF.Exp, scale=-1.0)
        self.w(te, "act")

        if DN_STAGE < 5:
            return
        for h in range(8):
            bank = pb[0] if h < 4 else pb[1]
            ins = te.matmul(bank[:, _ts(h % 4, 128)], fmh("kr_fm", h), fmh("kr_fm", h),
                            start=True, stop=True)
        self.inc(ins, "pe", 1)
        for h in range(8):
            bank = pb[2] if h < 4 else pb[3]
            ins = te.matmul(bank[:, _ts(h % 4, 128)], fmh("kr_fm", h), fmh("qn_fm", h),
                            start=True, stop=True)
        self.inc(ins, "pe", 1)

        if DN_STAGE < 6:
            return
        self.w(v, "act"); self.w(v, "pe")
        v.tensor_copy(fx.rearrange("p (h f) -> p h f", h=8),
                               bslice.unsqueeze(-1).broadcast_to([128, 8, 128]))
        v.tensor_tensor(t["btril"][:], fx, t["tril01"][:], OP.mult)
        v.tensor_tensor(ea, ea, t["btril"][:], OP.mult)
        v.tensor_tensor(e2, e2, t["diag01"][:], OP.mult)
        for half in range(2):
            ab = pb[0] if half == 0 else pb[1]
            qb = pb[2] if half == 0 else pb[3]
            v.tensor_tensor(t["x0m"][:, _ts(half, 512)], ea[:, _ts(half, 512)],
                            ab[:, 0:512], OP.mult)
            ins = v.tensor_tensor(t["ptm"][:, _ts(half, 512)], e2[:, _ts(half, 512)],
                                  qb[:, 0:512], OP.mult)
        ins

        if DN_STAGE < 7:
            return
        self.w(te, "dve")
        for half in range(2):
            bank = self.ptrA if half == 0 else self.ptrB
            for j in range(4):
                hh = half * 4 + j
                ins = te.transpose(bank[:, _ts(j, 128)], hsl("x0m", hh), t["identb"][:])
            self.inc(ins, "pe", 1)
            self.w(v, "pe")
            v.tensor_copy(t["y0m"][:, _ts(half, 512)], bank[:, 0:512])

        if DN_STAGE < 8:
            return
        self.w(te, "dve")
        for h in range(8):
            ins = te.matmul(pb[0][:, h * 64:(h + 1) * 64], fmh("kr_fm", h),
                            t["S_bf"][0:64, (h * 8 + m) * 64:(h * 8 + m + 1) * 64],
                            start=True, stop=True)
        self.inc(ins, "pe", 1)
        bx = t["bexp"][:]
        v.tensor_copy(bx.rearrange("p (h f) -> p h f", h=8),
                               f8t[:, 0:8].unsqueeze(-1).broadcast_to([128, 8, 64]))
        self.w(v, "pe")
        v.tensor_tensor(t["dfp"][:], bx, pb[0][:, 0:512], OP.mult)
        v.tensor_tensor(t["dfp"][:], t["v_tm"][:, _ts(c, 512)], t["dfp"][:],
                                 OP.subtract)
        v.tensor_copy(t["bexp2"][:].rearrange("p (h f) -> p h f", h=8),
                               bslice.unsqueeze(-1).broadcast_to([128, 8, 64]))
        v.tensor_tensor(t["dfp"][:], t["dfp"][:], t["bexp2"][:], OP.mult)
        v.tensor_copy(t["dsol"][:], t["dfp"][:])
        self.w(te, "dve")

        if DN_STAGE < 9:
            return
        def xy(j):
            if j == 0:
                return ("x0m", 0), ("y0m", 0)
            sl = (j - 1) % 2
            return ("xpow", sl * 1024), ("ypow", sl * 1024)

        for j in range(7):
            (xn, xo), (yn, yo) = xy(j)
            for h in range(8):
                ins = te.matmul(pb[0][:, h * 64:(h + 1) * 64],
                                t[yn][:, yo + h * 128: yo + (h + 1) * 128],
                                t["dsol"][:, h * 64:(h + 1) * 64],
                                start=True, stop=True)
            self.inc(ins, "pe", 1)
            if j < 6:
                for h in range(8):
                    bank = pb[1] if h < 4 else pb[2]
                    ins = te.matmul(bank[:, _ts(h % 4, 128)],
                                    t[yn][:, yo + h * 128: yo + (h + 1) * 128],
                                    t[xn][:, xo + h * 128: xo + (h + 1) * 128],
                                    start=True, stop=True)
                self.inc(ins, "pe", 1)
                for h in range(8):
                    bank = pb[3] if h < 4 else pb[4]
                    ins = te.matmul(bank[:, _ts(h % 4, 128)],
                                    t[xn][:, xo + h * 128: xo + (h + 1) * 128],
                                    t[yn][:, yo + h * 128: yo + (h + 1) * 128],
                                    start=True, stop=True)
                self.inc(ins, "pe", 1)
            self.w(v, "pe")
            op = OP.subtract if j == 0 else OP.add
            v.tensor_tensor(t["dfp"][:], t["dfp"][:], pb[0][:, 0:512], op)
            v.tensor_copy(t["dsol"][:], t["dfp"][:])
            if j < 6:
                sl = j % 2
                v.tensor_copy(t["xpow"][:, sl * 1024: sl * 1024 + 512], pb[1][:, 0:512])
                v.tensor_copy(t["xpow"][:, sl * 1024 + 512: sl * 1024 + 1024], pb[2][:, 0:512])
                v.tensor_copy(t["ypow"][:, sl * 1024: sl * 1024 + 512], pb[3][:, 0:512])
                v.tensor_copy(t["ypow"][:, sl * 1024 + 512: sl * 1024 + 1024],
                                       pb[4][:, 0:512])
            self.w(te, "dve")

        if DN_STAGE < 10:
            return
        first = (m == 0)
        for h in range(8):
            te.matmul(pb[5][:, h * 64:(h + 1) * 64], fmh("qe_fm", h),
                      t["S_bf"][0:64, (h * 8 + m) * 64:(h * 8 + m + 1) * 64],
                      start=(first and h == 0), stop=False, skip_group_check=True)
            ins = te.matmul(pb[5][:, h * 64:(h + 1) * 64], hsl("ptm", h),
                            t["dsol"][:, h * 64:(h + 1) * 64],
                            start=False, stop=(m == NM - 1 and h == 7), skip_group_check=True)
        self.inc(ins, "pe", 1)

        if DN_STAGE < 11:
            return
        gc = gchunk[:, 127:128]
        v.tensor_tensor(t["sm8b"][:, 0:8], t["e8diag"][:],
                                 gc.broadcast_to([8, 8]), OP.mult)
        self.w(te, "dve")
        ins = te.matmul(pb[4][:, 0:8], t["ones8"][:], t["sm8b"][:, 0:8], start=True, stop=True)
        self.inc(ins, "pe", 1)
        self.w(v, "pe")
        v.tensor_tensor(f8t[:, 16:24], pb[4][:, 0:8], gslice, OP.subtract)
        self.w(s, "dve"); self.w(s, "pe")
        s.activation(f8t[:, 24:32], f8t[:, 16:24], AF.Exp)
        s.activation(t["sups"][0:64, 1024:1032], pb[4][0:64, 0:8], AF.Exp)
        self.w(v, "act")
        v.tensor_copy(bx.rearrange("p (h f) -> p h f", h=8),
                               f8t[:, 24:32].unsqueeze(-1).broadcast_to([128, 8, 64]))
        v.tensor_tensor(t["d2bf"][:], t["dfp"][:], bx, OP.mult)
        self.w(te, "dve")
        for h in range(8):
            ins = te.matmul(pb[4][0:64, h * 64:(h + 1) * 64],
                            t["kr_tm"][:, h * 64:(h + 1) * 64],
                            t["d2bf"][:, h * 64:(h + 1) * 64],
                            start=True, stop=True)
        self.inc(ins, "pe", 1)
        S3 = t["S"][:].rearrange("k (h m v) -> k h m v", h=8, m=8)[:, :, m, :]
        S3b = t["S_bf"][0:64].rearrange("k (h m v) -> k h m v", h=8, m=8)[:, :, m, :]
        eg = t["sups"][0:64, 1024:1032]
        egx = t["sups"][0:64, 0:512]
        sc2 = t["sups"][0:64, 512:1024]
        v.tensor_copy(egx.rearrange("k (h f) -> k h f", h=8),
                               eg.unsqueeze(-1).broadcast_to([64, 8, 64]))
        v.tensor_tensor(sc2.rearrange("k (h f) -> k h f", h=8), S3,
                                 egx.rearrange("k (h f) -> k h f", h=8), OP.mult)
        self.w(v, "pe")
        v.tensor_tensor(S3, sc2.rearrange("k (h f) -> k h f", h=8),
                                 pb[4][0:64, 0:512].rearrange("k (h f) -> k h f", h=8), OP.add)
        v.tensor_copy(S3b, S3)
        self.w(te, "dve")

    def p3_attn_out(self):
        t, dr = self.t, self.dr
        v, s, te = self.v, self.s, self.te
        pb = self.pb
        self.dma(self.g, "ldw", t["wo"][:].rearrange("p (k c) -> p k c", c=1024),
                 dr["Wo"].rearrange("(k p) c -> p k c", p=128))
        self.w(te, "ldw")
        for a in range(NTL_AT):
            c = a + 1
            self.w(te, "dve")
            for j in range(4):
                ins = te.transpose(self.ptrA[:, _ts(j, 128)],
                                   t["og"][:, c * 512 + j * 128: c * 512 + (j + 1) * 128],
                                   t["identb"][:])
            self.inc(ins, "pe", 1)
            self.w(v, "pe")
            v.tensor_copy(t["aofm"][:], self.ptrA[:, 0:512])
            self.w(te, "dve")
            for n in range(2):
                for k in range(4):
                    ins = te.matmul(pb[n][:, 0:512], t["aofm"][:, _ts(k, 128)],
                                    t["wo"][:, k * 1024 + n * 512: k * 1024 + (n + 1) * 512],
                                    start=(k == 0), stop=(k == 3))
                self.inc(ins, "pe", 1)
            self.w(v, "pe"); self.w(s, "pe")
            st = t["stat"]
            v.tensor_reduce(st[:, 0:1], pb[0][:, 0:512], axis=mybir.AxisListType.X, op=OP.add)
            v.tensor_reduce(st[:, 1:2], pb[1][:, 0:512], axis=mybir.AxisListType.X, op=OP.add)
            v.tensor_tensor(st[:, 0:1], st[:, 0:1], st[:, 1:2], OP.add)
            v.tensor_scalar(st[:, 2:3], st[:, 0:1], 1.0 / D, None, OP.mult)
            self.w(s, "dve")
            s.activation(t["sqscr3"][:, 0:512], pb[0][:, 0:512], AF.Square,
                                  accum_out=st[:, 3:4])
            s.activation(t["sqscr3"][:, 512:1024], pb[1][:, 0:512], AF.Square,
                                  accum_out=st[:, 4:5])
            self.w(v, "act")
            v.tensor_tensor(st[:, 3:4], st[:, 3:4], st[:, 4:5], OP.add)
            v.tensor_scalar(st[:, 5:6], st[:, 3:4], 1.0 / D, None, OP.mult)
            v.tensor_tensor(st[:, 6:7], st[:, 2:3], st[:, 2:3], OP.mult)
            v.tensor_tensor(st[:, 6:7], st[:, 5:6], st[:, 6:7], OP.subtract)
            self.w(s, "dve")
            s.activation(st[:, 7:8], st[:, 6:7], AF.Sqrt, bias=t["cconst"][:, 0:1])
            self.w(v, "act")
            v.reciprocal(st[:, 8:9], st[:, 7:8])
            par = a % 2
            self.w(self.sy, "dve")
            self.dma(self.sy, "ld", t["x0t3"][:, _ts(par, D)],
                     dr["x0"][CHK + a * CHK: CHK + (a + 1) * CHK, :])
            self.w(v, "ld")
            for n in range(2):
                v.tensor_scalar(t["x1t"][:, par * D + n * 512: par * D + (n + 1) * 512],
                                pb[n][:, 0:512], st[:, 2:3], st[:, 8:9], OP.subtract, OP.mult)
            v.tensor_tensor(t["x1t"][:, _ts(par, D)], t["x1t"][:, _ts(par, D)],
                                     t["x0t3"][:, _ts(par, D)], OP.add)
            self.w(te, "dve")
            self.w(self.g, "dve")
            self.dma(self.g, "sto", dr["x1_spill"][_ts(a, 128), :], t["x1t"][:, _ts(par, D)])
            if a >= 1 and BUILD_PHASES < 4:
                self.dma(self.g, "sto", dr["out"][_ts(a - 1, 128), :], t["x1t"][:, _ts(par, D)])
            if a >= 1:
                self.w(v, "sto")

    def emit_ffn_block(self):
        with ExitStack() as scB:
            sbB = lambda n, sh, dt=F32: self.t.__setitem__(
                n, scB.enter_context(self.nc.sbuf_tensor(n, sh, dt)))
            sbB("h2_fm", [128, 8 * OWN], BF16)
            sbB("ce_all", [128, NTL_OWN * NE], F32)
            sbB("ones8b", [8, 128], BF16)
            v = self.v
            self.w(v, "ld")
            v.tensor_copy(self.t["ones8b"][0:8, :], self.t["ones8"][0:8, :])
            self.p4_window_attn()
            if BUILD_PHASES >= 5:
                self.p5_gate()
            if BUILD_PHASES >= 6:
                self.p6_moe()
            self.p_taps(7)
            self.phase_barrier()

    def p4_window_attn(self):
        t, dr = self.t, self.dr
        v, s, te, g, sy = self.v, self.s, self.te, self.g, self.sy
        pb, st = self.pb, self.t["stat"]
        with ExitStack() as sc4:
            sb4 = lambda n, sh, dt=F32: self.t.__setitem__(
                n, sc4.enter_context(self.nc.sbuf_tensor(n, sh, dt)))
            sb4("wqkv", [128, 8 * 1536], BF16)
            sb4("projw", [128, 8 * 1024], BF16)
            sb4("awf", [128, 8], F32)
            sb4("wffm", [128, 8], F32)
            sb4("qkvb8f", [8, 1536], F32); sb4("qkvb8", [8, 1536], BF16)
            sb4("projb8f", [8, 1024], F32); sb4("projb8", [8, 1024], BF16)
            sb4("snk8", [8, 16], F32); sb4("snk", [128, 16], F32)
            sb4("mb", [128, 256], F32); sb4("mf", [128, 256], F32)
            sb4("x1ld", [128, 2 * D], F32)
            sb4("ropc", [128, 2 * D], F32)
            sb4("rops", [128, 2 * D], F32)
            sb4("y1t", [128, D], F32)
            sb4("hnb", [128, D], BF16)
            sb4("hfm", [128, 8 * 128], BF16)
            sb4("qkvt", [128, 1536], F32)
            sb4("shuf", [128, D], F32)
            sb4("rr", [128, D], F32)
            sb4("qrb", [128, D], BF16)
            sb4("krb", [128, 256], BF16)
            sb4("qhf", [64, 16 * 128], BF16)
            sb4("khf", [64, 2 * 4 * 128], BF16)
            sb4("vrng", [128, 2 * 256], BF16)
            sb4("lg", [128, 256], F32)
            sb4("pexp", [128, 256], BF16)
            sb4("att", [128, 256], BF16)
            sb4("ytm", [128, D], BF16)
            sb4("yfm", [128, 8 * 128], BF16)
            sb4("x2t", [128, 2 * D], F32)
            sb4("x2nb", [128, D], BF16)
            sb4("sq4", [128, D], F32)
            self.dma(g, "ldw", t["wqkv"][:].rearrange("p (k c) -> p k c", c=1536),
                     dr["qkv_w"].rearrange("(k p) c -> p k c", p=128))
            self.dma(g, "ldw", t["projw"][:].rearrange("p (k c) -> p k c", c=1024),
                     dr["proj_w"].rearrange("(k p) c -> p k c", p=128))
            self.dma(g, "ldw", t["awf"][:],
                     dr["attn_norm_w"].rearrange("o (k p) -> (o p) k", p=128),
                     allow_slow_non_contiguous=True)
            self.dma(g, "ldw", t["wffm"][:],
                     dr["ffn_norm_w"].rearrange("o (k p) -> (o p) k", p=128),
                     allow_slow_non_contiguous=True)
            self.dma(g, "ldw", t["mb"][:], dr["maskband"])
            self.dma(g, "ldw", t["mf"][:], dr["maskfirst"])
            for p in range(8):
                self.dma(sy, "ld", t["qkvb8f"][p:p + 1, :], dr["qkv_b"])
                self.dma(sy, "ld", t["projb8f"][p:p + 1, :], dr["proj_b"])
                self.dma(sy, "ld", t["snk8"][p:p + 1, :], dr["attn_sink"])
            self.w(v, "ld")
            v.tensor_scalar(t["qkvb8"][0:8, :], t["qkvb8f"][0:8, :], 0.125, None, OP.mult)
            v.tensor_scalar(t["projb8"][0:8, :], t["projb8f"][0:8, :], 0.125, None, OP.mult)
            v.tensor_scalar(t["snk8"][0:8, :], t["snk8"][0:8, :], 0.125, None, OP.mult)
            self.w(te, "dve")
            ins = te.matmul(pb[5][:, 0:16], t["ones8"][0:8, :], t["snk8"][0:8, :],
                            start=True, stop=True)
            self.inc(ins, "pe", 1)
            self.w(v, "pe")
            v.tensor_copy(t["snk"][:], pb[5][:, 0:16])
            self.w(v, "ldw")
            for k in range(8):
                v.tensor_scalar(t["wqkv"][:, _ts(k, 1536)], t["wqkv"][:, _ts(k, 1536)],
                                t["awf"][:, k:k + 1], None, OP.mult)
            self.w(te, "dve")
            for a in range(NTL_AT):
                par = a % 2
                prev, cur = 1 - par, par
                self.w(sy, "dve")
                self.dma(sy, "ld", t["x1ld"][:, _ts(par, D)], dr["x1_spill"][_ts(a, 128), :])
                self.dma(sy, "ld", t["ropc"][:, _ts(par, D)], dr["ropecos"][_ts(a, 128), :])
                self.dma(sy, "ld", t["rops"][:, _ts(par, D)], dr["ropesin"][_ts(a, 128), :])
                self.w(v, "ld"); self.w(s, "ld")
                x1 = t["x1ld"][:, _ts(par, D)]
                v.tensor_reduce(st[:, 0:1], x1, axis=mybir.AxisListType.X, op=OP.add)
                v.tensor_scalar(st[:, 1:2], st[:, 0:1], 1.0 / D, None, OP.mult)
                self.w(s, "dve")
                s.activation(t["sq4"][:], x1, AF.Square, accum_out=st[:, 2:3])
                self.w(v, "act")
                v.tensor_scalar(st[:, 3:4], st[:, 2:3], 1.0 / D, None, OP.mult)
                v.tensor_tensor(st[:, 4:5], st[:, 1:2], st[:, 1:2], OP.mult)
                v.tensor_tensor(st[:, 4:5], st[:, 3:4], st[:, 4:5], OP.subtract)
                self.w(s, "dve")
                s.activation(st[:, 5:6], st[:, 4:5], AF.Sqrt, bias=t["cconst"][:, 0:1])
                self.w(v, "act")
                v.reciprocal(st[:, 6:7], st[:, 5:6])
                v.tensor_scalar(t["y1t"][:], x1, st[:, 1:2], st[:, 6:7],
                                OP.subtract, OP.mult)
                self.w(s, "dve")
                s.activation(t["sq4"][:], t["y1t"][:], AF.Square, accum_out=st[:, 7:8])
                self.w(v, "act")
                v.tensor_scalar(st[:, 8:9], st[:, 7:8], 1.0 / D, None, OP.mult)
                self.w(s, "dve")
                s.activation(st[:, 9:10], st[:, 8:9], AF.Sqrt, bias=t["cconst"][:, 0:1])
                self.w(v, "act")
                v.reciprocal(st[:, 10:11], st[:, 9:10])
                v.tensor_scalar(t["hnb"][:], t["y1t"][:], st[:, 10:11], None, OP.mult)
                self.w(te, "dve")
                for j in range(8):
                    ptr = self.ptrA if j < 4 else self.ptrB
                    ins = te.transpose(ptr[:, _ts(j % 4, 128)], t["hnb"][:, _ts(j, 128)],
                                       t["identb"][:])
                self.inc(ins, "pe", 1)
                self.w(v, "pe")
                v.tensor_copy(t["hfm"][:, 0:512], self.ptrA[:, 0:512])
                v.tensor_copy(t["hfm"][:, 512:1024], self.ptrB[:, 0:512])
                self.w(te, "dve"); self.w(te, "ldw")
                for n in range(3):
                    for k in range(8):
                        te.matmul(pb[n][:, 0:512], t["hfm"][:, _ts(k, 128)],
                                  t["wqkv"][:, k * 1536 + n * 512: k * 1536 + (n + 1) * 512],
                                  start=(k == 0), stop=False, skip_group_check=True)
                    ins = te.matmul(pb[n][:, 0:512], t["ones8b"][0:8, :],
                                    t["qkvb8"][0:8, _ts(n, 512)],
                                    start=False, stop=True, skip_group_check=True)
                    self.inc(ins, "pe", 1)
                self.w(v, "pe")
                for n in range(3):
                    v.tensor_copy(t["qkvt"][:, _ts(n, 512)], pb[n][:, 0:512])
                self.w(te, "dve")
                cosT = t["ropc"][:, _ts(par, D)]
                sinT = t["rops"][:, _ts(par, D)]
                kk = t["qkvt"][:, 1024:1280]
                shkA = t["shuf"][:, 0:256].rearrange("p (i two) -> p i two", two=2)
                kkA = kk.rearrange("p (i two) -> p i two", two=2)
                v.tensor_copy(shkA[:, :, 0:1], kkA[:, :, 1:2])
                v.tensor_copy(shkA[:, :, 1:2], kkA[:, :, 0:1])
                v.tensor_tensor(t["rr"][:, 0:256], kk, cosT[:, 0:256], OP.mult)
                v.tensor_tensor(t["shuf"][:, 0:256], t["shuf"][:, 0:256], sinT[:, 0:256],
                                OP.mult)
                v.tensor_tensor(t["rr"][:, 0:256], t["rr"][:, 0:256], t["shuf"][:, 0:256],
                                OP.add)
                v.tensor_copy(t["krb"][:], t["rr"][:, 0:256])
                v.tensor_copy(t["vrng"][:, _ts(par, 256)], t["qkvt"][:, 1280:1536])
                self.w(te, "dve")
                for gi in range(4):
                    ins = te.transpose(self.ptrB[0:64, _ts(gi, 128)],
                                       t["krb"][:, _ts(gi, 64)], t["identb"][:])
                self.inc(ins, "pe", 1)
                self.w(v, "pe")
                v.tensor_copy(t["khf"][0:64, _ts(par, 512)], self.ptrB[0:64, 0:512])
                self.w(te, "dve")
                if a == 0:
                    continue
                qq = t["qkvt"][:, 0:1024]
                shA = t["shuf"][:].rearrange("p (i two) -> p i two", two=2)
                qqA = qq.rearrange("p (i two) -> p i two", two=2)
                v.tensor_copy(shA[:, :, 0:1], qqA[:, :, 1:2])
                v.tensor_copy(shA[:, :, 1:2], qqA[:, :, 0:1])
                v.tensor_tensor(t["rr"][:], qq, cosT, OP.mult)
                v.tensor_tensor(t["shuf"][:], t["shuf"][:], sinT, OP.mult)
                v.tensor_tensor(t["rr"][:], t["rr"][:], t["shuf"][:], OP.add)
                v.tensor_copy(t["qrb"][:], t["rr"][:])
                for hh in range(4):
                    self.w(te, "dve")
                    for j in range(4):
                        h = hh * 4 + j
                        ins = te.transpose(self.ptrA[0:64, _ts(j, 128)],
                                           t["qrb"][:, _ts(h, 64)], t["identb"][:])
                    self.inc(ins, "pe", 1)
                    self.w(v, "pe")
                    v.tensor_copy(t["qhf"][0:64, _ts(hh, 512)], self.ptrA[0:64, 0:512])
                mask = t["mf"] if a == 1 else t["mb"]
                for h in range(FH):
                    gi = h // 4
                    self.w(te, "dve"); self.w(te, "act")
                    for kt, slot in enumerate((prev, cur)):
                        ins = te.matmul(
                            pb[3][:, kt * 128:(kt + 1) * 128],
                            t["qhf"][0:64, _ts(h, 128)],
                            t["khf"][0:64, slot * 512 + gi * 128: slot * 512 + (gi + 1) * 128],
                            start=True, stop=True, skip_group_check=True)
                    self.inc(ins, "pe", 1)
                    self.w(v, "pe")
                    v.scalar_tensor_tensor(t["lg"][:], pb[3][:, 0:256], SCALE_AT, mask[:],
                                           OP.mult, OP.add)
                    v.tensor_reduce(st[:, 11:12], t["lg"][:], axis=mybir.AxisListType.X,
                                    op=OP.max)
                    v.tensor_scalar(st[:, 11:12], st[:, 11:12], t["snk"][:, h:h + 1],
                                    None, OP.max)
                    v.tensor_scalar(st[:, 12:13], st[:, 11:12], -1.0, None, OP.mult)
                    self.w(s, "dve")
                    s.activation(t["pexp"][:], t["lg"][:], AF.Exp, bias=st[:, 12:13],
                                 accum_out=st[:, 13:14])
                    s.activation(st[:, 14:15], t["snk"][:, h:h + 1], AF.Exp,
                                 bias=st[:, 12:13])
                    self.w(v, "act")
                    v.tensor_tensor(st[:, 13:14], st[:, 13:14], st[:, 14:15], OP.add)
                    v.reciprocal(st[:, 15:16], st[:, 13:14])
                    self.w(te, "act"); self.w(te, "dve")
                    for kt in range(2):
                        ins = te.transpose(self.ptrB[:, kt * 128:(kt + 1) * 128],
                                           t["pexp"][:, kt * 128:(kt + 1) * 128],
                                           t["identb"][:])
                    self.inc(ins, "pe", 1)
                    self.w(v, "pe")
                    v.tensor_copy(t["att"][:], self.ptrB[:, 0:256])
                    self.w(te, "dve")
                    for kt, slot in enumerate((prev, cur)):
                        ins = te.matmul(
                            pb[4][:, _ts(h % 8, 64)],
                            t["att"][:, kt * 128:(kt + 1) * 128],
                            t["vrng"][:, slot * 256 + gi * 64: slot * 256 + (gi + 1) * 64],
                            start=(kt == 0), stop=(kt == 1), skip_group_check=True)
                    self.inc(ins, "pe", 1)
                    self.w(v, "pe")
                    v.tensor_scalar(t["ytm"][:, _ts(h, 64)], pb[4][:, _ts(h % 8, 64)],
                                    st[:, 15:16], None, OP.mult)
                self.w(te, "dve")
                for j in range(8):
                    ptr = self.ptrA if j < 4 else self.ptrB
                    ins = te.transpose(ptr[:, _ts(j % 4, 128)], t["ytm"][:, _ts(j, 128)],
                                       t["identb"][:])
                self.inc(ins, "pe", 1)
                self.w(v, "pe")
                v.tensor_copy(t["yfm"][:, 0:512], self.ptrA[:, 0:512])
                v.tensor_copy(t["yfm"][:, 512:1024], self.ptrB[:, 0:512])
                self.w(te, "dve")
                for n in range(2):
                    for k in range(8):
                        te.matmul(pb[n][:, 0:512], t["yfm"][:, _ts(k, 128)],
                                  t["projw"][:, k * 1024 + n * 512: k * 1024 + (n + 1) * 512],
                                  start=(k == 0), stop=False, skip_group_check=True)
                    ins = te.matmul(pb[n][:, 0:512], t["ones8b"][0:8, :],
                                    t["projb8"][0:8, _ts(n, 512)],
                                    start=False, stop=True, skip_group_check=True)
                    self.inc(ins, "pe", 1)
                self.w(v, "pe")
                if a >= 2:
                    self.w(v, "sto")
                x2 = t["x2t"][:, _ts(par, D)]
                for n in range(2):
                    v.tensor_tensor(x2[:, _ts(n, 512)], pb[n][:, 0:512],
                                    t["y1t"][:, _ts(n, 512)], OP.add)
                self.w(te, "dve")
                self.w(g, "dve")
                self.dma(g, "sto", dr["x2_spill"][_ts(a - 1, 128), :], x2)
                self.w(s, "dve")
                s.activation(t["sq4"][:], x2, AF.Square, accum_out=st[:, 16:17])
                self.w(v, "act")
                v.tensor_scalar(st[:, 17:18], st[:, 16:17], 1.0 / D, None, OP.mult)
                self.w(s, "dve")
                s.activation(st[:, 18:19], st[:, 17:18], AF.Sqrt, bias=t["cconst"][:, 0:1])
                self.w(v, "act")
                v.reciprocal(st[:, 19:20], st[:, 18:19])
                v.tensor_scalar(t["x2nb"][:], x2, st[:, 19:20], None, OP.mult)
                self.w(te, "dve")
                for j in range(8):
                    ptr = self.ptrA if j < 4 else self.ptrB
                    ins = te.transpose(ptr[:, _ts(j % 4, 128)], t["x2nb"][:, _ts(j, 128)],
                                       t["identb"][:])
                self.inc(ins, "pe", 1)
                self.w(v, "pe")
                for j in range(8):
                    ptr = self.ptrA if j < 4 else self.ptrB
                    v.tensor_scalar(t["h2_fm"][:, j * OWN + (a - 1) * 128: j * OWN + a * 128],
                                    ptr[:, _ts(j % 4, 128)], t["wffm"][:, j:j + 1],
                                    None, OP.mult)
                self.w(te, "dve")
            self.p_taps(4)
            self.phase_barrier()

    def p5_gate(self):
        t, dr = self.t, self.dr
        v, s, te, g, sy = self.v, self.s, self.te, self.g, self.sy
        pb, st = self.pb, self.t["stat"]
        with ExitStack() as sc5:
            sb5 = lambda n, sh, dt=F32: self.t.__setitem__(
                n, sc5.enter_context(self.nc.sbuf_tensor(n, sh, dt)))
            sb5("gatew", [128, 8 * 32], BF16)
            sb5("gateb8f", [8, 32], F32)
            sb5("gateb8", [8, 32], BF16)
            sb5("glogt", [128, 32], F32)
            sb5("ebuf", [128, 4 * 32], F32)
            sb5("mvals", [128, 4], F32)
            sb5("tmp5", [128, 32], F32)
            self.dma(g, "ldw", t["gatew"][:].rearrange("p (k c) -> p k c", c=32),
                     dr["gate_w"].rearrange("(k p) c -> p k c", p=128))
            for p in range(8):
                self.dma(sy, "ld", t["gateb8f"][p:p + 1, :], dr["gate_b"])
            self.w(v, "ld")
            v.tensor_scalar(t["gateb8"][0:8, :], t["gateb8f"][0:8, :], 0.125, None, OP.mult)
            self.w(te, "ldw"); self.w(te, "dve")
            for tt in range(NTL_OWN):
                for k in range(8):
                    te.matmul(pb[5][:, 0:32],
                              t["h2_fm"][:, k * OWN + tt * 128: k * OWN + (tt + 1) * 128],
                              t["gatew"][:, k * 32:(k + 1) * 32],
                              start=(k == 0), stop=False, skip_group_check=True)
                ins = te.matmul(pb[5][:, 0:32], t["ones8b"][0:8, :], t["gateb8"][0:8, :],
                                start=False, stop=True, skip_group_check=True)
                self.inc(ins, "pe", 1)
                self.w(v, "pe")
                v.tensor_copy(t["glogt"][:], pb[5][:, 0:32])
                self.w(te, "dve")
                for r in range(TOPK):
                    v.tensor_reduce(t["mvals"][:, r:r + 1], t["glogt"][:],
                                    axis=mybir.AxisListType.X, op=OP.max)
                    v.tensor_scalar(t["ebuf"][:, _ts(r, 32)], t["glogt"][:],
                                    t["mvals"][:, r:r + 1], None, OP.is_equal)
                    if r < TOPK - 1:
                        v.scalar_tensor_tensor(t["glogt"][:], t["ebuf"][:, _ts(r, 32)],
                                               -1e30, t["glogt"][:], OP.mult, OP.add)
                v.tensor_copy(st[:, 20:21], t["mvals"][:, 0:1])
                v.tensor_scalar(t["mvals"][:], t["mvals"][:], st[:, 20:21], None,
                                OP.subtract)
                self.w(s, "dve")
                s.activation(t["mvals"][:], t["mvals"][:], AF.Exp, accum_out=st[:, 21:22])
                self.w(v, "act")
                v.reciprocal(st[:, 22:23], st[:, 21:22])
                v.tensor_scalar(t["mvals"][:], t["mvals"][:], st[:, 22:23], None, OP.mult)
                ce = t["ce_all"][:, _ts(tt, 32)]
                v.tensor_scalar(ce, t["ebuf"][:, 0:32], t["mvals"][:, 0:1], None, OP.mult)
                for r in range(1, TOPK):
                    v.tensor_scalar(t["tmp5"][:], t["ebuf"][:, _ts(r, 32)],
                                    t["mvals"][:, r:r + 1], None, OP.mult)
                    v.tensor_tensor(ce, ce, t["tmp5"][:], OP.add)
            self.p_taps(5)
            self.phase_barrier()

    def p6_moe(self):
        t, dr = self.t, self.dr
        v, s, te, g, sy = self.v, self.s, self.te, self.g, self.sy
        pb, st = self.pb, self.t["stat"]
        with ExitStack() as sc6:
            sb6 = lambda n, sh, dt=F32: self.t.__setitem__(
                n, sc6.enter_context(self.nc.sbuf_tensor(n, sh, dt)))
            sb6("wup", [128, 4 * 8 * 256], BF16)
            sb6("wdn", [128, 2 * 8 * 1024], BF16)
            sb6("act6", [128, 8 * 1024], BF16)
            sb6("sg", [128, 512], F32)
            sb6("tmp6", [128, 512], F32)
            sb6("y2b", [128, 8 * 1024], F32)
            sb6("x2ld", [128, 2 * D], F32)
            sb6("x1l7", [128, 2 * D], F32)
            sb6("ffnt", [128, 2 * D], F32)
            sb6("sq6", [128, D], F32)

            slice_ready = {}

            def up_slice_dma(ti):
                b, e, m = triples[ti]
                mp = m % 4
                sem = "ldw" if ti % 2 == 0 else "ldw2"
                wu3 = dr["W_up"][e].rearrange("(k p) c -> p k c", p=128)
                self.w(g, "pe")
                self.dma(g, sem,
                         t["wup"][:].rearrange("p (ms k c) -> p ms k c", ms=4, c=256)
                         [:, mp, :, 0:128],
                         wu3[:, :, m * 128:(m + 1) * 128])
                self.dma(g, sem,
                         t["wup"][:].rearrange("p (ms k c) -> p ms k c", ms=4, c=256)
                         [:, mp, :, 128:256],
                         wu3[:, :, DFF + m * 128: DFF + (m + 1) * 128])
                slice_ready[ti] = (sem, self.cnt[sem])

            triples = [(b, e, m) for b in range(2) for e in range(NE) for m in range(8)]
            for ti0 in range(3):
                up_slice_dma(ti0)
            dve_mark = {}
            for b in range(2):
                self.w(v, "pe")
                v.memset(t["y2b"][:], 0.0)
                for e in range(NE):
                    em = e % 2
                    self.w(g, "pe")
                    self.dma(g, "ldd",
                             t["wdn"][:].rearrange("p (es k c) -> p es k c", es=2, c=1024)
                             [:, em, :, :],
                             dr["W_down"][e].rearrange("(k p) c -> p k c", p=128))
                    for m in range(8):
                        mp = m % 4
                        ti = b * NE * 8 + e * 8 + m
                        if ti + 3 < len(triples):
                            up_slice_dma(ti + 3)
                        rsem, rcnt = slice_ready[ti]
                        te.wait_ge(self.sems[rsem], rcnt)
                        for nb2 in range(2):
                            bu, bg = pb[2 * nb2], pb[2 * nb2 + 1]
                            key_u, key_g = ("up", nb2, 0), ("up", nb2, 1)
                            te.wait_ge(self.sems["dve"], dve_mark.get(key_u, 0))
                            te.wait_ge(self.sems["dve"], dve_mark.get(key_g, 0))
                            te.wait_ge(self.sems["act"], dve_mark.get(("upa", nb2), 0))
                            tok = b * 1024 + nb2 * 512
                            for k in range(8):
                                te.matmul(bu[:, 0:512],
                                          t["wup"][:, (mp * 8 + k) * 256:
                                                   (mp * 8 + k) * 256 + 128],
                                          t["h2_fm"][:, k * OWN + tok: k * OWN + tok + 512],
                                          start=(k == 0), stop=(k == 7),
                                          skip_group_check=True)
                            for k in range(8):
                                ins = te.matmul(bg[:, 0:512],
                                                t["wup"][:, (mp * 8 + k) * 256 + 128:
                                                         (mp * 8 + k) * 256 + 256],
                                                t["h2_fm"][:, k * OWN + tok:
                                                           k * OWN + tok + 512],
                                                start=(k == 0), stop=(k == 7),
                                                skip_group_check=True)
                            self.inc(ins, "pe", 2)
                            self.w(s, "pe")
                            s.activation(t["sg"][:], bg[:, 0:512], AF.Silu)
                            dve_mark[("upa", nb2)] = self.cnt["act"]
                            self.w(v, "act"); self.w(v, "pe")
                            v.tensor_tensor(
                                t["act6"][:, m * 1024 + nb2 * 512: m * 1024 + nb2 * 512 + 512],
                                t["sg"][:], bu[:, 0:512], OP.mult)
                            dve_mark[key_u] = self.cnt["dve"]
                            dve_mark[key_g] = self.cnt["dve"]
                    self.w(te, "ldd")
                    for half in range(2):
                        for tt8 in range(8):
                            bank = pb[4 + tt8 % 2]
                            key = ("dn", tt8 % 2)
                            te.wait_ge(self.sems["dve"], dve_mark.get(key, 0))
                            for k in range(8):
                                ins = te.matmul(
                                    bank[:, 0:512],
                                    t["act6"][:, k * 1024 + tt8 * 128:
                                              k * 1024 + (tt8 + 1) * 128],
                                    t["wdn"][:, (em * 8 + k) * 1024 + half * 512:
                                             (em * 8 + k) * 1024 + half * 512 + 512],
                                    start=(k == 0), stop=(k == 7), skip_group_check=True)
                            self.inc(ins, "pe", 1)
                            self.w(v, "pe")
                            cecol = t["ce_all"][:, (b * 8 + tt8) * 32 + e:
                                               (b * 8 + tt8) * 32 + e + 1]
                            v.tensor_scalar(t["tmp6"][:], bank[:, 0:512], cecol, None,
                                            OP.mult)
                            v.tensor_tensor(
                                t["y2b"][:, tt8 * 1024 + half * 512:
                                         tt8 * 1024 + half * 512 + 512],
                                t["y2b"][:, tt8 * 1024 + half * 512:
                                         tt8 * 1024 + half * 512 + 512],
                                t["tmp6"][:], OP.add)
                            dve_mark[key] = self.cnt["dve"]
                for tt8 in range(8):
                    par7 = tt8 % 2
                    self.w(sy, "dve")
                    self.dma(sy, "ld", t["x2ld"][:, _ts(par7, D)],
                             dr["x2_spill"][_ts(b * 8 + tt8, 128), :])
                    self.dma(sy, "ld", t["x1l7"][:, _ts(par7, D)],
                             dr["x1_spill"][_ts(1 + b * 8 + tt8, 128), :])
                    self.w(v, "ld")
                    if b * 8 + tt8 >= 2:
                        self.w(v, "sto")
                    ff = t["ffnt"][:, _ts(par7, D)]
                    v.tensor_tensor(ff, t["y2b"][:, _ts(tt8, D)],
                                    t["x2ld"][:, _ts(par7, D)], OP.add)
                    v.tensor_reduce(st[:, 24:25], ff, axis=mybir.AxisListType.X, op=OP.add)
                    v.tensor_scalar(st[:, 25:26], st[:, 24:25], 1.0 / D, None, OP.mult)
                    self.w(s, "dve")
                    s.activation(t["sq6"][:], ff, AF.Square, accum_out=st[:, 26:27])
                    self.w(v, "act")
                    v.tensor_scalar(st[:, 27:28], st[:, 26:27], 1.0 / D, None, OP.mult)
                    v.tensor_tensor(st[:, 28:29], st[:, 25:26], st[:, 25:26], OP.mult)
                    v.tensor_tensor(st[:, 28:29], st[:, 27:28], st[:, 28:29], OP.subtract)
                    self.w(s, "dve")
                    s.activation(st[:, 29:30], st[:, 28:29], AF.Sqrt,
                                 bias=t["cconst"][:, 0:1])
                    self.w(v, "act")
                    v.reciprocal(st[:, 30:31], st[:, 29:30])
                    v.tensor_scalar(ff, ff, st[:, 25:26], st[:, 30:31],
                                    OP.subtract, OP.mult)
                    v.tensor_tensor(ff, ff, t["x1l7"][:, _ts(par7, D)], OP.add)
                    self.w(g, "dve")
                    self.dma(g, "sto", dr["out"][_ts(b * 8 + tt8, 128), :], ff)
            self.p_taps(6)

    def p_taps(self, phase):
        t, dr = self.t, self.dr
        g = self.g
        for name, tile, lo, hi, ph in DEBUG_TAPS:
            if ph != phase:
                continue
            self.w(g, "dve"); self.w(g, "act"); self.w(g, "pe")
            ap = t[tile][:, lo:hi]
            self.dma(g, "sto", dr["tap_" + name][0:ap.shape[0], :], ap)

    def p_finish(self):
        g = self.g
        self.w(g, "dve"); self.w(g, "act"); self.w(g, "pe")
        g.wait_ge(self.sems["sto"], self.cnt["sto"])



_CACHED = {}


def _build_cached():
    key = (BUILD_PHASES, tuple(n for n, *_ in DEBUG_TAPS))
    if key not in _CACHED:
        kb = KB()
        _CACHED[key] = kb.build()
    return _CACHED[key]


def _constants():
    c = {}
    c["identsb"] = np.eye(128, dtype=BF)
    c["identsf"] = np.eye(128, dtype=np.float32)
    c["ones8c"] = np.ones((8, 128), np.float32)
    c["ones1r"] = np.ones((1, 128), np.float32)
    cc = np.zeros((128, 4), np.float32)
    cc[:, 0] = LN_EPS; cc[:, 1] = L2_EPS; cc[:, 2] = PI / 2; cc[:, 3] = 1.0
    c["cconstc"] = cc
    tt2 = np.arange(128)
    c["cumsumc"] = (tt2[:, None] <= tt2[None, :]).astype(np.float32)
    e8 = np.zeros((8, 8 * CHK), np.float32)
    for hp in range(8):
        e8[hp, hp * CHK:(hp + 1) * CHK] = 1.0
    c["e8bcc"] = e8
    c["e8negc"] = -e8
    c["e8diagc"] = np.eye(8, dtype=np.float32)
    i_idx = np.arange(CHK)[:, None]
    s_idx = np.arange(CHK)[None, :]
    tril = (s_idx < i_idx).astype(np.float32)
    diag = (i_idx <= s_idx).astype(np.float32)
    c["tril01c"] = np.tile(tril, (1, 8)).astype(BF)
    c["diag01c"] = np.tile(diag, (1, 8)).astype(BF)
    invf = (1.0 / 10000.0 ** (np.arange(0, DK, 2, dtype=np.float32) / DK)).astype(np.float32)
    iv8 = np.zeros((8, 8 * 32), np.float32)
    for hp in range(8):
        iv8[hp, hp * 32:(hp + 1) * 32] = invf
    c["invf8c"] = iv8
    r = np.arange(CHK)[:, None]
    j = np.arange(2 * CHK)[None, :]
    band = np.where((j >= r) & (j <= r + CHK), 0.0, -1e30).astype(np.float32)
    first = np.where((j >= CHK) & (j <= r + CHK) & (j >= r), 0.0, -1e30).astype(np.float32)
    c["maskband"] = band
    c["maskfirst_normal"] = band
    c["maskfirst_first"] = first
    pos = np.arange(T, dtype=np.float64) / 32.0
    inv2 = 1.0 / (150000.0 ** (np.arange(0, FDH, 2, dtype=np.float64) / FDH))
    fr = pos[:, None] * inv2[None, :]
    cos = np.cos(fr); sin = np.sin(fr)
    cos2 = np.repeat(cos, 2, axis=1)
    sin2 = np.repeat(sin, 2, axis=1)
    sin2[:, 0::2] *= -1.0
    c["ropecos_full"] = np.tile(cos2, (1, FH)).astype(np.float32)
    c["ropesin_full"] = np.tile(sin2, (1, FH)).astype(np.float32)
    return c


def _stage_in_maps(inputs):
    cst = _constants()
    xfull = np.ascontiguousarray(inputs["L0_attn_hidden_states"])
    wbt = np.concatenate([np.asarray(inputs["L0_attn_Wbeta"]),
                          np.asarray(inputs["L0_attn_Wtg"])], axis=1)
    shared = {
        "Wq": inputs["L0_attn_Wq"], "Wk": inputs["L0_attn_Wk"], "Wv": inputs["L0_attn_Wv"],
        "Wbt": wbt, "Wg": inputs["L0_attn_Wg"], "Wo": inputs["L0_attn_Wo"],
        "log_decay": inputs["L0_attn_log_decay"],
        "mode_logits": inputs["L0_attn_mode_logits"],
        "omega_ls": inputs["L0_attn_omega_log_scale"],
        "attn_norm_w": np.asarray(inputs["L0_ffn_attn_norm_w"])[None, :],
        "ffn_norm_w": np.asarray(inputs["L0_ffn_ffn_norm_w"])[None, :],
        "qkv_w": inputs["L0_ffn_qkv_w"], "qkv_b": np.asarray(inputs["L0_ffn_qkv_b"])[None, :],
        "proj_w": inputs["L0_ffn_proj_w"], "proj_b": np.asarray(inputs["L0_ffn_proj_b"])[None, :],
        "attn_sink": np.asarray(inputs["L0_ffn_attn_sink"])[None, :],
        "gate_w": inputs["L0_ffn_gate_w"], "gate_b": np.asarray(inputs["L0_ffn_gate_b"])[None, :],
        "W_up": inputs["L0_ffn_W_up"], "W_down": inputs["L0_ffn_W_down"],
        "identsb": cst["identsb"], "identsf": cst["identsf"],
        "ones8c": cst["ones8c"], "ones1r": cst["ones1r"],
        "e8bcc": cst["e8bcc"], "e8negc": cst["e8negc"], "e8diagc": cst["e8diagc"],
        "tril01c": cst["tril01c"], "diag01c": cst["diag01c"], "invf8c": cst["invf8c"],
        "cconstc": cst["cconstc"], "cumsumc": cst["cumsumc"],
        "maskband": cst["maskband"],
    }
    shared = {k: np.ascontiguousarray(np.asarray(val)) for k, val in shared.items()}
    in_maps = []
    for core in range(NCORE):
        seq, p = core // 4, core % 4
        base = p * OWN
        x0 = np.zeros((NT_DN, D), np.float32)
        lo = base - 256
        if lo < 0:
            x0[256:] = xfull[seq, base:base + OWN]
        else:
            x0[:] = xfull[seq, lo:base + OWN]
        posmat = (np.arange(128)[:, None] + np.arange(NCH)[None, :] * 128 + (base - 256)
                  ).astype(np.float32)
        at_lo = base - 128
        if at_lo < 0:
            ropec = np.zeros((NT_AT, D), np.float32)
            ropes = np.zeros((NT_AT, D), np.float32)
            ropec[128:] = cst["ropecos_full"][base:base + OWN]
            ropes[128:] = cst["ropesin_full"][base:base + OWN]
        else:
            ropec = cst["ropecos_full"][at_lo:base + OWN]
            ropes = cst["ropesin_full"][at_lo:base + OWN]
        m = dict(shared)
        m["x0"] = x0
        m["pos_tc"] = posmat
        m["maskfirst"] = cst["maskfirst_first"] if p == 0 else cst["maskfirst_normal"]
        m["ropecos"] = np.ascontiguousarray(ropec)
        m["ropesin"] = np.ascontiguousarray(ropes)
        in_maps.append(m)
    return in_maps


def kernel(**inputs):
    nc = _build_cached()
    in_maps = _stage_in_maps(inputs)
    res = run_bass_kernel_spmd(nc, in_maps, core_ids=list(range(NCORE)))
    outs = [res.results[i]["out"] for i in range(NCORE)]
    full = np.concatenate(outs, axis=0).reshape(2, T, D).astype(np.float32)
    if DEBUG_TAPS:
        kernel.taps = [{k: v for k, v in r.items() if k.startswith("tap_")}
                       for r in res.results]
    return full


# version 03 (final)
# speedup vs baseline: 1.5027x; 1.0108x over previous
import numpy as np
import ml_dtypes
import concourse.bass as bass
import concourse.mybir as mybir
from concourse.bass_utils import run_bass_kernel_spmd
from concourse.alu_op_type import AluOpType as OP
from contextlib import ExitStack

F32 = mybir.dt.float32
BF16 = mybir.dt.bfloat16
AF = mybir.ActivationFunctionType
BF = ml_dtypes.bfloat16

D = 1024
T = 8192
NCORE = 8
OWN = 2048
CHK = 128
NT_DN = OWN + 256
NT_AT = OWN + 128
NCH = NT_DN // CHK
NTL_AT = NT_AT // CHK
NTL_OWN = OWN // CHK
H, DK, DV, NM = 8, 64, 64, 8
FH, FKV, FDH = 16, 4, 64
NE, DFF = 32, 1024
TOPK = 4
LN_EPS = 1e-5
L2_EPS = 1e-6
CLIP = 1e-6
SCALE_DN = float(DK) ** -0.5
SCALE_AT = float(FDH) ** -0.5
MAGIC = 12582912.0
TWO_PI = float(2 * np.pi)
INV_2PI = float(1.0 / TWO_PI)
RR_C1 = 6.28125
RR_C2 = float(np.float32(TWO_PI) - np.float32(6.28125))
PI = float(np.pi)

DEBUG_TAPS = []
DN_STAGE = 99
DN_CHUNKS = 99
BUILD_PHASES = 99


def _ts(i, n):
    return slice(i * n, (i + 1) * n)


_CHAIN_OPS = {
    "tensor_copy", "tensor_tensor", "tensor_scalar", "tensor_single_scalar",
    "scalar_tensor_tensor", "tensor_reduce", "reduce_max", "reduce_sum", "memset",
    "memzero", "reciprocal", "tensor_tensor_scan", "max_index", "max_with_indices",
    "select", "activation", "copy", "add", "mul", "sqrt", "square", "sign",
    "tensor_add", "tensor_mul", "tensor_sub", "tensor_max", "tensor_relu",
    "tensor_scalar_add", "tensor_scalar_mul", "tensor_scalar_sub", "tensor_scalar_max",
    "tensor_scalar_min",
}


class _Chain:

    def __init__(self, kb, eng, sem):
        self._kb = kb
        self._eng = eng
        self._sem = sem

    def __getattr__(self, name):
        real = getattr(self._eng, name)
        if name not in _CHAIN_OPS:
            return real
        kb, eng, sem = self._kb, self._eng, self._sem

        def call(*a, **k):
            eng.wait_ge(kb.sems[sem], kb.cnt[sem])
            ins = real(*a, **k)
            kb.cnt[sem] += 1
            ins.then_inc(kb.sems[sem], 1)
            return ins
        return call


class KB:
    def __init__(self):
        self.nc = bass.Bass("TRN2", target_bir_lowering=False, debug=False,
                            detect_race_conditions=False)
        self.t = {}
        self.dr = {}
        self.cnt = {}
        self.sems = {}

    def sem(self, name):
        self.sems[name] = self.stack.enter_context(self.nc.semaphore(name))
        self.cnt[name] = 0

    def inc(self, instr, name, n=1):
        self.cnt[name] += n
        instr.then_inc(self.sems[name], n)

    def w(self, eng, name):
        eng.wait_ge(self.sems[name], self.cnt[name])

    def dma(self, eng, sem, out, in_, **kw):
        eng.wait_ge(self.sems[sem], self.cnt[sem])
        self.inc(eng.dma_start(out=out, in_=in_, **kw), sem, 16)

    def mk_chains(self):
        self._vchain = _Chain(self, self.nc.vector, "dve")
        self._schain = _Chain(self, self.nc.scalar, "act")

    def din(self, name, shape, dt=F32):
        self.dr[name] = self.nc.dram_tensor(name, shape, dt, kind="ExternalInput").ap()

    def sb(self, name, sh, dt=F32):
        self.t[name] = self.stack.enter_context(self.nc.sbuf_tensor(name, sh, dt))

    @property
    def v(self): return self._vchain
    @property
    def s(self): return self._schain
    @property
    def te(self): return self.nc.tensor
    @property
    def g(self): return self.nc.gpsimd
    @property
    def sy(self): return self.nc.sync

    def build(self):
        nc = self.nc
        for name, shape, dt in [
            ("x0", [NT_DN, D], F32),
            ("Wq", [D, 512], F32), ("Wk", [D, 512], F32), ("Wv", [D, 512], F32),
            ("Wbt", [D, 128], F32), ("Wg", [D, 512], F32), ("Wo", [512, D], F32),
            ("log_decay", [8, 8], F32), ("mode_logits", [8, 8], F32), ("omega_ls", [8, 8], F32),
            ("attn_norm_w", [1, D], F32), ("ffn_norm_w", [1, D], F32),
            ("qkv_w", [D, 1536], F32), ("qkv_b", [1, 1536], F32),
            ("proj_w", [D, D], F32), ("proj_b", [1, D], F32),
            ("attn_sink", [1, FH], F32),
            ("gate_w", [D, NE], F32), ("gate_b", [1, NE], F32),
            ("W_up", [NE, D, 2 * DFF], F32), ("W_down", [NE, DFF, D], F32),
            ("identsb", [128, 128], BF16), ("identsf", [128, 128], F32),
            ("ones8c", [8, 128], F32), ("ones1r", [1, 128], F32),
            ("e8bcc", [8, 8 * CHK], F32), ("e8negc", [8, 8 * CHK], F32),
            ("e8diagc", [8, 8], F32),
            ("tril01c", [128, 8 * CHK], BF16), ("diag01c", [128, 8 * CHK], BF16),
            ("pos_tc", [128, NCH], F32), ("invf8c", [8, 8 * 32], F32),
            ("cconstc", [128, 4], F32),
            ("cumsumc", [128, 128], F32),
            ("maskband", [128, 2 * CHK], F32), ("maskfirst", [128, 2 * CHK], F32),
            ("ropecos", [NT_AT, D], F32), ("ropesin", [NT_AT, D], F32),
        ]:
            self.din(name, shape, dt)
        self.dr["out"] = nc.dram_tensor("out", [OWN, D], F32, kind="ExternalOutput").ap()
        self.dr["x1_spill"] = nc.dram_tensor("x1_spill", [NT_AT, D], F32, kind="Internal").ap()
        self.dr["x2_spill"] = nc.dram_tensor("x2_spill", [OWN, D], F32, kind="Internal").ap()
        self.dr["flat_scr"] = nc.dram_tensor("flat_scr", [16, 64], F32, kind="Internal").ap()
        for name, tile, lo, hi, ph in DEBUG_TAPS:
            self.dr["tap_" + name] = nc.dram_tensor(
                "tap_" + name, [128, hi - lo], F32, kind="ExternalOutput").ap()

        with ExitStack() as stack:
            self.stack = stack
            for sname in ["ld", "ldw", "ldw2", "ldw3", "ldw4", "ldd",
                          "pe", "dve", "act", "sto"]:
                self.sem(sname)
            self.mk_chains()
            sb = self.sb
            ps = lambda name, sh, dt=F32: self.t.__setitem__(
                name, stack.enter_context(nc.psum_tensor(name, sh, dt)))
            sb("identb", [128, 128], BF16); sb("identf", [128, 128], F32)
            sb("ones8", [8, 128], F32); sb("ones1", [1, 128], F32)
            sb("stat", [128, 8 * NCH], F32)
            sb("cconst", [128, 4], F32)
            self.pb = []
            for i in range(6):
                ps(f"pb{i}", [128, 512], F32)
                self.pb.append(self.t[f"pb{i}"])
            ps("ptrA", [128, 512], BF16)
            ps("ptrB", [128, 512], BF16)
            self.ptrA = self.t["ptrA"]; self.ptrB = self.t["ptrB"]

            blk = stack.enter_context(nc.Block())

            @blk.vector
            def _(_v):
                self.emit()
        return nc

    def emit(self):
        sb = self.sb
        self.p_const_smalls()
        with ExitStack() as scA:
            sbA = lambda n, sh, dt=F32: self.t.__setitem__(
                n, scA.enter_context(self.nc.sbuf_tensor(n, sh, dt)))
            sbA("q_tm", [128, NCH * 512], BF16)
            sbA("k_tm", [128, NCH * 512], BF16)
            sbA("v_tm", [128, NCH * 512], BF16)
            sbA("beta", [128, NCH * 64], F32)
            sbA("g_tm", [128, NCH * 64], F32)
            sbA("gate", [128, NCH * 512], BF16)
            sbA("og", [128, NCH * 512], BF16)
            sbA("rsq_q", [128, NCH * 8], F32)
            sbA("rsq_k", [128, NCH * 8], F32)
            sbA("wrow", [128, 64], F32)
            sbA("rhow", [128, 64], F32)
            sbA("pos", [128, NCH], F32)
            sbA("omf", [128, NM * 256], F32)
            sbA("e8diag", [8, 8], F32)
            sbA("e8bc", [8, 8 * CHK], F32)
            sbA("invf8", [8, 8 * 32], F32)
            sbA("cumsum", [128, 128], F32)
            sbA("sm64", [128, 64], F32); sbA("sm64b", [128, 64], F32)
            sbA("sm8", [8, 256], F32); sbA("sm8b", [8, 256], F32)
            self.p_const_dn()
            self.emit_attn_block()
        if BUILD_PHASES >= 4:
            self.emit_ffn_block()
        self.p_finish()

    def emit_attn_block(self):
        sb = self.sb
        with ExitStack() as sc1:
            self._scope = sc1
            sb1 = lambda n, sh, dt=F32: self.t.__setitem__(
                n, sc1.enter_context(self.nc.sbuf_tensor(n, sh, dt)))
            sb1("lnx_fm", [128, 8 * NT_DN], BF16)
            sb1("wbuf", [128, 8 * 512], BF16)
            sb1("x0t", [128, 2 * D], F32)
            sb1("lnt", [128, 2 * D], BF16)
            sb1("sqscr", [128, D], F32)
            self.p1a_small()
            self.p0_lnx()
            if BUILD_PHASES >= 1:
                self.p1b_proj()
            if BUILD_PHASES >= 1.5:
                self.p1c_cumsum()
            self.p_taps(1)
            self.phase_barrier()
        if BUILD_PHASES >= 2:
            with ExitStack() as sc2:
                sb2 = lambda n, sh, dt=F32: self.t.__setitem__(
                    n, sc2.enter_context(self.nc.sbuf_tensor(n, sh, dt)))
                sb2("S", [64, 64 * DV], F32)
                sb2("S_bf", [64, 64 * DV], BF16)
                sb2("gmc", [8, 8 * 128], F32)
                sb2("ghi", [128, 64], BF16)
                sb2("glo", [128, 64], BF16)
                sb2("tril01", [128, 8 * CHK], BF16)
                sb2("diag01", [128, 8 * CHK], BF16)
                sb2("trg", [128, 256], F32); sb2("trg2", [128, 256], F32)
                sb2("trg3", [128, 256], F32)
                sb2("sinv", [128, 256], F32); sb2("cosv", [128, 256], F32)
                sb2("rot1", [128, 256], F32); sb2("rot2", [128, 256], F32)
                sb2("rot", [128, 512], F32)
                sb2("qe_tm", [128, 512], BF16); sb2("kr_tm", [128, 512], BF16)
                sb2("qn_tm", [128, 512], BF16); sb2("qn_fm", [64, 1024], BF16)
                sb2("qe_fm", [64, 1024], BF16); sb2("kr_fm", [64, 1024], BF16)
                sb2("fexp", [128, 1024], F32)
                sb2("btril", [128, 1024], BF16)
                sb2("eargp", [128, 1024], BF16)
                sb2("earg2", [128, 1024], BF16)
                sb2("x0m", [128, 1024], BF16)
                sb2("y0m", [128, 1024], BF16)
                sb2("ptm", [128, 1024], BF16)
                sb2("xpow", [128, 2 * 1024], BF16)
                sb2("ypow", [128, 2 * 1024], BF16)
                sb2("dsol", [128, 512], BF16)
                sb2("dfp", [128, 512], F32)
                sb2("bexp", [128, 512], F32)
                sb2("bexp2", [128, 512], F32)
                sb2("f8", [128, 32], F32)
                sb2("sups", [64, 1088], F32)
                sb2("d2bf", [128, 512], BF16)
                self.load_dn_consts()
                self.p2_deltanet()
                self.p_taps(2)
                self.phase_barrier()
        if BUILD_PHASES >= 2.7:
            with ExitStack() as sc3:
                sb3 = lambda n, sh, dt=F32: self.t.__setitem__(
                    n, sc3.enter_context(self.nc.sbuf_tensor(n, sh, dt)))
                sb3("wo", [128, 4 * 1024], BF16)
                sb3("x0t3", [128, 2 * D], F32)
                sb3("x1t", [128, 2 * D], F32)
                sb3("aofm", [128, 512], BF16)
                sb3("sqscr3", [128, D], F32)
                if BUILD_PHASES >= 3:
                    self.p3_attn_out()
                self.p_taps(3)
                self.phase_barrier()

    def phase_barrier(self):
        for eng in [self.v, self.s, self.te, self.g, self.sy]:
            for sname in ["ld", "ldw", "ldw2", "ldw3", "ldw4", "ldd",
                          "pe", "dve", "act", "sto"]:
                eng.wait_ge(self.sems[sname], self.cnt[sname])

    def p_const_smalls(self):
        t, dr = self.t, self.dr
        for name, dst in [("identsb", "identb"), ("identsf", "identf"), ("ones8c", "ones8"),
                          ("ones1r", "ones1"), ("cconstc", "cconst")]:
            self.dma(self.sy, "ld", t[dst][:], dr[name])

    def p_const_dn(self):
        t, dr = self.t, self.dr
        for name, dst in [("e8diagc", "e8diag"),
                          ("invf8c", "invf8"), ("pos_tc", "pos"),
                          ("e8bcc", "e8bc"),
                          ("cumsumc", "cumsum")]:
            self.dma(self.sy, "ld", t[dst][:], dr[name])
        self.dma(self.sy, "ld", t["sm8"][:, 0:8], dr["log_decay"])
        self.dma(self.sy, "ld", t["sm8"][:, 8:16], dr["mode_logits"])
        self.dma(self.sy, "ld", t["sm8"][:, 16:24], dr["omega_ls"])

    def load_dn_consts(self):
        t, dr = self.t, self.dr
        for name, dst in [("tril01c", "tril01"), ("diag01c", "diag01")]:
            self.dma(self.sy, "ld", t[dst][:], dr[name])

    def p1a_small(self):
        t, dr = self.t, self.dr
        v, s, te, g = self.v, self.s, self.te, self.g
        sm = t["sm8"]
        self.w(v, "ld"); self.w(s, "ld")
        s.activation(sm[:, 24:32], sm[:, 0:8], AF.Abs)
        self.w(s, "act")
        s.activation(sm[:, 32:40], sm[:, 24:32], AF.Exp, scale=-1.0)
        self.w(s, "act")
        s.activation(sm[:, 40:48], sm[:, 32:40], AF.Ln, bias=self.t["cconst"][0:8, 3:4])
        self.w(v, "act")
        v.tensor_scalar(sm[:, 48:56], sm[:, 0:8], 0.0, None, OP.max)
        v.tensor_tensor(sm[:, 48:56], sm[:, 48:56], sm[:, 40:48], OP.add)
        self.w(s, "dve")
        s.activation(sm[:, 56:64], sm[:, 48:56], AF.Exp, scale=-1.0)
        v.tensor_reduce(sm[:, 64:65], sm[:, 8:16], axis=mybir.AxisListType.X,
                                 op=OP.max)
        v.tensor_scalar(sm[:, 72:80], sm[:, 8:16], sm[:, 64:65], None, OP.subtract)
        self.w(s, "dve")
        s.activation(sm[:, 80:88], sm[:, 72:80], AF.Exp, accum_out=sm[:, 88:89])
        self.w(v, "act")
        v.reciprocal(sm[:, 96:97], sm[:, 88:89])
        v.tensor_scalar(sm[:, 104:112], sm[:, 80:88], sm[:, 96:97], SCALE_DN,
                                 OP.mult, OP.mult)
        self.w(s, "dve")
        s.activation(sm[:, 112:120], sm[:, 16:24], AF.Exp)
        e8sel = self.t["e8bc"][:].rearrange("p (h s) -> p h s", h=8)[:, :, 0:8]
        smb = self.t["sm8b"]
        self.w(v, "act"); self.w(v, "dve")
        v.tensor_copy(smb[0:8, 0:64].rearrange("p (h m) -> p h m", m=8),
                      sm[:, 56:64].unsqueeze(1).broadcast_to([8, 8, 8]))
        v.tensor_tensor(smb[0:8, 0:64].rearrange("p (h m) -> p h m", m=8),
                        smb[0:8, 0:64].rearrange("p (h m) -> p h m", m=8), e8sel, OP.mult)
        v.tensor_copy(smb[0:8, 64:128].rearrange("p (h m) -> p h m", m=8),
                      sm[:, 104:112].unsqueeze(1).broadcast_to([8, 8, 8]))
        v.tensor_tensor(smb[0:8, 64:128].rearrange("p (h m) -> p h m", m=8),
                                 smb[0:8, 64:128].rearrange("p (h m) -> p h m", m=8),
                                 e8sel, OP.mult)
        self.w(te, "dve"); self.w(te, "ld")
        ins = te.matmul(self.pb[5][:, 0:128], t["ones8"][:], smb[0:8, 0:128],
                        start=True, stop=True)
        self.inc(ins, "pe", 1)
        self.w(v, "pe")
        v.tensor_copy(t["rhow"][:], self.pb[5][:, 0:64])
        v.tensor_copy(t["wrow"][:], self.pb[5][:, 64:128])
        self.w(te, "dve")
        for m in range(NM):
            v.tensor_scalar(t["sm8b"][:, 0:256], t["invf8"][:],
                                     sm[:, 112 + m:113 + m], None, OP.mult)
            self.w(te, "dve")
            ins = te.matmul(self.pb[5][:, 0:256], t["ones8"][:], t["sm8b"][:, 0:256],
                            start=True, stop=True)
            self.inc(ins, "pe", 1)
            self.w(v, "pe")
            v.tensor_copy(t["omf"][:, _ts(m, 256)], self.pb[5][:, 0:256])
            self.w(te, "dve")

    def p0_lnx(self):
        t, dr = self.t, self.dr
        v, s, te = self.v, self.s, self.te
        stat = t["stat"]
        rel = {}
        for i in range(NCH):
            par = i % 2
            if i >= 2:
                self.sy.wait_ge(self.sems["act"], rel[i - 2])
            self.dma(self.sy, "ld", t["x0t"][:, _ts(par, D)], dr["x0"][_ts(i, 128), :])
            x = t["x0t"][:, _ts(par, D)]
            self.w(v, "ld")
            v.tensor_reduce(stat[:, i:i + 1], x, axis=mybir.AxisListType.X, op=OP.add)
            self.w(s, "ld"); self.w(s, "dve")
            s.activation(t["sqscr"][:], x, AF.Square,
                                  accum_out=stat[:, NCH + i:NCH + i + 1])
            rel[i] = self.cnt["act"]
        v_ = v
        self.w(v_, "act")
        v_.tensor_scalar(stat[:, 2 * NCH:3 * NCH], stat[:, 0:NCH], 1.0 / D, None, OP.mult)
        v_.tensor_scalar(stat[:, 3 * NCH:4 * NCH], stat[:, NCH:2 * NCH], 1.0 / D, None, OP.mult)
        v_.tensor_tensor(stat[:, 4 * NCH:5 * NCH], stat[:, 2 * NCH:3 * NCH],
                         stat[:, 2 * NCH:3 * NCH], OP.mult)
        v_.tensor_tensor(stat[:, 4 * NCH:5 * NCH], stat[:, 3 * NCH:4 * NCH],
                                  stat[:, 4 * NCH:5 * NCH], OP.subtract)
        self.w(s, "dve")
        s.activation(stat[:, 5 * NCH:6 * NCH], stat[:, 4 * NCH:5 * NCH], AF.Sqrt,
                              bias=self.t["cconst"][:, 0:1])
        self.w(v, "act")
        v.reciprocal(stat[:, 6 * NCH:7 * NCH], stat[:, 5 * NCH:6 * NCH])
        self.w(self.g, "ld"); self.w(self.g, "act"); self.w(self.g, "dve")
        relb = {}
        for i in range(NCH):
            par = i % 2
            if i >= 2:
                self.g.wait_ge(self.sems["dve"], relb[i - 2])
            self.dma(self.g, "ldw", t["x0t"][:, _ts(par, D)], dr["x0"][_ts(i, 128), :])
            self.w(v, "ldw")
            v.tensor_scalar(t["lnt"][:, _ts(par, D)], t["x0t"][:, _ts(par, D)],
                                     stat[:, 2 * NCH + i:2 * NCH + i + 1],
                                     stat[:, 6 * NCH + i:6 * NCH + i + 1],
                                     OP.subtract, OP.mult)
            relb[i] = self.cnt["dve"]
            self.w(te, "dve")
            for half in range(2):
                bank = self.ptrA if half == 0 else self.ptrB
                for j in range(4):
                    kk = half * 4 + j
                    ins = te.transpose(bank[:, _ts(j, 128)],
                                       t["lnt"][:, par * D + kk * 128: par * D + (kk + 1) * 128],
                                       t["identb"][:])
                self.inc(ins, "pe", 1)
                self.w(v, "pe")
                for j in range(4):
                    kk = half * 4 + j
                    ins = v.tensor_copy(
                        t["lnx_fm"][:, kk * NT_DN + i * 128: kk * NT_DN + (i + 1) * 128],
                        bank[:, _ts(j, 128)])
                ins
                self.w(te, "dve")

    def p1b_proj(self):
        t = self.t
        v, s, te, g = self.v, self.s, self.te, self.g
        pb = self.pb
        self.w(te, "dve")

        def loadw(src, cols):
            self.w(g, "pe")
            self.dma(g, "ldw", t["wbuf"][:, 0:8 * cols].rearrange("p (k c) -> p k c", c=cols),
                     self.dr[src].rearrange("(k p) c -> p k c", p=128))
            self.w(te, "ldw")

        def mm8(dst_bank, cols, i):
            for k in range(8):
                ins = te.matmul(dst_bank[:, 0:cols],
                                t["lnx_fm"][:, k * NT_DN + i * 128: k * NT_DN + (i + 1) * 128],
                                t["wbuf"][:, k * cols:(k + 1) * cols],
                                start=(k == 0), stop=(k == 7))
            self.inc(ins, "pe", 1)

        for wname, dst, rsq in [("Wq", "q_tm", "rsq_q"), ("Wk", "k_tm", "rsq_k")]:
            loadw(wname, 512)
            for i in range(NCH):
                mm8(pb[i % 2], 512, i)
                self.w(v, "pe")
                v.tensor_copy(t[dst][:, _ts(i, 512)], pb[i % 2][:, 0:512])
                self.w(s, "dve")
                s.activation(t["sqscr"][:, 0:512], t[dst][:, _ts(i, 512)],
                                      AF.Square)
                self.w(v, "act")
                v.tensor_reduce(t[rsq][:, _ts(i, 8)],
                                         t["sqscr"][:, 0:512].rearrange("p (h f) -> p h f", h=8),
                                         axis=mybir.AxisListType.X, op=OP.add)
                self.w(te, "dve")
        loadw("Wv", 512)
        for i in range(NCH):
            mm8(pb[i % 2], 512, i)
            self.w(v, "pe")
            v.tensor_copy(t["v_tm"][:, _ts(i, 512)], pb[i % 2][:, 0:512])
            if i >= 1:
                self.w(te, "dve")
        loadw("Wbt", 128)
        for i in range(NCH):
            mm8(pb[i % 2], 128, i)
            self.w(s, "pe")
            s.activation(t["beta"][:, _ts(i, 64)], pb[i % 2][:, 0:64], AF.Sigmoid)
            s.activation(t["sm64"][:], pb[i % 2][:, 64:128], AF.Sigmoid)
            self.w(v, "act")
            v.tensor_tensor(t["sm64b"][:], t["sm64"][:], t["rhow"][:], OP.mult)
            v.tensor_scalar(t["sm64b"][:], t["sm64b"][:], CLIP, None, OP.max)
            self.w(s, "dve")
            s.activation(t["g_tm"][:, _ts(i, 64)], t["sm64b"][:], AF.Ln)
            self.w(te, "act")
        loadw("Wg", 512)
        for i in range(NCH):
            mm8(pb[i % 2], 512, i)
            self.w(s, "pe")
            s.activation(t["gate"][:, _ts(i, 512)], pb[i % 2][:, 0:512], AF.Sigmoid)
            if i >= 1:
                self.w(te, "act")
        self.w(s, "dve")
        s.activation(t["rsq_q"][:], t["rsq_q"][:], AF.Sqrt,
                              bias=self.t["cconst"][:, 1:2])
        s.activation(t["rsq_k"][:], t["rsq_k"][:], AF.Sqrt,
                              bias=self.t["cconst"][:, 1:2])
        self.w(v, "act")
        v.reciprocal(t["rsq_q"][:], t["rsq_q"][:])
        v.reciprocal(t["rsq_k"][:], t["rsq_k"][:])

    def p1c_cumsum(self):
        t = self.t
        v, te = self.v, self.te
        pb5 = self.pb[5]
        for i in range(NCH):
            self.w(te, "act"); self.w(te, "dve")
            ins = te.matmul(pb5[:, 0:64], t["cumsum"][:], t["g_tm"][:, _ts(i, 64)],
                            start=True, stop=True)
            self.inc(ins, "pe", 1)
            self.w(v, "pe")
            v.tensor_copy(t["g_tm"][:, _ts(i, 64)], pb5[:, 0:64])
            self.w(te, "dve")

    def p2_deltanet(self):
        t = self.t
        v, te = self.v, self.te
        v.memset(t["S"][:], 0.0)
        v.memset(t["S_bf"][:], 0.0)
        for c in range(min(NCH, DN_CHUNKS)):
            v.tensor_copy(t["ghi"][:], t["g_tm"][:, _ts(c, 64)])
            v.tensor_tensor(t["glo"][:], t["g_tm"][:, _ts(c, 64)], t["ghi"][:],
                                     OP.subtract)
            for m in range(NM):
                self.w(te, "dve")
                srh = t["ghi"][:].rearrange("p (h m) -> p m h", m=8)[:, m, :]
                srl = t["glo"][:].rearrange("p (h m) -> p m h", m=8)[:, m, :]
                te.transpose(self.ptrA[0:8, 0:128], srh, t["identb"][:])
                ins = te.transpose(self.ptrA[0:8, 128:256], srl, t["identb"][:])
                self.inc(ins, "pe", 1)
                self.w(v, "pe")
                v.tensor_copy(t["gmc"][0:8, _ts(m, 128)], self.ptrA[0:8, 0:128])
                v.tensor_tensor(t["gmc"][0:8, _ts(m, 128)],
                                         t["gmc"][0:8, _ts(m, 128)],
                                         self.ptrA[0:8, 128:256], OP.add)
                self.w(te, "dve")
            for m in range(NM):
                self.dn_cm(c, m)
            self.w(v, "pe")
            v.tensor_tensor(t["og"][:, _ts(c, 512)], self.pb[5][:, 0:512],
                                     t["gate"][:, _ts(c, 512)], OP.mult)
            self.w(te, "dve")

    def dn_cm(self, c, m):
        t = self.t
        v, s, te = self.v, self.s, self.te
        pb = self.pb
        gchunk = t["gmc"][0:8, _ts(m, 128)]
        c64 = t["beta"][:, _ts(c, 64)].rearrange("p (h m) -> p m h", m=8)
        bslice = c64[:, m, :]
        gslice = t["g_tm"][:, _ts(c, 64)].rearrange("p (h m) -> p m h", m=8)[:, m, :]
        ws = t["wrow"][:].rearrange("p (h m) -> p m h", m=8)[:, m, :]
        f8t = t["f8"][:]

        def hsl(name, h):
            return t[name][:, h * 128:(h + 1) * 128]

        def fmh(name, h):
            return t[name][0:64, h * 128:(h + 1) * 128]

        ang = t["trg"][:]; t2 = t["trg2"][:]; t3 = t["trg3"][:]
        sv_ = t["sinv"][:]; cv_ = t["cosv"][:]
        v.tensor_scalar(ang, t["omf"][:, _ts(m, 256)], t["pos"][:, c:c + 1], None,
                                 OP.mult)
        v.tensor_scalar(t2, ang, INV_2PI, None, OP.mult)
        v.tensor_scalar(t2, t2, MAGIC, None, OP.add)
        v.tensor_scalar(t2, t2, MAGIC, None, OP.subtract)
        v.scalar_tensor_tensor(t3, t2, -RR_C1, ang, OP.mult, OP.add)
        v.scalar_tensor_tensor(t3, t2, -RR_C2, t3, OP.mult, OP.add)
        v.tensor_scalar(t3, t3, -PI, PI, OP.max, OP.min)
        self.w(s, "dve")
        s.activation(sv_, t3, AF.Sin)
        v.tensor_scalar(t2, t3, PI / 2, None, OP.is_gt)
        v.scalar_tensor_tensor(t2, t2, -TWO_PI, t3, OP.mult, OP.add)
        v.tensor_scalar(t2, t2, -PI / 2 - PI, PI - PI / 2, OP.max, OP.min)
        self.w(s, "dve")
        s.activation(cv_, t2, AF.Sin, bias=t["cconst"][:, 2:3])
        s.activation(f8t[:, 0:8], gslice, AF.Exp)

        if DN_STAGE < 2:
            return
        self.w(v, "act")
        rot = t["rot"][:]
        r13 = t["rot1"][:].rearrange("p (h f) -> p h f", h=8)
        r23 = t["rot2"][:].rearrange("p (h f) -> p h f", h=8)
        c3 = cv_.rearrange("p (h f) -> p h f", h=8)
        s3 = sv_.rearrange("p (h f) -> p h f", h=8)
        rot3 = rot.rearrange("p (h f) -> p h f", h=8)
        for srcn, dstn, isq in [("q_tm", "qe_tm", True), ("k_tm", "kr_tm", False)]:
            src3 = t[srcn][:, _ts(c, 512)].rearrange("p (h f) -> p h f", h=8)
            lo = src3[:, :, 0:32]; hi = src3[:, :, 32:64]
            v.tensor_tensor(r13, lo, c3, OP.mult)
            v.tensor_tensor(r23, hi, s3, OP.mult)
            v.tensor_tensor(rot3[:, :, 0:32], r13, r23, OP.add)
            v.tensor_tensor(r13, hi, c3, OP.mult)
            v.tensor_tensor(r23, lo, s3, OP.mult)
            v.tensor_tensor(rot3[:, :, 32:64], r13, r23, OP.subtract)
            bx = t["bexp"][:]
            if isq:
                rq = t["rsq_q"][:, _ts(c, 8)]
                v.tensor_tensor(f8t[:, 8:16], rq, ws, OP.mult)
                v.tensor_copy(bx.rearrange("p (h f) -> p h f", h=8),
                                       f8t[:, 8:16].unsqueeze(-1).broadcast_to([128, 8, 64]))
                v.tensor_tensor(t["qn_tm"][:], rot, bx, OP.mult)
                v.tensor_tensor(f8t[:, 8:16], f8t[:, 8:16], f8t[:, 0:8], OP.mult)
                fcol = f8t[:, 8:16]
            else:
                fcol = t["rsq_k"][:, _ts(c, 8)]
            v.tensor_copy(bx.rearrange("p (h f) -> p h f", h=8),
                                   fcol.unsqueeze(-1).broadcast_to([128, 8, 64]))
            v.tensor_tensor(t[dstn][:], rot, bx, OP.mult)

        if DN_STAGE < 3:
            return
        for srcn, dstn, bank in [("qe_tm", "qe_fm", self.ptrA), ("kr_tm", "kr_fm", self.ptrB),
                                 ("qn_tm", "qn_fm", self.ptrA)]:
            self.w(te, "dve")
            for half in range(2):
                for j in range(4):
                    hh = half * 4 + j
                    ins = te.transpose(bank[0:64, _ts(j, 128)],
                                       t[srcn][:, hh * 64:(hh + 1) * 64], t["identb"][:])
                self.inc(ins, "pe", 1)
                self.w(v, "pe")
                v.tensor_copy(t[dstn][0:64, half * 512:(half + 1) * 512],
                                       bank[0:64, 0:512])
                self.w(te, "dve")
        self.w(te, "dve")

        if DN_STAGE < 4:
            return
        fx = t["fexp"][:]
        v.tensor_copy(fx[0:8, :].rearrange("p (h f) -> p h f", h=8),
                               gchunk.unsqueeze(1).broadcast_to([8, 8, 128]))
        v.tensor_tensor(fx[0:8, :], fx[0:8, :], t["e8bc"][:], OP.mult)
        v.tensor_scalar(fx[0:8, :], fx[0:8, :], -1.0, None, OP.mult)
        self.w(te, "dve")
        for half in range(2):
            bank = pb[half]
            te.matmul(bank[:, 0:512], gchunk, t["e8bc"][:, _ts(half, 512)],
                      start=True, stop=False)
            ins = te.matmul(bank[:, 0:512], t["ones8"][:], fx[0:8, _ts(half, 512)],
                            start=False, stop=True)
        self.inc(ins, "pe", 1)
        self.w(v, "pe")
        ea = t["eargp"][:]; e2 = t["earg2"][:]
        fx2 = t["fexp"][:]
        v.tensor_scalar(fx2[:, 0:512], pb[0][:, 0:512], 0.0, None, OP.min)
        v.tensor_scalar(fx2[:, 512:1024], pb[1][:, 0:512], 0.0, None, OP.min)
        v.tensor_scalar(t["btril"][:].bitcast(F32)[:, 0:256] if False else t["rot"][:],
                        pb[0][:, 0:512], 0.0, None, OP.max)
        v.tensor_scalar(t["bexp"][:], pb[1][:, 0:512], 0.0, None, OP.max)
        self.w(s, "dve")
        s.activation(ea[:, 0:512], fx2[:, 0:512], AF.Exp)
        s.activation(ea[:, 512:1024], fx2[:, 512:1024], AF.Exp)
        s.activation(e2[:, 0:512], t["rot"][:], AF.Exp, scale=-1.0)
        s.activation(e2[:, 512:1024], t["bexp"][:], AF.Exp, scale=-1.0)
        self.w(te, "act")

        if DN_STAGE < 5:
            return
        for h in range(8):
            bank = pb[0] if h < 4 else pb[1]
            ins = te.matmul(bank[:, _ts(h % 4, 128)], fmh("kr_fm", h), fmh("kr_fm", h),
                            start=True, stop=True)
        self.inc(ins, "pe", 1)
        for h in range(8):
            bank = pb[2] if h < 4 else pb[3]
            ins = te.matmul(bank[:, _ts(h % 4, 128)], fmh("kr_fm", h), fmh("qn_fm", h),
                            start=True, stop=True)
        self.inc(ins, "pe", 1)

        if DN_STAGE < 6:
            return
        self.w(v, "act"); self.w(v, "pe")
        v.tensor_copy(fx.rearrange("p (h f) -> p h f", h=8),
                               bslice.unsqueeze(-1).broadcast_to([128, 8, 128]))
        v.tensor_tensor(t["btril"][:], fx, t["tril01"][:], OP.mult)
        v.tensor_tensor(ea, ea, t["btril"][:], OP.mult)
        v.tensor_tensor(e2, e2, t["diag01"][:], OP.mult)
        for half in range(2):
            ab = pb[0] if half == 0 else pb[1]
            qb = pb[2] if half == 0 else pb[3]
            v.tensor_tensor(t["x0m"][:, _ts(half, 512)], ea[:, _ts(half, 512)],
                            ab[:, 0:512], OP.mult)
            ins = v.tensor_tensor(t["ptm"][:, _ts(half, 512)], e2[:, _ts(half, 512)],
                                  qb[:, 0:512], OP.mult)
        ins

        if DN_STAGE < 7:
            return
        self.w(te, "dve")
        for half in range(2):
            bank = self.ptrA if half == 0 else self.ptrB
            for j in range(4):
                hh = half * 4 + j
                ins = te.transpose(bank[:, _ts(j, 128)], hsl("x0m", hh), t["identb"][:])
            self.inc(ins, "pe", 1)
            self.w(v, "pe")
            v.tensor_copy(t["y0m"][:, _ts(half, 512)], bank[:, 0:512])

        if DN_STAGE < 8:
            return
        self.w(te, "dve")
        for h in range(8):
            ins = te.matmul(pb[0][:, h * 64:(h + 1) * 64], fmh("kr_fm", h),
                            t["S_bf"][0:64, (h * 8 + m) * 64:(h * 8 + m + 1) * 64],
                            start=True, stop=True)
        self.inc(ins, "pe", 1)
        bx = t["bexp"][:]
        v.tensor_copy(bx.rearrange("p (h f) -> p h f", h=8),
                               f8t[:, 0:8].unsqueeze(-1).broadcast_to([128, 8, 64]))
        self.w(v, "pe")
        v.tensor_tensor(t["dfp"][:], bx, pb[0][:, 0:512], OP.mult)
        v.tensor_tensor(t["dfp"][:], t["v_tm"][:, _ts(c, 512)], t["dfp"][:],
                                 OP.subtract)
        v.tensor_copy(t["bexp2"][:].rearrange("p (h f) -> p h f", h=8),
                               bslice.unsqueeze(-1).broadcast_to([128, 8, 64]))
        v.tensor_tensor(t["dfp"][:], t["dfp"][:], t["bexp2"][:], OP.mult)
        v.tensor_copy(t["dsol"][:], t["dfp"][:])
        self.w(te, "dve")

        if DN_STAGE < 9:
            return
        def xy(j):
            if j == 0:
                return ("x0m", 0), ("y0m", 0)
            sl = (j - 1) % 2
            return ("xpow", sl * 1024), ("ypow", sl * 1024)

        for j in range(7):
            (xn, xo), (yn, yo) = xy(j)
            for h in range(8):
                ins = te.matmul(pb[0][:, h * 64:(h + 1) * 64],
                                t[yn][:, yo + h * 128: yo + (h + 1) * 128],
                                t["dsol"][:, h * 64:(h + 1) * 64],
                                start=True, stop=True)
            self.inc(ins, "pe", 1)
            if j < 6:
                for h in range(8):
                    bank = pb[1] if h < 4 else pb[2]
                    ins = te.matmul(bank[:, _ts(h % 4, 128)],
                                    t[yn][:, yo + h * 128: yo + (h + 1) * 128],
                                    t[xn][:, xo + h * 128: xo + (h + 1) * 128],
                                    start=True, stop=True)
                self.inc(ins, "pe", 1)
                for h in range(8):
                    bank = pb[3] if h < 4 else pb[4]
                    ins = te.matmul(bank[:, _ts(h % 4, 128)],
                                    t[xn][:, xo + h * 128: xo + (h + 1) * 128],
                                    t[yn][:, yo + h * 128: yo + (h + 1) * 128],
                                    start=True, stop=True)
                self.inc(ins, "pe", 1)
            self.w(v, "pe")
            op = OP.subtract if j == 0 else OP.add
            v.tensor_tensor(t["dfp"][:], t["dfp"][:], pb[0][:, 0:512], op)
            v.tensor_copy(t["dsol"][:], t["dfp"][:])
            if j < 6:
                sl = j % 2
                v.tensor_copy(t["xpow"][:, sl * 1024: sl * 1024 + 512], pb[1][:, 0:512])
                v.tensor_copy(t["xpow"][:, sl * 1024 + 512: sl * 1024 + 1024], pb[2][:, 0:512])
                v.tensor_copy(t["ypow"][:, sl * 1024: sl * 1024 + 512], pb[3][:, 0:512])
                v.tensor_copy(t["ypow"][:, sl * 1024 + 512: sl * 1024 + 1024],
                                       pb[4][:, 0:512])
            self.w(te, "dve")

        if DN_STAGE < 10:
            return
        first = (m == 0)
        for h in range(8):
            te.matmul(pb[5][:, h * 64:(h + 1) * 64], fmh("qe_fm", h),
                      t["S_bf"][0:64, (h * 8 + m) * 64:(h * 8 + m + 1) * 64],
                      start=(first and h == 0), stop=False, skip_group_check=True)
            ins = te.matmul(pb[5][:, h * 64:(h + 1) * 64], hsl("ptm", h),
                            t["dsol"][:, h * 64:(h + 1) * 64],
                            start=False, stop=(m == NM - 1 and h == 7), skip_group_check=True)
        self.inc(ins, "pe", 1)

        if DN_STAGE < 11:
            return
        gc = gchunk[:, 127:128]
        v.tensor_tensor(t["sm8b"][:, 0:8], t["e8diag"][:],
                                 gc.broadcast_to([8, 8]), OP.mult)
        self.w(te, "dve")
        ins = te.matmul(pb[4][:, 0:8], t["ones8"][:], t["sm8b"][:, 0:8], start=True, stop=True)
        self.inc(ins, "pe", 1)
        self.w(v, "pe")
        v.tensor_tensor(f8t[:, 16:24], pb[4][:, 0:8], gslice, OP.subtract)
        self.w(s, "dve"); self.w(s, "pe")
        s.activation(f8t[:, 24:32], f8t[:, 16:24], AF.Exp)
        s.activation(t["sups"][0:64, 1024:1032], pb[4][0:64, 0:8], AF.Exp)
        self.w(v, "act")
        v.tensor_copy(bx.rearrange("p (h f) -> p h f", h=8),
                               f8t[:, 24:32].unsqueeze(-1).broadcast_to([128, 8, 64]))
        v.tensor_tensor(t["d2bf"][:], t["dfp"][:], bx, OP.mult)
        self.w(te, "dve")
        for h in range(8):
            ins = te.matmul(pb[4][0:64, h * 64:(h + 1) * 64],
                            t["kr_tm"][:, h * 64:(h + 1) * 64],
                            t["d2bf"][:, h * 64:(h + 1) * 64],
                            start=True, stop=True)
        self.inc(ins, "pe", 1)
        S3 = t["S"][:].rearrange("k (h m v) -> k h m v", h=8, m=8)[:, :, m, :]
        S3b = t["S_bf"][0:64].rearrange("k (h m v) -> k h m v", h=8, m=8)[:, :, m, :]
        eg = t["sups"][0:64, 1024:1032]
        egx = t["sups"][0:64, 0:512]
        sc2 = t["sups"][0:64, 512:1024]
        v.tensor_copy(egx.rearrange("k (h f) -> k h f", h=8),
                               eg.unsqueeze(-1).broadcast_to([64, 8, 64]))
        v.tensor_tensor(sc2.rearrange("k (h f) -> k h f", h=8), S3,
                                 egx.rearrange("k (h f) -> k h f", h=8), OP.mult)
        self.w(v, "pe")
        v.tensor_tensor(S3, sc2.rearrange("k (h f) -> k h f", h=8),
                                 pb[4][0:64, 0:512].rearrange("k (h f) -> k h f", h=8), OP.add)
        v.tensor_copy(S3b, S3)
        self.w(te, "dve")

    def p3_attn_out(self):
        t, dr = self.t, self.dr
        v, s, te = self.v, self.s, self.te
        pb = self.pb
        self.dma(self.g, "ldw", t["wo"][:].rearrange("p (k c) -> p k c", c=1024),
                 dr["Wo"].rearrange("(k p) c -> p k c", p=128))
        self.w(te, "ldw")
        for a in range(NTL_AT):
            c = a + 1
            self.w(te, "dve")
            for j in range(4):
                ins = te.transpose(self.ptrA[:, _ts(j, 128)],
                                   t["og"][:, c * 512 + j * 128: c * 512 + (j + 1) * 128],
                                   t["identb"][:])
            self.inc(ins, "pe", 1)
            self.w(v, "pe")
            v.tensor_copy(t["aofm"][:], self.ptrA[:, 0:512])
            self.w(te, "dve")
            for n in range(2):
                for k in range(4):
                    ins = te.matmul(pb[n][:, 0:512], t["aofm"][:, _ts(k, 128)],
                                    t["wo"][:, k * 1024 + n * 512: k * 1024 + (n + 1) * 512],
                                    start=(k == 0), stop=(k == 3))
                self.inc(ins, "pe", 1)
            self.w(v, "pe"); self.w(s, "pe")
            st = t["stat"]
            v.tensor_reduce(st[:, 0:1], pb[0][:, 0:512], axis=mybir.AxisListType.X, op=OP.add)
            v.tensor_reduce(st[:, 1:2], pb[1][:, 0:512], axis=mybir.AxisListType.X, op=OP.add)
            v.tensor_tensor(st[:, 0:1], st[:, 0:1], st[:, 1:2], OP.add)
            v.tensor_scalar(st[:, 2:3], st[:, 0:1], 1.0 / D, None, OP.mult)
            self.w(s, "dve")
            s.activation(t["sqscr3"][:, 0:512], pb[0][:, 0:512], AF.Square,
                                  accum_out=st[:, 3:4])
            s.activation(t["sqscr3"][:, 512:1024], pb[1][:, 0:512], AF.Square,
                                  accum_out=st[:, 4:5])
            self.w(v, "act")
            v.tensor_tensor(st[:, 3:4], st[:, 3:4], st[:, 4:5], OP.add)
            v.tensor_scalar(st[:, 5:6], st[:, 3:4], 1.0 / D, None, OP.mult)
            v.tensor_tensor(st[:, 6:7], st[:, 2:3], st[:, 2:3], OP.mult)
            v.tensor_tensor(st[:, 6:7], st[:, 5:6], st[:, 6:7], OP.subtract)
            self.w(s, "dve")
            s.activation(st[:, 7:8], st[:, 6:7], AF.Sqrt, bias=t["cconst"][:, 0:1])
            self.w(v, "act")
            v.reciprocal(st[:, 8:9], st[:, 7:8])
            par = a % 2
            self.w(self.sy, "dve")
            self.dma(self.sy, "ld", t["x0t3"][:, _ts(par, D)],
                     dr["x0"][CHK + a * CHK: CHK + (a + 1) * CHK, :])
            self.w(v, "ld")
            for n in range(2):
                v.tensor_scalar(t["x1t"][:, par * D + n * 512: par * D + (n + 1) * 512],
                                pb[n][:, 0:512], st[:, 2:3], st[:, 8:9], OP.subtract, OP.mult)
            v.tensor_tensor(t["x1t"][:, _ts(par, D)], t["x1t"][:, _ts(par, D)],
                                     t["x0t3"][:, _ts(par, D)], OP.add)
            self.w(te, "dve")
            self.w(self.g, "dve")
            self.dma(self.g, "sto", dr["x1_spill"][_ts(a, 128), :], t["x1t"][:, _ts(par, D)])
            if a >= 1 and BUILD_PHASES < 4:
                self.dma(self.g, "sto", dr["out"][_ts(a - 1, 128), :], t["x1t"][:, _ts(par, D)])
            if a >= 1:
                self.w(v, "sto")

    def emit_ffn_block(self):
        with ExitStack() as scB:
            sbB = lambda n, sh, dt=F32: self.t.__setitem__(
                n, scB.enter_context(self.nc.sbuf_tensor(n, sh, dt)))
            sbB("h2_fm", [128, 8 * OWN], BF16)
            sbB("ce_all", [128, NTL_OWN * NE], F32)
            sbB("ones8b", [8, 128], BF16)
            v = self.v
            self.w(v, "ld")
            v.tensor_copy(self.t["ones8b"][0:8, :], self.t["ones8"][0:8, :])
            self.p4_window_attn()
            if BUILD_PHASES >= 5:
                self.p5_gate()
            if BUILD_PHASES >= 6:
                self.p6_moe()
            self.p_taps(7)
            self.phase_barrier()

    def p4_window_attn(self):
        t, dr = self.t, self.dr
        v, s, te, g, sy = self.v, self.s, self.te, self.g, self.sy
        pb, st = self.pb, self.t["stat"]
        with ExitStack() as sc4:
            sb4 = lambda n, sh, dt=F32: self.t.__setitem__(
                n, sc4.enter_context(self.nc.sbuf_tensor(n, sh, dt)))
            sb4("wqkv", [128, 8 * 1536], BF16)
            sb4("projw", [128, 8 * 1024], BF16)
            sb4("awf", [128, 8], F32)
            sb4("wffm", [128, 8], F32)
            sb4("qkvb8f", [8, 1536], F32); sb4("qkvb8", [8, 1536], BF16)
            sb4("projb8f", [8, 1024], F32); sb4("projb8", [8, 1024], BF16)
            sb4("snk8", [8, 16], F32); sb4("snk", [128, 16], F32)
            sb4("mb", [128, 256], F32); sb4("mf", [128, 256], F32)
            sb4("x1ld", [128, 2 * D], F32)
            sb4("ropc", [128, 2 * D], F32)
            sb4("rops", [128, 2 * D], F32)
            sb4("y1t", [128, D], F32)
            sb4("hnb", [128, D], BF16)
            sb4("hfm", [128, 8 * 128], BF16)
            sb4("qkvt", [128, 1536], F32)
            sb4("shuf", [128, D], F32)
            sb4("rr", [128, D], F32)
            sb4("qrb", [128, D], BF16)
            sb4("krb", [128, 256], BF16)
            sb4("qhf", [64, 16 * 128], BF16)
            sb4("khf", [64, 2 * 4 * 128], BF16)
            sb4("vrng", [128, 2 * 256], BF16)
            sb4("lg", [128, 256], F32)
            sb4("pexp", [128, 256], BF16)
            sb4("att", [128, 256], BF16)
            sb4("ytm", [128, D], BF16)
            sb4("yfm", [128, 8 * 128], BF16)
            sb4("x2t", [128, 2 * D], F32)
            sb4("x2nb", [128, D], BF16)
            sb4("sq4", [128, D], F32)
            self.dma(g, "ldw", t["wqkv"][:].rearrange("p (k c) -> p k c", c=1536),
                     dr["qkv_w"].rearrange("(k p) c -> p k c", p=128))
            self.dma(g, "ldw", t["projw"][:].rearrange("p (k c) -> p k c", c=1024),
                     dr["proj_w"].rearrange("(k p) c -> p k c", p=128))
            self.dma(g, "ldw", t["awf"][:],
                     dr["attn_norm_w"].rearrange("o (k p) -> (o p) k", p=128),
                     allow_slow_non_contiguous=True)
            self.dma(g, "ldw", t["wffm"][:],
                     dr["ffn_norm_w"].rearrange("o (k p) -> (o p) k", p=128),
                     allow_slow_non_contiguous=True)
            self.dma(g, "ldw", t["mb"][:], dr["maskband"])
            self.dma(g, "ldw", t["mf"][:], dr["maskfirst"])
            for p in range(8):
                self.dma(sy, "ld", t["qkvb8f"][p:p + 1, :], dr["qkv_b"])
                self.dma(sy, "ld", t["projb8f"][p:p + 1, :], dr["proj_b"])
                self.dma(sy, "ld", t["snk8"][p:p + 1, :], dr["attn_sink"])
            self.w(v, "ld")
            v.tensor_scalar(t["qkvb8"][0:8, :], t["qkvb8f"][0:8, :], 0.125, None, OP.mult)
            v.tensor_scalar(t["projb8"][0:8, :], t["projb8f"][0:8, :], 0.125, None, OP.mult)
            v.tensor_scalar(t["snk8"][0:8, :], t["snk8"][0:8, :], 0.125, None, OP.mult)
            self.w(te, "dve")
            ins = te.matmul(pb[5][:, 0:16], t["ones8"][0:8, :], t["snk8"][0:8, :],
                            start=True, stop=True)
            self.inc(ins, "pe", 1)
            self.w(v, "pe")
            v.tensor_copy(t["snk"][:], pb[5][:, 0:16])
            self.w(v, "ldw")
            for k in range(8):
                v.tensor_scalar(t["wqkv"][:, _ts(k, 1536)], t["wqkv"][:, _ts(k, 1536)],
                                t["awf"][:, k:k + 1], None, OP.mult)
            self.w(te, "dve")
            for a in range(NTL_AT):
                par = a % 2
                prev, cur = 1 - par, par
                self.w(sy, "dve")
                self.dma(sy, "ld", t["x1ld"][:, _ts(par, D)], dr["x1_spill"][_ts(a, 128), :])
                self.dma(sy, "ld", t["ropc"][:, _ts(par, D)], dr["ropecos"][_ts(a, 128), :])
                self.dma(sy, "ld", t["rops"][:, _ts(par, D)], dr["ropesin"][_ts(a, 128), :])
                self.w(v, "ld"); self.w(s, "ld")
                x1 = t["x1ld"][:, _ts(par, D)]
                v.tensor_reduce(st[:, 0:1], x1, axis=mybir.AxisListType.X, op=OP.add)
                v.tensor_scalar(st[:, 1:2], st[:, 0:1], 1.0 / D, None, OP.mult)
                self.w(s, "dve")
                s.activation(t["sq4"][:], x1, AF.Square, accum_out=st[:, 2:3])
                self.w(v, "act")
                v.tensor_scalar(st[:, 3:4], st[:, 2:3], 1.0 / D, None, OP.mult)
                v.tensor_tensor(st[:, 4:5], st[:, 1:2], st[:, 1:2], OP.mult)
                v.tensor_tensor(st[:, 4:5], st[:, 3:4], st[:, 4:5], OP.subtract)
                self.w(s, "dve")
                s.activation(st[:, 5:6], st[:, 4:5], AF.Sqrt, bias=t["cconst"][:, 0:1])
                self.w(v, "act")
                v.reciprocal(st[:, 6:7], st[:, 5:6])
                v.tensor_scalar(t["y1t"][:], x1, st[:, 1:2], st[:, 6:7],
                                OP.subtract, OP.mult)
                self.w(s, "dve")
                s.activation(t["sq4"][:], t["y1t"][:], AF.Square, accum_out=st[:, 7:8])
                self.w(v, "act")
                v.tensor_scalar(st[:, 8:9], st[:, 7:8], 1.0 / D, None, OP.mult)
                self.w(s, "dve")
                s.activation(st[:, 9:10], st[:, 8:9], AF.Sqrt, bias=t["cconst"][:, 0:1])
                self.w(v, "act")
                v.reciprocal(st[:, 10:11], st[:, 9:10])
                v.tensor_scalar(t["hnb"][:], t["y1t"][:], st[:, 10:11], None, OP.mult)
                self.w(te, "dve")
                for j in range(8):
                    ptr = self.ptrA if j < 4 else self.ptrB
                    ins = te.transpose(ptr[:, _ts(j % 4, 128)], t["hnb"][:, _ts(j, 128)],
                                       t["identb"][:])
                self.inc(ins, "pe", 1)
                self.w(v, "pe")
                v.tensor_copy(t["hfm"][:, 0:512], self.ptrA[:, 0:512])
                v.tensor_copy(t["hfm"][:, 512:1024], self.ptrB[:, 0:512])
                self.w(te, "dve"); self.w(te, "ldw")
                for n in range(3):
                    for k in range(8):
                        te.matmul(pb[n][:, 0:512], t["hfm"][:, _ts(k, 128)],
                                  t["wqkv"][:, k * 1536 + n * 512: k * 1536 + (n + 1) * 512],
                                  start=(k == 0), stop=False, skip_group_check=True)
                    ins = te.matmul(pb[n][:, 0:512], t["ones8b"][0:8, :],
                                    t["qkvb8"][0:8, _ts(n, 512)],
                                    start=False, stop=True, skip_group_check=True)
                    self.inc(ins, "pe", 1)
                self.w(v, "pe")
                for n in range(3):
                    v.tensor_copy(t["qkvt"][:, _ts(n, 512)], pb[n][:, 0:512])
                self.w(te, "dve")
                cosT = t["ropc"][:, _ts(par, D)]
                sinT = t["rops"][:, _ts(par, D)]
                kk = t["qkvt"][:, 1024:1280]
                shkA = t["shuf"][:, 0:256].rearrange("p (i two) -> p i two", two=2)
                kkA = kk.rearrange("p (i two) -> p i two", two=2)
                v.tensor_copy(shkA[:, :, 0:1], kkA[:, :, 1:2])
                v.tensor_copy(shkA[:, :, 1:2], kkA[:, :, 0:1])
                v.tensor_tensor(t["rr"][:, 0:256], kk, cosT[:, 0:256], OP.mult)
                v.tensor_tensor(t["shuf"][:, 0:256], t["shuf"][:, 0:256], sinT[:, 0:256],
                                OP.mult)
                v.tensor_tensor(t["rr"][:, 0:256], t["rr"][:, 0:256], t["shuf"][:, 0:256],
                                OP.add)
                v.tensor_copy(t["krb"][:], t["rr"][:, 0:256])
                v.tensor_copy(t["vrng"][:, _ts(par, 256)], t["qkvt"][:, 1280:1536])
                self.w(te, "dve")
                for gi in range(4):
                    ins = te.transpose(self.ptrB[0:64, _ts(gi, 128)],
                                       t["krb"][:, _ts(gi, 64)], t["identb"][:])
                self.inc(ins, "pe", 1)
                self.w(v, "pe")
                v.tensor_copy(t["khf"][0:64, _ts(par, 512)], self.ptrB[0:64, 0:512])
                self.w(te, "dve")
                if a == 0:
                    continue
                qq = t["qkvt"][:, 0:1024]
                shA = t["shuf"][:].rearrange("p (i two) -> p i two", two=2)
                qqA = qq.rearrange("p (i two) -> p i two", two=2)
                v.tensor_copy(shA[:, :, 0:1], qqA[:, :, 1:2])
                v.tensor_copy(shA[:, :, 1:2], qqA[:, :, 0:1])
                v.tensor_tensor(t["rr"][:], qq, cosT, OP.mult)
                v.tensor_tensor(t["shuf"][:], t["shuf"][:], sinT, OP.mult)
                v.tensor_tensor(t["rr"][:], t["rr"][:], t["shuf"][:], OP.add)
                v.tensor_copy(t["qrb"][:], t["rr"][:])
                for hh in range(4):
                    self.w(te, "dve")
                    for j in range(4):
                        h = hh * 4 + j
                        ins = te.transpose(self.ptrA[0:64, _ts(j, 128)],
                                           t["qrb"][:, _ts(h, 64)], t["identb"][:])
                    self.inc(ins, "pe", 1)
                    self.w(v, "pe")
                    v.tensor_copy(t["qhf"][0:64, _ts(hh, 512)], self.ptrA[0:64, 0:512])
                mask = t["mf"] if a == 1 else t["mb"]
                for h in range(FH):
                    gi = h // 4
                    self.w(te, "dve"); self.w(te, "act")
                    for kt, slot in enumerate((prev, cur)):
                        ins = te.matmul(
                            pb[3][:, kt * 128:(kt + 1) * 128],
                            t["qhf"][0:64, _ts(h, 128)],
                            t["khf"][0:64, slot * 512 + gi * 128: slot * 512 + (gi + 1) * 128],
                            start=True, stop=True, skip_group_check=True)
                    self.inc(ins, "pe", 1)
                    self.w(v, "pe")
                    v.scalar_tensor_tensor(t["lg"][:], pb[3][:, 0:256], SCALE_AT, mask[:],
                                           OP.mult, OP.add)
                    v.tensor_reduce(st[:, 11:12], t["lg"][:], axis=mybir.AxisListType.X,
                                    op=OP.max)
                    v.tensor_scalar(st[:, 11:12], st[:, 11:12], t["snk"][:, h:h + 1],
                                    None, OP.max)
                    v.tensor_scalar(st[:, 12:13], st[:, 11:12], -1.0, None, OP.mult)
                    self.w(s, "dve")
                    s.activation(t["pexp"][:], t["lg"][:], AF.Exp, bias=st[:, 12:13],
                                 accum_out=st[:, 13:14])
                    s.activation(st[:, 14:15], t["snk"][:, h:h + 1], AF.Exp,
                                 bias=st[:, 12:13])
                    self.w(v, "act")
                    v.tensor_tensor(st[:, 13:14], st[:, 13:14], st[:, 14:15], OP.add)
                    v.reciprocal(st[:, 15:16], st[:, 13:14])
                    self.w(te, "act"); self.w(te, "dve")
                    for kt in range(2):
                        ins = te.transpose(self.ptrB[:, kt * 128:(kt + 1) * 128],
                                           t["pexp"][:, kt * 128:(kt + 1) * 128],
                                           t["identb"][:])
                    self.inc(ins, "pe", 1)
                    self.w(v, "pe")
                    v.tensor_copy(t["att"][:], self.ptrB[:, 0:256])
                    self.w(te, "dve")
                    for kt, slot in enumerate((prev, cur)):
                        ins = te.matmul(
                            pb[4][:, _ts(h % 8, 64)],
                            t["att"][:, kt * 128:(kt + 1) * 128],
                            t["vrng"][:, slot * 256 + gi * 64: slot * 256 + (gi + 1) * 64],
                            start=(kt == 0), stop=(kt == 1), skip_group_check=True)
                    self.inc(ins, "pe", 1)
                    self.w(v, "pe")
                    v.tensor_scalar(t["ytm"][:, _ts(h, 64)], pb[4][:, _ts(h % 8, 64)],
                                    st[:, 15:16], None, OP.mult)
                self.w(te, "dve")
                for j in range(8):
                    ptr = self.ptrA if j < 4 else self.ptrB
                    ins = te.transpose(ptr[:, _ts(j % 4, 128)], t["ytm"][:, _ts(j, 128)],
                                       t["identb"][:])
                self.inc(ins, "pe", 1)
                self.w(v, "pe")
                v.tensor_copy(t["yfm"][:, 0:512], self.ptrA[:, 0:512])
                v.tensor_copy(t["yfm"][:, 512:1024], self.ptrB[:, 0:512])
                self.w(te, "dve")
                for n in range(2):
                    for k in range(8):
                        te.matmul(pb[n][:, 0:512], t["yfm"][:, _ts(k, 128)],
                                  t["projw"][:, k * 1024 + n * 512: k * 1024 + (n + 1) * 512],
                                  start=(k == 0), stop=False, skip_group_check=True)
                    ins = te.matmul(pb[n][:, 0:512], t["ones8b"][0:8, :],
                                    t["projb8"][0:8, _ts(n, 512)],
                                    start=False, stop=True, skip_group_check=True)
                    self.inc(ins, "pe", 1)
                self.w(v, "pe")
                if a >= 2:
                    self.w(v, "sto")
                x2 = t["x2t"][:, _ts(par, D)]
                for n in range(2):
                    v.tensor_tensor(x2[:, _ts(n, 512)], pb[n][:, 0:512],
                                    t["y1t"][:, _ts(n, 512)], OP.add)
                self.w(te, "dve")
                self.w(g, "dve")
                self.dma(g, "sto", dr["x2_spill"][_ts(a - 1, 128), :], x2)
                self.w(s, "dve")
                s.activation(t["sq4"][:], x2, AF.Square, accum_out=st[:, 16:17])
                self.w(v, "act")
                v.tensor_scalar(st[:, 17:18], st[:, 16:17], 1.0 / D, None, OP.mult)
                self.w(s, "dve")
                s.activation(st[:, 18:19], st[:, 17:18], AF.Sqrt, bias=t["cconst"][:, 0:1])
                self.w(v, "act")
                v.reciprocal(st[:, 19:20], st[:, 18:19])
                v.tensor_scalar(t["x2nb"][:], x2, st[:, 19:20], None, OP.mult)
                self.w(te, "dve")
                for j in range(8):
                    ptr = self.ptrA if j < 4 else self.ptrB
                    ins = te.transpose(ptr[:, _ts(j % 4, 128)], t["x2nb"][:, _ts(j, 128)],
                                       t["identb"][:])
                self.inc(ins, "pe", 1)
                self.w(v, "pe")
                for j in range(8):
                    ptr = self.ptrA if j < 4 else self.ptrB
                    v.tensor_scalar(t["h2_fm"][:, j * OWN + (a - 1) * 128: j * OWN + a * 128],
                                    ptr[:, _ts(j % 4, 128)], t["wffm"][:, j:j + 1],
                                    None, OP.mult)
                self.w(te, "dve")
            self.p_taps(4)
            self.phase_barrier()

    def p5_gate(self):
        t, dr = self.t, self.dr
        v, s, te, g, sy = self.v, self.s, self.te, self.g, self.sy
        pb, st = self.pb, self.t["stat"]
        with ExitStack() as sc5:
            sb5 = lambda n, sh, dt=F32: self.t.__setitem__(
                n, sc5.enter_context(self.nc.sbuf_tensor(n, sh, dt)))
            sb5("gatew", [128, 8 * 32], BF16)
            sb5("gateb8f", [8, 32], F32)
            sb5("gateb8", [8, 32], BF16)
            sb5("glogt", [128, 32], F32)
            sb5("ebuf", [128, 4 * 32], F32)
            sb5("mvals", [128, 4], F32)
            sb5("tmp5", [128, 32], F32)
            self.dma(g, "ldw", t["gatew"][:].rearrange("p (k c) -> p k c", c=32),
                     dr["gate_w"].rearrange("(k p) c -> p k c", p=128))
            for p in range(8):
                self.dma(sy, "ld", t["gateb8f"][p:p + 1, :], dr["gate_b"])
            self.w(v, "ld")
            v.tensor_scalar(t["gateb8"][0:8, :], t["gateb8f"][0:8, :], 0.125, None, OP.mult)
            self.w(te, "ldw"); self.w(te, "dve")
            for tt in range(NTL_OWN):
                for k in range(8):
                    te.matmul(pb[5][:, 0:32],
                              t["h2_fm"][:, k * OWN + tt * 128: k * OWN + (tt + 1) * 128],
                              t["gatew"][:, k * 32:(k + 1) * 32],
                              start=(k == 0), stop=False, skip_group_check=True)
                ins = te.matmul(pb[5][:, 0:32], t["ones8b"][0:8, :], t["gateb8"][0:8, :],
                                start=False, stop=True, skip_group_check=True)
                self.inc(ins, "pe", 1)
                self.w(v, "pe")
                v.tensor_copy(t["glogt"][:], pb[5][:, 0:32])
                self.w(te, "dve")
                for r in range(TOPK):
                    v.tensor_reduce(t["mvals"][:, r:r + 1], t["glogt"][:],
                                    axis=mybir.AxisListType.X, op=OP.max)
                    v.tensor_scalar(t["ebuf"][:, _ts(r, 32)], t["glogt"][:],
                                    t["mvals"][:, r:r + 1], None, OP.is_equal)
                    if r < TOPK - 1:
                        v.scalar_tensor_tensor(t["glogt"][:], t["ebuf"][:, _ts(r, 32)],
                                               -1e30, t["glogt"][:], OP.mult, OP.add)
                v.tensor_copy(st[:, 20:21], t["mvals"][:, 0:1])
                v.tensor_scalar(t["mvals"][:], t["mvals"][:], st[:, 20:21], None,
                                OP.subtract)
                self.w(s, "dve")
                s.activation(t["mvals"][:], t["mvals"][:], AF.Exp, accum_out=st[:, 21:22])
                self.w(v, "act")
                v.reciprocal(st[:, 22:23], st[:, 21:22])
                v.tensor_scalar(t["mvals"][:], t["mvals"][:], st[:, 22:23], None, OP.mult)
                ce = t["ce_all"][:, _ts(tt, 32)]
                v.tensor_scalar(ce, t["ebuf"][:, 0:32], t["mvals"][:, 0:1], None, OP.mult)
                for r in range(1, TOPK):
                    v.tensor_scalar(t["tmp5"][:], t["ebuf"][:, _ts(r, 32)],
                                    t["mvals"][:, r:r + 1], None, OP.mult)
                    v.tensor_tensor(ce, ce, t["tmp5"][:], OP.add)
            self.p_taps(5)
            self.phase_barrier()

    def p6_moe(self):
        t, dr = self.t, self.dr
        v, s, te, g, sy = self.v, self.s, self.te, self.g, self.sy
        pb, st = self.pb, self.t["stat"]
        with ExitStack() as sc6:
            sb6 = lambda n, sh, dt=F32: self.t.__setitem__(
                n, sc6.enter_context(self.nc.sbuf_tensor(n, sh, dt)))
            sb6("wup", [128, 8 * 8 * 256], BF16)
            sb6("wdn", [128, 2 * 8 * 1024], BF16)
            sb6("act6", [128, 8 * 1024], BF16)
            sb6("sg", [128, 512], F32)
            sb6("tmp6", [128, 512], F32)
            sb6("y2b", [128, 8 * 1024], F32)
            sb6("x2ld", [128, 2 * D], F32)
            sb6("x1l7", [128, 2 * D], F32)
            sb6("ffnt", [128, 2 * D], F32)
            sb6("sq6", [128, D], F32)

            slice_ready = {}

            def up_slice_dma(ti):
                b, e, m = triples[ti]
                mp = m % 8
                sem = ("ldw", "ldw2", "ldw3", "ldw4")[ti % 4]
                wu3 = dr["W_up"][e].rearrange("(k p) c -> p k c", p=128)
                self.w(g, "pe")
                self.dma(g, sem,
                         t["wup"][:].rearrange("p (ms k c) -> p ms k c", ms=8, c=256)
                         [:, mp, :, 0:128],
                         wu3[:, :, m * 128:(m + 1) * 128])
                self.dma(g, sem,
                         t["wup"][:].rearrange("p (ms k c) -> p ms k c", ms=8, c=256)
                         [:, mp, :, 128:256],
                         wu3[:, :, DFF + m * 128: DFF + (m + 1) * 128])
                slice_ready[ti] = (sem, self.cnt[sem])

            triples = [(b, e, m) for b in range(2) for e in range(NE) for m in range(8)]
            for ti0 in range(6):
                up_slice_dma(ti0)
            dve_mark = {}
            for b in range(2):
                self.w(v, "pe")
                v.memset(t["y2b"][:], 0.0)
                for e in range(NE):
                    em = e % 2
                    self.w(g, "pe")
                    self.dma(g, "ldd",
                             t["wdn"][:].rearrange("p (es k c) -> p es k c", es=2, c=1024)
                             [:, em, :, :],
                             dr["W_down"][e].rearrange("(k p) c -> p k c", p=128))
                    for m in range(8):
                        mp = m % 8
                        ti = b * NE * 8 + e * 8 + m
                        if ti + 6 < len(triples):
                            up_slice_dma(ti + 6)
                        rsem, rcnt = slice_ready[ti]
                        te.wait_ge(self.sems[rsem], rcnt)
                        for nb2 in range(2):
                            bu, bg = pb[2 * nb2], pb[2 * nb2 + 1]
                            key_u, key_g = ("up", nb2, 0), ("up", nb2, 1)
                            te.wait_ge(self.sems["dve"], dve_mark.get(key_u, 0))
                            te.wait_ge(self.sems["dve"], dve_mark.get(key_g, 0))
                            te.wait_ge(self.sems["act"], dve_mark.get(("upa", nb2), 0))
                            tok = b * 1024 + nb2 * 512
                            for k in range(8):
                                te.matmul(bu[:, 0:512],
                                          t["wup"][:, (mp * 8 + k) * 256:
                                                   (mp * 8 + k) * 256 + 128],
                                          t["h2_fm"][:, k * OWN + tok: k * OWN + tok + 512],
                                          start=(k == 0), stop=(k == 7),
                                          skip_group_check=True)
                            for k in range(8):
                                ins = te.matmul(bg[:, 0:512],
                                                t["wup"][:, (mp * 8 + k) * 256 + 128:
                                                         (mp * 8 + k) * 256 + 256],
                                                t["h2_fm"][:, k * OWN + tok:
                                                           k * OWN + tok + 512],
                                                start=(k == 0), stop=(k == 7),
                                                skip_group_check=True)
                            self.inc(ins, "pe", 2)
                            self.w(s, "pe")
                            s.activation(t["sg"][:], bg[:, 0:512], AF.Silu)
                            dve_mark[("upa", nb2)] = self.cnt["act"]
                            self.w(v, "act"); self.w(v, "pe")
                            v.tensor_tensor(
                                t["act6"][:, m * 1024 + nb2 * 512: m * 1024 + nb2 * 512 + 512],
                                t["sg"][:], bu[:, 0:512], OP.mult)
                            dve_mark[key_u] = self.cnt["dve"]
                            dve_mark[key_g] = self.cnt["dve"]
                    self.w(te, "ldd")
                    for half in range(2):
                        for tt8 in range(8):
                            bank = pb[4 + tt8 % 2]
                            key = ("dn", tt8 % 2)
                            te.wait_ge(self.sems["dve"], dve_mark.get(key, 0))
                            for k in range(8):
                                ins = te.matmul(
                                    bank[:, 0:512],
                                    t["act6"][:, k * 1024 + tt8 * 128:
                                              k * 1024 + (tt8 + 1) * 128],
                                    t["wdn"][:, (em * 8 + k) * 1024 + half * 512:
                                             (em * 8 + k) * 1024 + half * 512 + 512],
                                    start=(k == 0), stop=(k == 7), skip_group_check=True)
                            self.inc(ins, "pe", 1)
                            self.w(v, "pe")
                            cecol = t["ce_all"][:, (b * 8 + tt8) * 32 + e:
                                               (b * 8 + tt8) * 32 + e + 1]
                            v.tensor_scalar(t["tmp6"][:], bank[:, 0:512], cecol, None,
                                            OP.mult)
                            v.tensor_tensor(
                                t["y2b"][:, tt8 * 1024 + half * 512:
                                         tt8 * 1024 + half * 512 + 512],
                                t["y2b"][:, tt8 * 1024 + half * 512:
                                         tt8 * 1024 + half * 512 + 512],
                                t["tmp6"][:], OP.add)
                            dve_mark[key] = self.cnt["dve"]
                for tt8 in range(8):
                    par7 = tt8 % 2
                    self.w(sy, "dve")
                    self.dma(sy, "ld", t["x2ld"][:, _ts(par7, D)],
                             dr["x2_spill"][_ts(b * 8 + tt8, 128), :])
                    self.dma(sy, "ld", t["x1l7"][:, _ts(par7, D)],
                             dr["x1_spill"][_ts(1 + b * 8 + tt8, 128), :])
                    self.w(v, "ld")
                    if b * 8 + tt8 >= 2:
                        self.w(v, "sto")
                    ff = t["ffnt"][:, _ts(par7, D)]
                    v.tensor_tensor(ff, t["y2b"][:, _ts(tt8, D)],
                                    t["x2ld"][:, _ts(par7, D)], OP.add)
                    v.tensor_reduce(st[:, 24:25], ff, axis=mybir.AxisListType.X, op=OP.add)
                    v.tensor_scalar(st[:, 25:26], st[:, 24:25], 1.0 / D, None, OP.mult)
                    self.w(s, "dve")
                    s.activation(t["sq6"][:], ff, AF.Square, accum_out=st[:, 26:27])
                    self.w(v, "act")
                    v.tensor_scalar(st[:, 27:28], st[:, 26:27], 1.0 / D, None, OP.mult)
                    v.tensor_tensor(st[:, 28:29], st[:, 25:26], st[:, 25:26], OP.mult)
                    v.tensor_tensor(st[:, 28:29], st[:, 27:28], st[:, 28:29], OP.subtract)
                    self.w(s, "dve")
                    s.activation(st[:, 29:30], st[:, 28:29], AF.Sqrt,
                                 bias=t["cconst"][:, 0:1])
                    self.w(v, "act")
                    v.reciprocal(st[:, 30:31], st[:, 29:30])
                    v.tensor_scalar(ff, ff, st[:, 25:26], st[:, 30:31],
                                    OP.subtract, OP.mult)
                    v.tensor_tensor(ff, ff, t["x1l7"][:, _ts(par7, D)], OP.add)
                    self.w(g, "dve")
                    self.dma(g, "sto", dr["out"][_ts(b * 8 + tt8, 128), :], ff)
            self.p_taps(6)

    def p_taps(self, phase):
        t, dr = self.t, self.dr
        g = self.g
        for name, tile, lo, hi, ph in DEBUG_TAPS:
            if ph != phase:
                continue
            self.w(g, "dve"); self.w(g, "act"); self.w(g, "pe")
            ap = t[tile][:, lo:hi]
            self.dma(g, "sto", dr["tap_" + name][0:ap.shape[0], :], ap)

    def p_finish(self):
        g = self.g
        self.w(g, "dve"); self.w(g, "act"); self.w(g, "pe")
        g.wait_ge(self.sems["sto"], self.cnt["sto"])



_CACHED = {}


def _build_cached():
    key = (BUILD_PHASES, tuple(n for n, *_ in DEBUG_TAPS))
    if key not in _CACHED:
        kb = KB()
        _CACHED[key] = kb.build()
    return _CACHED[key]


def _constants():
    c = {}
    c["identsb"] = np.eye(128, dtype=BF)
    c["identsf"] = np.eye(128, dtype=np.float32)
    c["ones8c"] = np.ones((8, 128), np.float32)
    c["ones1r"] = np.ones((1, 128), np.float32)
    cc = np.zeros((128, 4), np.float32)
    cc[:, 0] = LN_EPS; cc[:, 1] = L2_EPS; cc[:, 2] = PI / 2; cc[:, 3] = 1.0
    c["cconstc"] = cc
    tt2 = np.arange(128)
    c["cumsumc"] = (tt2[:, None] <= tt2[None, :]).astype(np.float32)
    e8 = np.zeros((8, 8 * CHK), np.float32)
    for hp in range(8):
        e8[hp, hp * CHK:(hp + 1) * CHK] = 1.0
    c["e8bcc"] = e8
    c["e8negc"] = -e8
    c["e8diagc"] = np.eye(8, dtype=np.float32)
    i_idx = np.arange(CHK)[:, None]
    s_idx = np.arange(CHK)[None, :]
    tril = (s_idx < i_idx).astype(np.float32)
    diag = (i_idx <= s_idx).astype(np.float32)
    c["tril01c"] = np.tile(tril, (1, 8)).astype(BF)
    c["diag01c"] = np.tile(diag, (1, 8)).astype(BF)
    invf = (1.0 / 10000.0 ** (np.arange(0, DK, 2, dtype=np.float32) / DK)).astype(np.float32)
    iv8 = np.zeros((8, 8 * 32), np.float32)
    for hp in range(8):
        iv8[hp, hp * 32:(hp + 1) * 32] = invf
    c["invf8c"] = iv8
    r = np.arange(CHK)[:, None]
    j = np.arange(2 * CHK)[None, :]
    band = np.where((j >= r) & (j <= r + CHK), 0.0, -1e30).astype(np.float32)
    first = np.where((j >= CHK) & (j <= r + CHK) & (j >= r), 0.0, -1e30).astype(np.float32)
    c["maskband"] = band
    c["maskfirst_normal"] = band
    c["maskfirst_first"] = first
    pos = np.arange(T, dtype=np.float64) / 32.0
    inv2 = 1.0 / (150000.0 ** (np.arange(0, FDH, 2, dtype=np.float64) / FDH))
    fr = pos[:, None] * inv2[None, :]
    cos = np.cos(fr); sin = np.sin(fr)
    cos2 = np.repeat(cos, 2, axis=1)
    sin2 = np.repeat(sin, 2, axis=1)
    sin2[:, 0::2] *= -1.0
    c["ropecos_full"] = np.tile(cos2, (1, FH)).astype(np.float32)
    c["ropesin_full"] = np.tile(sin2, (1, FH)).astype(np.float32)
    return c


def _stage_in_maps(inputs):
    cst = _constants()
    xfull = np.ascontiguousarray(inputs["L0_attn_hidden_states"])
    wbt = np.concatenate([np.asarray(inputs["L0_attn_Wbeta"]),
                          np.asarray(inputs["L0_attn_Wtg"])], axis=1)
    shared = {
        "Wq": inputs["L0_attn_Wq"], "Wk": inputs["L0_attn_Wk"], "Wv": inputs["L0_attn_Wv"],
        "Wbt": wbt, "Wg": inputs["L0_attn_Wg"], "Wo": inputs["L0_attn_Wo"],
        "log_decay": inputs["L0_attn_log_decay"],
        "mode_logits": inputs["L0_attn_mode_logits"],
        "omega_ls": inputs["L0_attn_omega_log_scale"],
        "attn_norm_w": np.asarray(inputs["L0_ffn_attn_norm_w"])[None, :],
        "ffn_norm_w": np.asarray(inputs["L0_ffn_ffn_norm_w"])[None, :],
        "qkv_w": inputs["L0_ffn_qkv_w"], "qkv_b": np.asarray(inputs["L0_ffn_qkv_b"])[None, :],
        "proj_w": inputs["L0_ffn_proj_w"], "proj_b": np.asarray(inputs["L0_ffn_proj_b"])[None, :],
        "attn_sink": np.asarray(inputs["L0_ffn_attn_sink"])[None, :],
        "gate_w": inputs["L0_ffn_gate_w"], "gate_b": np.asarray(inputs["L0_ffn_gate_b"])[None, :],
        "W_up": inputs["L0_ffn_W_up"], "W_down": inputs["L0_ffn_W_down"],
        "identsb": cst["identsb"], "identsf": cst["identsf"],
        "ones8c": cst["ones8c"], "ones1r": cst["ones1r"],
        "e8bcc": cst["e8bcc"], "e8negc": cst["e8negc"], "e8diagc": cst["e8diagc"],
        "tril01c": cst["tril01c"], "diag01c": cst["diag01c"], "invf8c": cst["invf8c"],
        "cconstc": cst["cconstc"], "cumsumc": cst["cumsumc"],
        "maskband": cst["maskband"],
    }
    shared = {k: np.ascontiguousarray(np.asarray(val)) for k, val in shared.items()}
    in_maps = []
    for core in range(NCORE):
        seq, p = core // 4, core % 4
        base = p * OWN
        x0 = np.zeros((NT_DN, D), np.float32)
        lo = base - 256
        if lo < 0:
            x0[256:] = xfull[seq, base:base + OWN]
        else:
            x0[:] = xfull[seq, lo:base + OWN]
        posmat = (np.arange(128)[:, None] + np.arange(NCH)[None, :] * 128 + (base - 256)
                  ).astype(np.float32)
        at_lo = base - 128
        if at_lo < 0:
            ropec = np.zeros((NT_AT, D), np.float32)
            ropes = np.zeros((NT_AT, D), np.float32)
            ropec[128:] = cst["ropecos_full"][base:base + OWN]
            ropes[128:] = cst["ropesin_full"][base:base + OWN]
        else:
            ropec = cst["ropecos_full"][at_lo:base + OWN]
            ropes = cst["ropesin_full"][at_lo:base + OWN]
        m = dict(shared)
        m["x0"] = x0
        m["pos_tc"] = posmat
        m["maskfirst"] = cst["maskfirst_first"] if p == 0 else cst["maskfirst_normal"]
        m["ropecos"] = np.ascontiguousarray(ropec)
        m["ropesin"] = np.ascontiguousarray(ropes)
        in_maps.append(m)
    return in_maps


def kernel(**inputs):
    nc = _build_cached()
    in_maps = _stage_in_maps(inputs)
    res = run_bass_kernel_spmd(nc, in_maps, core_ids=list(range(NCORE)))
    outs = [res.results[i]["out"] for i in range(NCORE)]
    full = np.concatenate(outs, axis=0).reshape(2, T, D).astype(np.float32)
    if DEBUG_TAPS:
        kernel.taps = [{k: v for k, v in r.items() if k.startswith("tap_")}
                       for r in res.results]
    return full
